# version 01
# baseline (speedup 1.0000x reference)
import contextlib
import math

import numpy as np
import concourse.bass as bass
import concourse.mybir as mybir
from concourse.bass_utils import run_bass_kernel_spmd

F32 = mybir.dt.float32
BF16 = mybir.dt.bfloat16
AF = mybir.ActivationFunctionType
ALU = mybir.AluOpType
AX = mybir.AxisListType

DMA_OPS = ("dma_start", "dma_start_transpose")
ENGINES = ("sync", "tensor", "vector", "scalar", "gpsimd")

RINT_C = 3.0 * (2.0 ** 22)


class _GrpEng:
    def __init__(self, group, name):
        self._group = group
        self._name = name

    def __getattr__(self, meth):
        def call(*a, **k):
            self._group.append((self._name, meth, a, k))
        return call


class _Grp:
    def __init__(self, items):
        for e in ENGINES:
            setattr(self, e, _GrpEng(items, e))


class Prog:
    def __init__(self):
        self.nc = bass.Bass("TRN2", target_bir_lowering=False)
        self.stack = contextlib.ExitStack()
        self.groups = []
        self._cms = {}

    def _enter(self, cm):
        h = cm.__enter__()
        self._cms[id(h)] = cm
        return h

    def free(self, h):
        cm = self._cms.pop(id(h))
        cm.__exit__(None, None, None)

    def sb(self, name, p, f, dt, side=None):
        if side is None:
            return self._enter(self.nc.sbuf_tensor(name, [p, f], dt))
        return self._enter(self.nc.sbuf_tensor(name, [p, f], dt, side=side))

    def psum(self, name, p, f, dt=F32):
        return self._enter(self.nc.psum_tensor(name, [p, f], dt))

    def din(self, name, shape, dt=F32):
        return self.nc.dram_tensor(name, list(shape), dt, kind="ExternalInput").ap()

    def dout(self, name, shape, dt=F32):
        return self.nc.dram_tensor(name, list(shape), dt, kind="ExternalOutput").ap()

    def dtmp(self, name, shape, dt=F32):
        return self.nc.dram_tensor(name, list(shape), dt, kind="Internal").ap()

    @contextlib.contextmanager
    def grp(self):
        items = []
        self.groups.append(items)
        yield _Grp(items)

    def finalize(self):
        nc = self.nc
        per = {e: [] for e in ENGINES}
        base = 0
        sem = self.stack.enter_context(nc.semaphore("gsem"))
        for group in self.groups:
            if not group:
                continue
            for (e, m, _, _) in group:
                assert not (e == "gpsimd" and m in DMA_OPS), "SWDGE (gpsimd) DMAs break the sim's semaphore rule"
            ginc = sum(16 if m in DMA_OPS else 1 for (_, m, _, _) in group)
            seen = set()
            for (e, m, a, k) in group:
                if e not in seen:
                    seen.add(e)
                    if base > 0:
                        per[e].append(("w", base))
                per[e].append(("op", m, a, k, 16 if m in DMA_OPS else 1))
            base += ginc
        total = base

        with nc.Block() as block:
            def mk(ename):
                def emit(eng):
                    for item in per[ename]:
                        if item[0] == "w":
                            eng.wait_ge(sem, item[1])
                        else:
                            _, m, a, k, inc = item
                            try:
                                inst = getattr(eng, m)(*a, **k)
                            except Exception:
                                import sys
                                print("FAILED OP:", ename, m,
                                      [getattr(x, "tensor", None) and x.tensor.name for x in a
                                       if hasattr(x, "tensor")],
                                      {kk: (getattr(vv, "tensor", None) and vv.tensor.name)
                                       for kk, vv in k.items() if hasattr(vv, "tensor")},
                                      file=sys.stderr)
                                raise
                            inst.then_inc(sem, inc)
                    if ename == "sync":
                        eng.wait_ge(sem, total)
                return emit

            block.sync(mk("sync"))
            block.tensor(mk("tensor"))
            block.vector(mk("vector"))
            block.scalar(mk("scalar"))
            block.gpsimd(mk("gpsimd"))
        for cm in reversed(list(self._cms.values())):
            cm.__exit__(None, None, None)
        self._cms.clear()
        self.stack.close()
        return nc


def _run(prog, in_maps, n_cores=8):
    nc = prog.finalize()
    res = run_bass_kernel_spmd(nc, in_maps, core_ids=list(range(n_cores)))
    return res.results


B, S, D = 4, 4096, 1024
NTOK = B * S
DC = D // 128
TT = 512
LN_EPS = 1e-5


def _bf16(x):
    import ml_dtypes
    return np.ascontiguousarray(x).astype(ml_dtypes.bfloat16)


CAST_FREE = 8192


def build_launch0(n_per_core, w1_cols, w2_cols):
    p = Prog()
    nc = p.nc
    wflat = p.din("wflat", [n_per_core], F32)
    wbf = p.dout("wbf", [n_per_core], BF16)
    w1t = p.din("w1t", [1024, 4096], F32)
    w2t = p.din("w2t", [4096, 1024], F32)
    w1q = p.dout("w1q", [128, 4096], BF16)
    w2q = p.dout("w2q", [512, 1024], BF16)
    wscale = p.dout("wscale", [1, 2], F32)

    ntile = n_per_core // (128 * CAST_FREE)
    wf2 = wflat.rearrange("(t p f) -> t p f", p=128, f=CAST_FREE)
    wb2 = wbf.rearrange("(t p f) -> t p f", p=128, f=CAST_FREE)
    del w1_cols, w2_cols

    a = p.sb("a", 128, CAST_FREE, F32)
    b_ = p.sb("b", 128, CAST_FREE, F32)
    abf = p.sb("abf", 128, CAST_FREE, BF16)
    bbf = p.sb("bbf", 128, CAST_FREE, BF16)

    bufs = [(a, abf), (b_, bbf)]
    with p.grp() as g:
        g.sync.dma_start(out=a[:], in_=wf2[0, :, :])
    for t in range(ntile):
        src, dst = bufs[t % 2]
        with p.grp() as g:
            g.vector.tensor_copy(dst[:], src[:])
            if t + 1 < ntile:
                nsrc = bufs[(t + 1) % 2][0]
                g.sync.dma_start(out=nsrc[:], in_=wf2[t + 1, :, :])
        with p.grp() as g:
            g.scalar.dma_start(out=wb2[t, :, :], in_=dst[:])

    wtile = p.sb("wtile", 128, 4096, F32)
    wtile2 = p.sb("wtile2", 128, 4096, F32)
    acc1 = p.sb("acc1", 128, 1, F32)
    acc2 = p.sb("acc2", 128, 1, F32)
    rowsum = p.sb("rowsum", 128, 1, F32)
    absd = p.sb("absd", 128, 4096, F32)
    ones = p.sb("ones", 128, 1, F32)
    scal = p.sb("scal", 1, 2, F32)
    sps = p.psum("sps", 1, 2, F32)
    bc = p.sb("bc", 128, 2, F32)
    bcp = p.psum("bcp", 128, 2, F32)
    ones1 = p.sb("ones1", 1, 128, F32)

    with p.grp() as g:
        g.vector.memset(acc1[:], 0.0)
        g.vector.memset(acc2[:], 0.0)
        g.gpsimd.memset(ones[:], 1.0)
    with p.grp() as g:
        g.gpsimd.memset(ones1[:], 1.0)
    for w_in, acc in ((w1t, acc1), (w2t, acc2)):
        nrt = w_in.shape[0] // 128
        for r in range(nrt):
            wt = (wtile, wtile2)[r % 2]
            with p.grp() as g:
                g.sync.dma_start(out=wt[:, 0:w_in.shape[1]], in_=w_in[r * 128:(r + 1) * 128, :])
            with p.grp() as g:
                g.scalar.activation(out=absd[:, 0:w_in.shape[1]], in_=wt[:, 0:w_in.shape[1]],
                                    func=AF.Abs, accum_out=rowsum[:])
            with p.grp() as g:
                g.vector.tensor_tensor(out=acc[:], in0=acc[:], in1=rowsum[:], op=ALU.add)
    accb = p.sb("accb", 128, 2, F32)
    with p.grp() as g:
        g.vector.tensor_copy(accb[:, 0:1], acc1[:])
        g.scalar.copy(out=accb[:, 1:2], in_=acc2[:])
    with p.grp() as g:
        g.tensor.matmul(sps[:], ones[:], accb[:], start=True, stop=True)
    nel = float(1024 * 4096)
    with p.grp() as g:
        g.scalar.mul(out=scal[:], in_=sps[:], mul=1.0 / nel)
    with p.grp() as g:
        g.vector.tensor_scalar(out=scal[:], in0=scal[:], scalar1=1e-5, scalar2=None,
                               op0=ALU.max, op1=ALU.bypass)
    with p.grp() as g:
        g.scalar.dma_start(out=wscale[:], in_=scal[:])
    with p.grp() as g:
        g.vector.reciprocal(out=scal[:], in_=scal[:])
    with p.grp() as g:
        g.tensor.matmul(bcp[:], ones1[:], scal[:], start=True, stop=True)
    with p.grp() as g:
        g.scalar.copy(out=bc[:], in_=bcp[:])

    w1s = p.din("w1s", [128, 4096], F32)
    w2s = p.din("w2s", [512, 1024], F32)
    q = p.sb("q", 128, 4096, F32)
    qb = p.sb("qb", 128, 4096, BF16)
    for (w_in, w_out, col, nrt, ncol) in ((w1s, w1q, 0, 1, 4096), (w2s, w2q, 1, 4, 1024)):
        for r in range(nrt):
            with p.grp() as g:
                g.sync.dma_start(out=q[:, 0:ncol], in_=w_in[r * 128:(r + 1) * 128, :])
            with p.grp() as g:
                g.vector.tensor_scalar(out=q[:, 0:ncol], in0=q[:, 0:ncol],
                                       scalar1=bc[:, col:col + 1], scalar2=None,
                                       op0=ALU.mult, op1=ALU.bypass)
            with p.grp() as g:
                g.vector.tensor_scalar(out=q[:, 0:ncol], in0=q[:, 0:ncol],
                                       scalar1=RINT_C, scalar2=None, op0=ALU.add, op1=ALU.bypass)
            with p.grp() as g:
                g.vector.tensor_scalar(out=q[:, 0:ncol], in0=q[:, 0:ncol],
                                       scalar1=RINT_C, scalar2=None, op0=ALU.subtract, op1=ALU.bypass)
            with p.grp() as g:
                g.vector.tensor_scalar(out=qb[:, 0:ncol], in0=q[:, 0:ncol],
                                       scalar1=1.0, scalar2=-1.0, op0=ALU.min, op1=ALU.max)
            with p.grp() as g:
                g.scalar.dma_start(out=w_out[r * 128:(r + 1) * 128, :], in_=qb[:, 0:ncol])
    return p


def host_launch0(inputs):
    specs = []

    perm = np.concatenate([np.arange(0, 128, 2), np.arange(1, 128, 2)])
    wq = inputs["L0_attn_a_Wq"]
    wqT = np.concatenate([wq[h].T[:, perm] for h in range(2)], axis=1)
    specs.append(("l0aa_wq", wqT))
    specs.append(("l0aa_wk", inputs["L0_attn_a_Wk"].T[:, perm]))
    specs.append(("l0aa_wv", inputs["L0_attn_a_Wv"].T))
    specs.append(("l0aa_wo", inputs["L0_attn_a_Wo"].T))
    specs.append(("l0ab_wqkv", inputs["L0_attn_b_w_qkv"].T))
    specs.append(("l0ab_wout", inputs["L0_attn_b_w_out"].T))
    specs.append(("l0f_w1", inputs["L0_ffn_w1"]))
    specs.append(("l0f_w2", inputs["L0_ffn_w2"]))
    specs.append(("l1aa_wq", inputs["L1_attn_a_Wq"].reshape(1024, 1024)))
    specs.append(("l1aa_wk", inputs["L1_attn_a_Wk"].reshape(1024, 1024)))
    specs.append(("l1aa_wv", inputs["L1_attn_a_Wv"].reshape(1024, 1024)))
    specs.append(("l1aa_wo", inputs["L1_attn_a_Wo"].reshape(1024, 1024)))
    specs.append(("l1ab_wuv", inputs["L1_attn_b_W_uv"]))
    specs.append(("l1ab_wo", inputs["L1_attn_b_W_o"]))

    shapes = [(k, a.shape) for k, a in specs]
    flat = np.concatenate([np.ascontiguousarray(a, dtype=np.float32).reshape(-1) for _, a in specs])
    unit = 8 * 128 * CAST_FREE
    pad = (-len(flat)) % unit
    flat = np.concatenate([flat, np.zeros(pad, np.float32)])
    npc = len(flat) // 8

    w1t = np.ascontiguousarray(inputs["L1_ffn_W1"].T, dtype=np.float32)
    w2t = np.ascontiguousarray(inputs["L1_ffn_W2"].T, dtype=np.float32)

    p = build_launch0(npc, 4096, 1024)
    in_maps = []
    for c in range(8):
        in_maps.append({
            "wflat": flat[c * npc:(c + 1) * npc],
            "w1t": w1t, "w2t": w2t,
            "w1s": w1t[c * 128:(c + 1) * 128, :],
            "w2s": w2t[c * 512:(c + 1) * 512, :],
        })
    res = _run(p, in_maps)
    flat_bf = np.concatenate([r["wbf"] for r in res])
    out = {}
    off = 0
    for k, shp in shapes:
        n = int(np.prod(shp))
        out[k] = flat_bf[off:off + n].reshape(shp)
        off += n
    out["l1f_w1q"] = np.concatenate([r["w1q"] for r in res], axis=0)
    out["l1f_w2q"] = np.concatenate([r["w2q"] for r in res], axis=0)
    out["l1f_wscale"] = res[0]["wscale"].reshape(2)
    return out


def ln_alloc(p, pfx=""):
    L = {}
    L["ones128"] = p.sb(pfx + "Lones128", 128, 1, F32)
    L["ones1"] = p.sb(pfx + "Lones1", 1, 128, F32)
    L["stats"] = p.sb(pfx + "Lstats", 1, 1024, F32)
    L["varb"] = p.sb(pfx + "Lvarb", 1, 512, F32)
    L["m2b"] = p.sb(pfx + "Lm2b", 1, 512, F32)
    L["bcs"] = p.sb(pfx + "Lbcs", 128, 1024, F32)
    L["xf"] = p.sb(pfx + "Lxf", 128, 8 * 512, F32)
    L["xg"] = p.sb(pfx + "Lxg", 128, 8 * 512, F32)
    with p.grp() as g:
        g.vector.memset(L["ones128"][:], 1.0)
        g.gpsimd.memset(L["ones1"][:], 1.0)
    return L


def ln_free(p, L):
    for k in ("xg", "xf", "bcs", "m2b", "varb", "stats", "ones1", "ones128"):
        p.free(L[k])


def emit_ln_tile(p, L, xin, tt, dst_sb, dst_T, dst_off, PA, PB):
    xf, xg = L["xf"], L["xg"]
    stats, varb, m2b, bcs = L["stats"], L["varb"], L["m2b"], L["bcs"]

    def xfc(c):
        return xf[:, c * 512:(c + 1) * 512]

    def xgc(c):
        return xg[:, c * 512:(c + 1) * 512]

    with p.grp() as g:
        for c in range(8):
            g.sync.dma_start(out=xfc(c), in_=xin[c * 128:(c + 1) * 128, tt * 512:(tt + 1) * 512])
    with p.grp() as g:
        for c in range(8):
            g.scalar.activation(out=xgc(c), in_=xfc(c), func=AF.Square)
    with p.grp() as g:
        for c in range(8):
            g.tensor.matmul(PA[0:1, :], L["ones128"][:], xfc(c), start=(c == 0), stop=(c == 7))
        for c in range(8):
            g.tensor.matmul(PB[0:1, :], L["ones128"][:], xgc(c), start=(c == 0), stop=(c == 7))
    with p.grp() as g:
        g.scalar.mul(out=stats[0:1, 0:512], in_=PA[0:1, :], mul=1.0 / 1024.0)
    with p.grp() as g:
        g.scalar.mul(out=varb[:], in_=PB[0:1, :], mul=1.0 / 1024.0)
        g.vector.tensor_tensor(out=m2b[:], in0=stats[0:1, 0:512], in1=stats[0:1, 0:512], op=ALU.mult)
    with p.grp() as g:
        g.vector.tensor_tensor(out=varb[:], in0=varb[:], in1=m2b[:], op=ALU.subtract)
    with p.grp() as g:
        g.vector.tensor_scalar(out=varb[:], in0=varb[:], scalar1=LN_EPS, scalar2=None,
                               op0=ALU.add, op1=ALU.bypass)
    with p.grp() as g:
        g.scalar.activation(out=m2b[:], in_=varb[:], func=AF.Ln)
    with p.grp() as g:
        g.scalar.activation(out=stats[0:1, 512:1024], in_=m2b[:], func=AF.Exp, scale=-0.5)
    with p.grp() as g:
        g.tensor.matmul(PA[:], L["ones1"][:], stats[0:1, 0:512], start=True, stop=True)
        g.tensor.matmul(PB[:], L["ones1"][:], stats[0:1, 512:1024], start=True, stop=True)
    with p.grp() as g:
        g.scalar.copy(out=bcs[:, 0:512], in_=PA[:])
        g.vector.tensor_copy(bcs[:, 512:1024], PB[:])
    with p.grp() as g:
        for c in range(8):
            g.vector.tensor_tensor(out=xgc(c), in0=xfc(c), in1=bcs[:, 0:512], op=ALU.subtract)
    with p.grp() as g:
        for c in range(8):
            g.vector.tensor_tensor(out=xfc(c), in0=xgc(c), in1=bcs[:, 512:1024], op=ALU.mult)
    if dst_sb is None:
        return
    o = dst_off + tt * 512
    with p.grp() as g:
        for c in range(8):
            if c % 2 == 0:
                g.vector.tensor_copy(dst_sb[:, c * dst_T + o:c * dst_T + o + 512], xfc(c))
            else:
                g.scalar.copy(out=dst_sb[:, c * dst_T + o:c * dst_T + o + 512], in_=xfc(c))


def build_moe(tpc=2048):
    p = Prog()
    xin = p.din("xin", [1024, tpc], F32)
    gw = p.din("gw", [1024, 8], F32)
    w1 = p.din("w1", [8, 1024, 4096], BF16)
    w2 = p.din("w2", [8, 4096, 1024], BF16)
    b1d = p.din("b1d", [8, 128, 32], F32)
    b2d = p.din("b2d", [8, 1024], F32)
    ident = p.din("ident", [128, 128], F32)
    sel = p.din("sel", [8, 1024], F32)
    xout = p.dout("xout", [1024, tpc], F32)
    xnscr = p.dtmp("xnscr", [128, 8 * tpc], BF16)

    NT = tpc // 512

    ones128 = p.sb("ones128", 128, 1, F32)
    ones1 = p.sb("ones1", 1, 128, F32)
    onesrow = p.sb("onesrow", 1, 512, BF16)
    gwsb = p.sb("gwsb", 128, 64, F32)
    idsb = p.sb("idsb", 128, 128, F32)
    selsb = p.sb("selsb", 8, 1024, F32)
    wmT = p.sb("wmT", 8, tpc, F32)
    accum = p.sb("accum", 128, 8 * tpc, BF16)
    stats = p.sb("stats", 1, 1024, F32)
    varb = p.sb("varb", 1, 512, F32)
    m2b = p.sb("m2b", 1, 512, F32)
    bcs = p.sb("bcs", 128, 1024, F32)
    mx = p.sb("mx", 128, 1, F32)
    negmx = p.sb("negmx", 128, 1, F32)
    zrow = p.sb("zrow", 128, 1, F32)
    zrec = p.sb("zrec", 128, 1, F32)
    pr = p.sb("pr", 128, 8, F32)
    m8 = p.sb("m8", 128, 8, F32)
    wm = p.sb("wm", 128, 8, F32)

    xf = p.sb("xf", 128, 8 * 512, F32)
    xg = p.sb("xg", 128, 8 * 512, F32)
    xnb = p.sb("xnb", 128, 8 * 512, BF16)

    ps0 = p.psum("ps0", 1, 512)
    ps1 = p.psum("ps1", 1, 512)
    pbc = p.psum("pbc", 128, 1024)
    pglog = p.psum("pglog", 128, 8)
    pwmT = p.psum("pwmT", 8, 128)

    with p.grp() as g:
        g.vector.memset(ones128[:], 1.0)
        g.gpsimd.memset(ones1[:], 1.0)
        g.gpsimd.memset(onesrow[:], 1.0)
        for c in range(8):
            g.sync.dma_start(out=gwsb[:, 8 * c:8 * c + 8], in_=gw[c * 128:(c + 1) * 128, :])
        g.sync.dma_start(out=idsb[:], in_=ident[:])
        g.sync.dma_start(out=selsb[:], in_=sel[:])

    def xfc(c, j0=0, n=512):
        return xf[:, c * 512 + j0:c * 512 + j0 + n]

    for tt in range(NT):
        with p.grp() as g:
            for c in range(8):
                g.sync.dma_start(out=xfc(c), in_=xin[c * 128:(c + 1) * 128, tt * 512:(tt + 1) * 512])
        with p.grp() as g:
            for c in range(8):
                g.scalar.activation(out=xg[:, c * 512:(c + 1) * 512], in_=xfc(c), func=AF.Square)
        with p.grp() as g:
            for c in range(8):
                g.tensor.matmul(ps0[:], ones128[:], xfc(c), start=(c == 0), stop=(c == 7))
            for c in range(8):
                g.tensor.matmul(ps1[:], ones128[:], xg[:, c * 512:(c + 1) * 512],
                                start=(c == 0), stop=(c == 7))
        with p.grp() as g:
            g.scalar.mul(out=stats[0:1, 0:512], in_=ps0[:], mul=1.0 / 1024.0)
        with p.grp() as g:
            g.scalar.mul(out=varb[:], in_=ps1[:], mul=1.0 / 1024.0)
            g.vector.tensor_tensor(out=m2b[:], in0=stats[0:1, 0:512], in1=stats[0:1, 0:512], op=ALU.mult)
        with p.grp() as g:
            g.vector.tensor_tensor(out=varb[:], in0=varb[:], in1=m2b[:], op=ALU.subtract)
        with p.grp() as g:
            g.vector.tensor_scalar(out=varb[:], in0=varb[:], scalar1=LN_EPS, scalar2=None,
                                   op0=ALU.add, op1=ALU.bypass)
        with p.grp() as g:
            g.scalar.activation(out=m2b[:], in_=varb[:], func=AF.Ln)
        with p.grp() as g:
            g.scalar.activation(out=stats[0:1, 512:1024], in_=m2b[:], func=AF.Exp, scale=-0.5)
        with p.grp() as g:
            g.tensor.matmul(pbc[:, 0:512], ones1[:], stats[0:1, 0:512], start=True, stop=True)
            g.tensor.matmul(pbc[:, 512:1024], ones1[:], stats[0:1, 512:1024], start=True, stop=True)
        with p.grp() as g:
            g.scalar.copy(out=bcs[:], in_=pbc[:])
        with p.grp() as g:
            for c in range(8):
                g.vector.tensor_tensor(out=xg[:, c * 512:(c + 1) * 512], in0=xfc(c),
                                       in1=bcs[:, 0:512], op=ALU.subtract)
        with p.grp() as g:
            for c in range(8):
                g.vector.tensor_tensor(out=xfc(c), in0=xg[:, c * 512:(c + 1) * 512],
                                       in1=bcs[:, 512:1024], op=ALU.mult)
        with p.grp() as g:
            for c in range(8):
                eng = g.vector if c % 2 == 0 else g.scalar
                if c % 2 == 0:
                    g.vector.tensor_copy(xnb[:, c * 512:(c + 1) * 512], xfc(c))
                else:
                    g.scalar.copy(out=xnb[:, c * 512:(c + 1) * 512], in_=xfc(c))
        with p.grp() as g:
            for c in range(8):
                g.sync.dma_start(out=xnscr[:, c * tpc + tt * 512:c * tpc + (tt + 1) * 512],
                                 in_=xnb[:, c * 512:(c + 1) * 512])
        for j in range(4):
            with p.grp() as g:
                for c in range(8):
                    g.tensor.matmul(pglog[:], xfc(c, j * 128, 128), gwsb[:, 8 * c:8 * c + 8],
                                    start=(c == 0), stop=(c == 7))
            with p.grp() as g:
                g.vector.reduce_max(mx[:], pglog[:], axis=AX.X)
            with p.grp() as g:
                g.scalar.mul(out=negmx[:], in_=mx[:], mul=-1.0)
            with p.grp() as g:
                g.scalar.activation(out=pr[:], in_=pglog[:], func=AF.Exp, bias=negmx[:],
                                    accum_out=zrow[:])
            with p.grp() as g:
                g.vector.tensor_scalar(out=m8[:], in0=pglog[:], scalar1=mx[:], scalar2=None,
                                       op0=ALU.is_equal, op1=ALU.bypass)
            with p.grp() as g:
                g.vector.reciprocal(out=zrec[:], in_=zrow[:])
            with p.grp() as g:
                g.vector.tensor_scalar(out=wm[:], in0=m8[:], scalar1=zrec[:], scalar2=None,
                                       op0=ALU.mult, op1=ALU.bypass)
            with p.grp() as g:
                g.tensor.transpose(pwmT[:], wm[:], idsb[:])
            with p.grp() as g:
                g.scalar.copy(out=wmT[0:8, tt * 512 + j * 128:tt * 512 + (j + 1) * 128], in_=pwmT[:])

    for t_ in (xnb, xg, xf, wm, m8, pr, zrec, zrow, negmx, mx, bcs, m2b, varb, stats):
        p.free(t_)
    for t_ in (pwmT, pglog, pbc, ps1, ps0):
        p.free(t_)

    xnh = p.sb("xnh", 128, 8 * 1024, BF16)
    w1e = [p.sb(f"w1e{c}", 128, 4096, BF16) for c in range(8)]
    hsb = p.sb("hsb", 128, 32 * 1024, BF16)
    w2b = [p.sb(f"w2b{i}", 128, 1024, BF16) for i in range(2)]
    b1sb = p.sb("b1sb", 128, 32, F32)
    b2f = p.sb("b2f", 1, 1024, F32)
    b2sb = p.sb("b2sb", 1, 1024, BF16)
    wb2_ = [p.sb(f"wb{i}", 128, 512, F32) for i in range(2)]
    tb = p.sb("tb", 128, 512, F32)
    P = [p.psum(f"P{i}", 128, 512) for i in range(8)]

    nh = tpc // 1024
    for half in range(nh):
        with p.grp() as g:
            for c in range(8):
                g.sync.dma_start(out=xnh[:, c * 1024:(c + 1) * 1024],
                                 in_=xnscr[:, c * tpc + half * 1024:c * tpc + (half + 1) * 1024])
        for e in range(8):
            with p.grp() as g:
                for c in range(8):
                    g.sync.dma_start(out=w1e[c][:], in_=w1[e, c * 128:(c + 1) * 128, :])
                g.scalar.dma_start(out=b1sb[:], in_=b1d[e, :, :])
                g.scalar.dma_start(out=b2f[:], in_=b2d[e:e + 1, :])
            with p.grp() as g:
                g.vector.tensor_copy(b2sb[:], b2f[:])
            prev = None
            for r in range(32):
                for j in range(2):
                    idx = (2 * r + j) % 2
                    with p.grp() as g:
                        for c in range(8):
                            g.tensor.matmul(P[idx][:], w1e[c][:, r * 128:(r + 1) * 128],
                                            xnh[:, c * 1024 + j * 512:c * 1024 + (j + 1) * 512],
                                            start=(c == 0), stop=(c == 7))
                        if prev is not None:
                            pr_, pj, pidx = prev
                            g.scalar.activation(
                                out=hsb[:, pr_ * 1024 + pj * 512:pr_ * 1024 + (pj + 1) * 512],
                                in_=P[pidx][:], func=AF.Relu, bias=b1sb[:, pr_:pr_ + 1])
                    prev = (r, j, idx)
            with p.grp() as g:
                pr_, pj, pidx = prev
                g.scalar.activation(out=hsb[:, pr_ * 1024 + pj * 512:pr_ * 1024 + (pj + 1) * 512],
                                    in_=P[pidx][:], func=AF.Relu, bias=b1sb[:, pr_:pr_ + 1])
            for j in range(2):
                with p.grp() as g:
                    g.tensor.matmul(P[2][:], selsb[:, e * 128:(e + 1) * 128],
                                    wmT[0:8, half * 1024 + j * 512:half * 1024 + (j + 1) * 512],
                                    start=True, stop=True)
                with p.grp() as g:
                    g.scalar.copy(out=wb2_[j][:], in_=P[2][:])
            for j in range(2):
                with p.grp() as g:
                    g.sync.dma_start(out=w2b[0][:], in_=w2[e, 0:128, :])
                for r in range(32):
                    with p.grp() as g:
                        if r + 1 < 32:
                            g.sync.dma_start(out=w2b[(r + 1) % 2][:],
                                             in_=w2[e, (r + 1) * 128:(r + 2) * 128, :])
                        for d in range(8):
                            g.tensor.matmul(P[d][:], w2b[r % 2][:, d * 128:(d + 1) * 128],
                                            hsb[:, r * 1024 + j * 512:r * 1024 + (j + 1) * 512],
                                            start=(r == 0), stop=False)
                with p.grp() as g:
                    for d in range(8):
                        g.tensor.matmul(P[d][:], b2sb[0:1, d * 128:(d + 1) * 128], onesrow[:],
                                        start=False, stop=True)
                for d in range(8):
                    asl = accum[:, d * tpc + half * 1024 + j * 512:d * tpc + half * 1024 + (j + 1) * 512]
                    if e == 0:
                        with p.grp() as g:
                            g.vector.tensor_tensor(out=asl, in0=P[d][:], in1=wb2_[j][:], op=ALU.mult)
                    else:
                        with p.grp() as g:
                            g.vector.tensor_tensor(out=tb[:], in0=P[d][:], in1=wb2_[j][:], op=ALU.mult)
                        with p.grp() as g:
                            g.vector.tensor_tensor(out=asl, in0=asl, in1=tb[:], op=ALU.add)

    for t_ in [tb] + wb2_[::-1] + [b2sb, b2f, b1sb] + w2b[::-1] + [hsb] + w1e[::-1] + [xnh]:
        p.free(t_)

    xep = p.sb("xep", 128, 8 * 512, F32)
    oep = p.sb("oep", 128, 8 * 512, F32)
    for tt in range(NT):
        with p.grp() as g:
            for c in range(8):
                g.sync.dma_start(out=xep[:, c * 512:(c + 1) * 512],
                                 in_=xin[c * 128:(c + 1) * 128, tt * 512:(tt + 1) * 512])
        with p.grp() as g:
            for c in range(8):
                g.vector.tensor_tensor(out=oep[:, c * 512:(c + 1) * 512],
                                       in0=xep[:, c * 512:(c + 1) * 512],
                                       in1=accum[:, c * tpc + tt * 512:c * tpc + (tt + 1) * 512],
                                       op=ALU.add)
        with p.grp() as g:
            for c in range(8):
                g.scalar.dma_start(out=xout[c * 128:(c + 1) * 128, tt * 512:(tt + 1) * 512],
                                   in_=oep[:, c * 512:(c + 1) * 512])
    return p


_SEL8 = None


def _moe_consts():
    global _SEL8
    if _SEL8 is None:
        s = np.zeros((8, 1024), np.float32)
        for e in range(8):
            s[e, e * 128:(e + 1) * 128] = 1.0
        _SEL8 = s
    return _SEL8


def host_moe(xfm, wb, inputs):
    b1 = inputs["L0_ffn_b1"]
    b1d = np.ascontiguousarray(np.stack([b1[e].reshape(32, 128).T for e in range(8)]), np.float32)
    p = build_moe()
    ident = np.eye(128, dtype=np.float32)
    in_maps = []
    for c in range(8):
        b, h = c // 2, c % 2
        in_maps.append({
            "xin": np.ascontiguousarray(xfm[b, :, h * 2048:(h + 1) * 2048]),
            "gw": np.ascontiguousarray(inputs["L0_ffn_gate_w"].T, np.float32),
            "w1": wb["l0f_w1"], "w2": wb["l0f_w2"],
            "b1d": b1d, "b2d": np.ascontiguousarray(inputs["L0_ffn_b2"], np.float32),
            "ident": ident, "sel": _moe_consts(),
        })
    res = _run(p, in_maps)
    out = np.empty_like(xfm)
    for c in range(8):
        b, h = c // 2, c % 2
        out[b, :, h * 2048:(h + 1) * 2048] = res[c]["xout"]
    return out


def build_attn_a():
    p = Prog()
    TQ, TK = 2048, 4096
    xin = p.din("xin", [1024, TK], F32)
    wq = p.din("wq", [1024, 256], BF16)
    wk = p.din("wk", [1024, 128], BF16)
    wv = p.din("wv", [1024, 128], BF16)
    wo = p.din("wo", [256, 1024], BF16)
    cosq = p.din("cosq", [128, TQ], BF16)
    sinq = p.din("sinq", [128, TQ], BF16)
    cosk = p.din("cosk", [128, TK], BF16)
    sink = p.din("sink", [128, TK], BF16)
    kglob = p.din("kglob", [128, 32], F32)
    qbase = p.din("qbase", [1, 4], F32)
    qiota = p.din("qiota", [1, 512], F32)
    xout = p.dout("xout", [1024, TQ], F32)

    xn = p.sb("xn", 128, 8 * TK, BF16)
    krot = p.sb("krot", 128, TK, BF16)
    vtm = p.sb("vtm", 128, 32 * 128, BF16)
    wqsb = p.sb("wqsb", 128, 8 * 256, BF16)
    wksb = p.sb("wksb", 128, 8 * 128, BF16)
    wvsb = p.sb("wvsb", 128, 8 * 128, BF16)
    wosb = p.sb("wosb", 128, 2 * 1024, BF16)
    cqs = p.sb("cqs", 128, TQ, BF16)
    sqs = p.sb("sqs", 128, TQ, BF16)
    cks = p.sb("cks", 128, TK, BF16)
    sks = p.sb("sks", 128, TK, BF16)
    kgs = p.sb("kgs", 128, 32, F32)
    qbs = p.sb("qbs", 1, 4, F32)
    qis = p.sb("qis", 1, 512, F32)
    ones128b = p.sb("ones128b", 128, 1, BF16)
    ones128f = p.sb("ones128f", 128, 1, F32)
    ones1f = p.sb("ones1f", 1, 128, F32)
    onesr1b = p.sb("onesr1b", 1, 128, BF16)
    kmax2 = p.sb("kmax2", 1, 1, F32)

    with p.grp() as g:
        for c in range(8):
            g.sync.dma_start(out=wqsb[:, c * 256:(c + 1) * 256], in_=wq[c * 128:(c + 1) * 128, :])
            g.sync.dma_start(out=wksb[:, c * 128:(c + 1) * 128], in_=wk[c * 128:(c + 1) * 128, :])
            g.sync.dma_start(out=wvsb[:, c * 128:(c + 1) * 128], in_=wv[c * 128:(c + 1) * 128, :])
        for h in range(2):
            g.sync.dma_start(out=wosb[:, h * 1024:(h + 1) * 1024], in_=wo[h * 128:(h + 1) * 128, :])
        g.scalar.dma_start(out=cqs[:], in_=cosq[:])
        g.scalar.dma_start(out=sqs[:], in_=sinq[:])
        g.scalar.dma_start(out=cks[:], in_=cosk[:])
        g.scalar.dma_start(out=sks[:], in_=sink[:])
        g.scalar.dma_start(out=kgs[:], in_=kglob[:])
        g.scalar.dma_start(out=qbs[:], in_=qbase[:])
        g.scalar.dma_start(out=qis[:], in_=qiota[:])
        g.vector.memset(ones128b[:], 1.0)
        g.vector.memset(ones128f[:], 1.0)
        g.gpsimd.memset(ones1f[:], 1.0)
        g.gpsimd.memset(onesr1b[:], 1.0)
        g.gpsimd.memset(kmax2[:], 0.0)

    Pq = p.psum("Pq", 128, 512)
    Ps0 = p.psum("Ps0", 128, 512)
    Ps1 = p.psum("Ps1", 128, 512)
    Pav = p.psum("Pav", 128, 512)
    Pb = p.psum("Pb", 128, 512)
    Pden = p.psum("Pden", 1, 512)

    L = ln_alloc(p)
    for tt in range(8):
        emit_ln_tile(p, L, xin, tt, xn, TK, 0, Pq[:], Ps0[:])
    ln_free(p, L)

    kfp = p.sb("kfp", 128, 512, F32)
    s64a = p.sb("s64a", 64, 512, F32)
    s64a2 = p.sb("s64a2", 64, 512, F32)
    s64b = p.sb("s64b", 64, 512, F32)
    sqbuf = p.sb("sqbuf", 128, 512, F32)
    st1 = p.sb("st1", 1, 512, F32)
    tmax = p.sb("tmax", 1, 1, F32)
    qrot = p.sb("qrot", 128, 512, BF16)
    qgrow = p.sb("qgrow", 1, 512, F32)
    qgbc = p.sb("qgbc", 128, 512, F32)
    maskb = [p.sb(f"maskb{i}", 128, 512, BF16) for i in range(2)]
    Eb = [p.sb(f"Eb{i}", 128, 512, BF16) for i in range(2)]
    Em = [p.sb(f"Em{i}", 128, 512, BF16) for i in range(2)]
    mrow = p.sb("mrow", 1, 512, F32)
    negmb = p.sb("negmb", 1, 512, BF16)
    rden = p.sb("rden", 1, 512, F32)
    bcr = p.sb("bcr", 128, 512, F32)
    obuf = p.sb("obuf", 128, 1024, BF16)
    xres = p.sb("xres", 128, 512, F32)
    oout = p.sb("oout", 128, 512, F32)

    def xnc(c, off, n=512):
        return xn[:, c * TK + off:c * TK + off + n]

    for tt in range(8):
        with p.grp() as g:
            for c in range(8):
                g.tensor.matmul(Pq[:], wksb[:, c * 128:(c + 1) * 128], xnc(c, tt * 512),
                                start=(c == 0), stop=(c == 7))
        with p.grp() as g:
            g.scalar.copy(out=kfp[:], in_=Pq[:])
        ce_t = cks[0:64, tt * 512:(tt + 1) * 512]
        co_t = cks[64:128, tt * 512:(tt + 1) * 512]
        se_t = sks[0:64, tt * 512:(tt + 1) * 512]
        so_t = sks[64:128, tt * 512:(tt + 1) * 512]
        with p.grp() as g:
            g.vector.tensor_tensor(out=s64a[:], in0=kfp[0:64, :], in1=ce_t, op=ALU.mult)
            g.gpsimd.tensor_tensor(out=s64b[:], in0=kfp[64:128, :], in1=so_t, op=ALU.mult)
        with p.grp() as g:
            g.vector.tensor_tensor(out=krot[0:64, tt * 512:(tt + 1) * 512],
                                   in0=s64a[:], in1=s64b[:], op=ALU.subtract)
        with p.grp() as g:
            g.vector.tensor_tensor(out=s64a2[:], in0=kfp[64:128, :], in1=co_t, op=ALU.mult)
            g.gpsimd.tensor_tensor(out=s64b[:], in0=kfp[0:64, :], in1=se_t, op=ALU.mult)
        with p.grp() as g:
            g.vector.tensor_tensor(out=krot[64:128, tt * 512:(tt + 1) * 512],
                                   in0=s64a2[:], in1=s64b[:], op=ALU.add)
        with p.grp() as g:
            g.scalar.activation(out=sqbuf[:], in_=krot[:, tt * 512:(tt + 1) * 512], func=AF.Square)
        with p.grp() as g:
            g.tensor.matmul(Pb[0:1, :], ones128f[:], sqbuf[:], start=True, stop=True)
        with p.grp() as g:
            g.scalar.copy(out=st1[:], in_=Pb[0:1, :])
        with p.grp() as g:
            g.vector.reduce_max(tmax[:], st1[:], axis=AX.X)
        with p.grp() as g:
            g.vector.tensor_tensor(out=kmax2[:], in0=kmax2[:], in1=tmax[:], op=ALU.max)

    for kt in range(32):
        with p.grp() as g:
            for c in range(8):
                g.tensor.matmul(Ps0[:, 0:128], xnc(c, kt * 128, 128), wvsb[:, c * 128:(c + 1) * 128],
                                start=(c == 0), stop=(c == 7))
        with p.grp() as g:
            g.scalar.copy(out=vtm[:, kt * 128:(kt + 1) * 128], in_=Ps0[:, 0:128])

    for tt in range(4):
        with p.grp() as g:
            g.vector.tensor_scalar(out=qgrow[:], in0=qis[:], scalar1=qbs[0:1, tt:tt + 1],
                                   scalar2=None, op0=ALU.add, op1=ALU.bypass)
        with p.grp() as g:
            g.tensor.matmul(Pb[:], ones1f[:], qgrow[:], start=True, stop=True)
        with p.grp() as g:
            g.scalar.copy(out=qgbc[:], in_=Pb[:])
        for h in range(2):
            with p.grp() as g:
                for c in range(8):
                    g.tensor.matmul(Pq[:], wqsb[:, c * 256 + h * 128:c * 256 + (h + 1) * 128],
                                    xnc(c, tt * 512), start=(c == 0), stop=(c == 7))
            with p.grp() as g:
                g.scalar.copy(out=kfp[:], in_=Pq[:])
            ce_t = cqs[0:64, tt * 512:(tt + 1) * 512]
            co_t = cqs[64:128, tt * 512:(tt + 1) * 512]
            se_t = sqs[0:64, tt * 512:(tt + 1) * 512]
            so_t = sqs[64:128, tt * 512:(tt + 1) * 512]
            with p.grp() as g:
                g.vector.tensor_tensor(out=s64a[:], in0=kfp[0:64, :], in1=ce_t, op=ALU.mult)
                g.gpsimd.tensor_tensor(out=s64b[:], in0=kfp[64:128, :], in1=so_t, op=ALU.mult)
            with p.grp() as g:
                g.vector.tensor_tensor(out=qrot[0:64, :], in0=s64a[:], in1=s64b[:], op=ALU.subtract)
            with p.grp() as g:
                g.vector.tensor_tensor(out=s64a2[:], in0=kfp[64:128, :], in1=co_t, op=ALU.mult)
                g.gpsimd.tensor_tensor(out=s64b[:], in0=kfp[0:64, :], in1=se_t, op=ALU.mult)
            with p.grp() as g:
                g.vector.tensor_tensor(out=qrot[64:128, :], in0=s64a2[:], in1=s64b[:], op=ALU.add)
            with p.grp() as g:
                g.scalar.activation(out=sqbuf[:], in_=qrot[:], func=AF.Square)
            with p.grp() as g:
                g.tensor.matmul(Pb[0:1, :], ones128f[:], sqbuf[:], start=True, stop=True)
            with p.grp() as g:
                g.scalar.copy(out=st1[:], in_=Pb[0:1, :])
            with p.grp() as g:
                g.vector.tensor_scalar(out=st1[:], in0=st1[:], scalar1=kmax2[0:1, 0:1],
                                       scalar2=None, op0=ALU.mult, op1=ALU.bypass)
            with p.grp() as g:
                g.scalar.activation(out=st1[:], in_=st1[:], func=AF.Ln)
            with p.grp() as g:
                g.scalar.activation(out=mrow[:], in_=st1[:], func=AF.Exp, scale=0.5)
            with p.grp() as g:
                g.scalar.mul(out=mrow[:], in_=mrow[:], mul=-1.0)
            with p.grp() as g:
                g.vector.tensor_copy(negmb[:], mrow[:])
            for kt in range(32):
                pidx = kt % 2
                Ps = Ps0 if pidx == 0 else Ps1
                with p.grp() as g:
                    g.tensor.matmul(Ps[:], krot[:, kt * 128:(kt + 1) * 128], qrot[:],
                                    start=True, stop=False)
                    g.gpsimd.tensor_scalar(out=maskb[pidx][:], in0=qgbc[:],
                                           scalar1=kgs[:, kt:kt + 1], scalar2=0.0,
                                           op0=ALU.subtract, op1=ALU.is_ge)
                with p.grp() as g:
                    g.tensor.matmul(Ps[:], onesr1b[:], negmb[:], start=False, stop=True)
                with p.grp() as g:
                    g.scalar.activation(out=Eb[pidx][:], in_=Ps[:], func=AF.Exp)
                with p.grp() as g:
                    g.vector.scalar_tensor_tensor(out=Em[pidx][:], in0=Eb[pidx][:], scalar=1.0,
                                                  in1=maskb[pidx][:], op0=ALU.mult, op1=ALU.mult)
                with p.grp() as g:
                    g.tensor.matmul(Pden[:], ones128b[:], Em[pidx][:],
                                    start=(kt == 0), stop=(kt == 31))
                    g.tensor.matmul(Pav[:], vtm[:, kt * 128:(kt + 1) * 128], Em[pidx][:],
                                    start=(kt == 0), stop=(kt == 31))
            with p.grp() as g:
                g.scalar.activation(out=rden[:], in_=Pden[:], func=AF.Ln)
            with p.grp() as g:
                g.scalar.activation(out=rden[:], in_=rden[:], func=AF.Exp, scale=-1.0)
            with p.grp() as g:
                g.tensor.matmul(Pb[:], ones1f[:], rden[:], start=True, stop=True)
            with p.grp() as g:
                g.scalar.copy(out=bcr[:], in_=Pb[:])
            with p.grp() as g:
                g.vector.tensor_tensor(out=obuf[:, h * 512:(h + 1) * 512], in0=Pav[:],
                                       in1=bcr[:], op=ALU.mult)
        for d in range(8):
            with p.grp() as g:
                g.sync.dma_start(out=xres[:], in_=xin[d * 128:(d + 1) * 128, tt * 512:(tt + 1) * 512])
                for h in range(2):
                    g.tensor.matmul(Pq[:], wosb[:, h * 1024 + d * 128:h * 1024 + (d + 1) * 128],
                                    obuf[:, h * 512:(h + 1) * 512], start=(h == 0), stop=(h == 1))
            with p.grp() as g:
                g.vector.tensor_tensor(out=oout[:], in0=Pq[:], in1=xres[:], op=ALU.add)
            with p.grp() as g:
                g.scalar.dma_start(out=xout[d * 128:(d + 1) * 128, tt * 512:(tt + 1) * 512],
                                   in_=oout[:])
    return p


_ROPE_CACHE = None


def _rope_tables():
    global _ROPE_CACHE
    if _ROPE_CACHE is None:
        i = np.arange(64, dtype=np.float64)
        theta = 10000.0 ** (-i / 32.0)
        t = np.arange(4096, dtype=np.float64)
        ang = np.outer(theta, t)
        _ROPE_CACHE = (np.cos(ang), np.sin(ang))
    return _ROPE_CACHE


def host_attn_a(xfm, wb):
    cosg, sing = _rope_tables()
    scale = 1.0 / math.sqrt(128.0)
    p = build_attn_a()
    qiota = np.arange(512, dtype=np.float32).reshape(1, 512)
    kidx = np.arange(128, dtype=np.float32)
    in_maps = []
    for c in range(8):
        b, qh = c // 2, c % 2
        qlo = qh * 2048
        perm = np.concatenate([np.arange(qlo, qlo + 2048), np.arange(0, qlo),
                               np.arange(qlo + 2048, 4096)])
        in_maps.append({
            "xin": np.ascontiguousarray(xfm[b][:, perm]),
            "wq": wb["l0aa_wq"], "wk": wb["l0aa_wk"], "wv": wb["l0aa_wv"], "wo": wb["l0aa_wo"],
            "cosq": _bf16(np.vstack([cosg[:, qlo:qlo + 2048]] * 2) * scale),
            "sinq": _bf16(np.vstack([sing[:, qlo:qlo + 2048]] * 2) * scale),
            "cosk": _bf16(np.vstack([cosg[:, perm]] * 2)),
            "sink": _bf16(np.vstack([sing[:, perm]] * 2)),
            "kglob": np.ascontiguousarray(
                (perm.reshape(32, 128).T).astype(np.float32)),
            "qbase": (qlo + np.arange(4, dtype=np.float32).reshape(1, 4) * 512),
            "qiota": qiota,
        })
    res = _run(p, in_maps)
    out = np.empty_like(xfm)
    for c in range(8):
        b, qh = c // 2, c % 2
        out[b, :, qh * 2048:(qh + 1) * 2048] = res[c]["xout"]
    return out


def build_attn2():
    p = Prog()
    TQ, TK = 2048, 4096
    xin = p.din("xin", [1024, TK], F32)
    wq = p.din("wq", [1024, 1024], BF16)
    wk = p.din("wk", [1024, 1024], BF16)
    wv = p.din("wv", [1024, 1024], BF16)
    wo = p.din("wo", [1024, 1024], BF16)
    bqd = p.din("bqd", [128, 8], F32)
    bkd = p.din("bkd", [128, 8], F32)
    bvd = p.din("bvd", [1, 1024], F32)
    bod = p.din("bod", [128, 8], F32)
    xout = p.dout("xout", [1024, TQ], F32)
    oscr = p.dtmp("oscr", [128, 8 * TQ], BF16)

    xn = p.sb("xn", 128, 8 * TK, BF16)
    wosb = p.sb("wosb", 128, 8 * 1024, BF16)
    bqs = p.sb("bqs", 128, 8, F32)
    bks = p.sb("bks", 128, 8, F32)
    bvf = p.sb("bvf", 1, 1024, F32)
    bvs = p.sb("bvs", 1, 1024, BF16)
    bos = p.sb("bos", 128, 8, F32)
    ones128b = p.sb("ones128b", 128, 1, BF16)
    onestop = p.sb("onestop", 128, 1, F32)
    onesbot = p.sb("onesbot", 128, 1, F32)
    ones1f = p.sb("ones1f", 1, 128, F32)
    onesr1b = p.sb("onesr1b", 1, 128, BF16)

    with p.grp() as g:
        for c in range(8):
            g.sync.dma_start(out=wosb[:, c * 1024:(c + 1) * 1024], in_=wo[c * 128:(c + 1) * 128, :])
        g.scalar.dma_start(out=bqs[:], in_=bqd[:])
        g.scalar.dma_start(out=bks[:], in_=bkd[:])
        g.scalar.dma_start(out=bvf[:], in_=bvd[:])
        g.scalar.dma_start(out=bos[:], in_=bod[:])
        g.vector.memset(ones128b[:], 1.0)
        g.vector.memset(onestop[:], 1.0)
        g.gpsimd.memset(onesbot[:], 0.0)
        g.gpsimd.memset(ones1f[:], 1.0)
    with p.grp() as g:
        g.vector.memset(onestop[64:128, 0:1], 0.0)
        g.gpsimd.memset(onesbot[64:128, 0:1], 1.0)
        g.scalar.mul(out=bqs[:], in_=bqs[:], mul=0.125)
    with p.grp() as g:
        g.gpsimd.memset(onesr1b[:], 1.0)
        g.vector.tensor_copy(bvs[:], bvf[:])

    Pq = p.psum("Pq", 128, 512)
    Ps0 = p.psum("Ps0", 128, 512)
    Ps1 = p.psum("Ps1", 128, 512)
    Pav = p.psum("Pav", 128, 512)
    Pb = p.psum("Pb", 128, 512)
    Pden = p.psum("Pden", 1, 512)

    L = ln_alloc(p)
    for tt in range(8):
        emit_ln_tile(p, L, xin, tt, xn, TK, 0, Pq[:], Ps0[:])
    ln_free(p, L)

    kp = p.sb("kp", 128, TK, BF16)
    vp = p.sb("vp", 128, 32 * 128, BF16)
    wqp = p.sb("wqp", 128, 8 * 128, BF16)
    wkp = p.sb("wkp", 128, 8 * 128, BF16)
    wvp = p.sb("wvp", 128, 8 * 128, BF16)
    sqbuf = p.sb("sqbuf", 128, 512, F32)
    st1 = p.sb("st1", 1, 512, F32)
    tmax = p.sb("tmax", 1, 1, F32)
    kmax2 = p.sb("kmax2", 1, 2, F32)
    qm = p.sb("qm", 128, 512, BF16)
    qhat = p.sb("qhat", 128, 512, BF16)
    Eb = [p.sb(f"Eb{i}", 128, 512, BF16) for i in range(2)]
    mrow = p.sb("mrow", 1, 512, F32)
    negmb = p.sb("negmb", 1, 512, BF16)
    rden = p.sb("rden", 1, 512, F32)
    bcr = p.sb("bcr", 128, 512, F32)
    obuf = p.sb("obuf", 128, 512, BF16)
    xres = p.sb("xres", 128, 512, F32)
    oout = p.sb("oout", 128, 512, F32)

    def xnc(c, off, n=512):
        return xn[:, c * TK + off:c * TK + off + n]

    for hp in range(8):
        with p.grp() as g:
            for c in range(8):
                g.sync.dma_start(out=wqp[:, c * 128:(c + 1) * 128],
                                 in_=wq[c * 128:(c + 1) * 128, hp * 128:(hp + 1) * 128])
                g.sync.dma_start(out=wkp[:, c * 128:(c + 1) * 128],
                                 in_=wk[c * 128:(c + 1) * 128, hp * 128:(hp + 1) * 128])
                g.scalar.dma_start(out=wvp[:, c * 128:(c + 1) * 128],
                                   in_=wv[c * 128:(c + 1) * 128, hp * 128:(hp + 1) * 128])
        with p.grp() as g:
            g.vector.memset(kmax2[:], 0.0)
        for tt in range(8):
            with p.grp() as g:
                for c in range(8):
                    g.tensor.matmul(Pq[:], wkp[:, c * 128:(c + 1) * 128], xnc(c, tt * 512),
                                    start=(c == 0), stop=(c == 7))
            with p.grp() as g:
                g.scalar.activation(out=kp[:, tt * 512:(tt + 1) * 512], in_=Pq[:],
                                    func=AF.Identity, bias=bks[:, hp:hp + 1])
            with p.grp() as g:
                g.scalar.activation(out=sqbuf[:], in_=kp[:, tt * 512:(tt + 1) * 512], func=AF.Square)
            for hh, ov in ((0, onestop), (1, onesbot)):
                with p.grp() as g:
                    g.tensor.matmul(Pb[0:1, :], ov[:], sqbuf[:], start=True, stop=True)
                with p.grp() as g:
                    g.scalar.copy(out=st1[:], in_=Pb[0:1, :])
                with p.grp() as g:
                    g.vector.reduce_max(tmax[:], st1[:], axis=AX.X)
                with p.grp() as g:
                    g.vector.tensor_tensor(out=kmax2[0:1, hh:hh + 1], in0=kmax2[0:1, hh:hh + 1],
                                           in1=tmax[:], op=ALU.max)
        for kt in range(32):
            with p.grp() as g:
                for c in range(8):
                    g.tensor.matmul(Ps0[:, 0:128], xnc(c, kt * 128, 128),
                                    wvp[:, c * 128:(c + 1) * 128], start=(c == 0), stop=False)
                g.tensor.matmul(Ps0[:, 0:128], onesr1b[:], bvs[0:1, hp * 128:(hp + 1) * 128],
                                start=False, stop=True)
            with p.grp() as g:
                g.scalar.copy(out=vp[:, kt * 128:(kt + 1) * 128], in_=Ps0[:, 0:128])
        for tt in range(4):
            with p.grp() as g:
                for c in range(8):
                    g.tensor.matmul(Pq[:], wqp[:, c * 128:(c + 1) * 128], xnc(c, tt * 512),
                                    start=(c == 0), stop=(c == 7))
            for hh in range(2):
                with p.grp() as g:
                    g.vector.memset(qm[:], 0.0)
                lo, hi = (0, 64) if hh == 0 else (64, 128)
                with p.grp() as g:
                    g.scalar.activation(out=qm[lo:hi, :], in_=Pq[lo:hi, :], func=AF.Identity,
                                        scale=0.125, bias=bqs[lo:hi, hp:hp + 1])
                with p.grp() as g:
                    g.scalar.activation(out=sqbuf[:], in_=qm[:], func=AF.Square)
                with p.grp() as g:
                    g.tensor.matmul(Pb[0:1, :], onestop[:] if hh == 0 else onesbot[:],
                                    sqbuf[:], start=True, stop=True)
                with p.grp() as g:
                    g.scalar.copy(out=st1[:], in_=Pb[0:1, :])
                with p.grp() as g:
                    g.vector.tensor_scalar(out=st1[:], in0=st1[:], scalar1=kmax2[0:1, hh:hh + 1],
                                           scalar2=None, op0=ALU.mult, op1=ALU.bypass)
                with p.grp() as g:
                    g.scalar.activation(out=st1[:], in_=st1[:], func=AF.Ln)
                with p.grp() as g:
                    g.scalar.activation(out=mrow[:], in_=st1[:], func=AF.Exp, scale=0.5)
                with p.grp() as g:
                    g.scalar.mul(out=mrow[:], in_=mrow[:], mul=-1.0)
                with p.grp() as g:
                    g.vector.tensor_copy(negmb[:], mrow[:])
                for kt in range(32):
                    Ps = Ps0 if kt % 2 == 0 else Ps1
                    with p.grp() as g:
                        g.tensor.matmul(Ps[:], kp[:, kt * 128:(kt + 1) * 128], qm[:],
                                        start=True, stop=False)
                    with p.grp() as g:
                        g.tensor.matmul(Ps[:], onesr1b[:], negmb[:], start=False, stop=True)
                    with p.grp() as g:
                        g.scalar.activation(out=Eb[kt % 2][:], in_=Ps[:], func=AF.Exp)
                    with p.grp() as g:
                        g.tensor.matmul(Pden[:], ones128b[:], Eb[kt % 2][:],
                                        start=(kt == 0), stop=(kt == 31))
                        g.tensor.matmul(Pav[0:64, :], vp[:, kt * 128 + hh * 64:kt * 128 + hh * 64 + 64],
                                        Eb[kt % 2][:], start=(kt == 0), stop=(kt == 31))
                with p.grp() as g:
                    g.scalar.activation(out=rden[:], in_=Pden[:], func=AF.Ln)
                with p.grp() as g:
                    g.scalar.activation(out=rden[:], in_=rden[:], func=AF.Exp, scale=-1.0)
                with p.grp() as g:
                    g.tensor.matmul(Pb[:], ones1f[:], rden[:], start=True, stop=True)
                with p.grp() as g:
                    g.scalar.copy(out=bcr[:], in_=Pb[:])
                with p.grp() as g:
                    g.vector.tensor_tensor(out=obuf[hh * 64:(hh + 1) * 64, :], in0=Pav[0:64, :],
                                           in1=bcr[0:64, :], op=ALU.mult)
            with p.grp() as g:
                g.scalar.dma_start(out=oscr[:, hp * TQ + tt * 512:hp * TQ + (tt + 1) * 512],
                                   in_=obuf[:])

    oc = p.sb("oc", 128, 8 * 512, BF16)
    o1 = p.sb("o1", 128, 512, F32)
    for tt in range(4):
        with p.grp() as g:
            for hp in range(8):
                g.sync.dma_start(out=oc[:, hp * 512:(hp + 1) * 512],
                                 in_=oscr[:, hp * TQ + tt * 512:hp * TQ + (tt + 1) * 512])
        for d in range(8):
            with p.grp() as g:
                g.sync.dma_start(out=xres[:], in_=xin[d * 128:(d + 1) * 128, tt * 512:(tt + 1) * 512])
                for hp in range(8):
                    g.tensor.matmul(Pq[:], wosb[:, hp * 1024 + d * 128:hp * 1024 + (d + 1) * 128],
                                    oc[:, hp * 512:(hp + 1) * 512], start=(hp == 0), stop=(hp == 7))
            with p.grp() as g:
                g.scalar.activation(out=o1[:], in_=Pq[:], func=AF.Identity, bias=bos[:, d:d + 1])
            with p.grp() as g:
                g.vector.tensor_tensor(out=oout[:], in0=o1[:], in1=xres[:], op=ALU.add)
            with p.grp() as g:
                g.scalar.dma_start(out=xout[d * 128:(d + 1) * 128, tt * 512:(tt + 1) * 512],
                                   in_=oout[:])
    return p


def host_attn2(xfm, wb, inputs):
    p = build_attn2()
    bq = np.ascontiguousarray(inputs["L1_attn_a_bq"].reshape(8, 128).T, np.float32)
    bk = np.ascontiguousarray(inputs["L1_attn_a_bk"].reshape(8, 128).T, np.float32)
    bv = np.ascontiguousarray(inputs["L1_attn_a_bv"].reshape(1, 1024), np.float32)
    bo = np.ascontiguousarray(inputs["L1_attn_a_bo"].reshape(8, 128).T, np.float32)
    in_maps = []
    for c in range(8):
        b, qh = c // 2, c % 2
        qlo = qh * 2048
        perm = np.concatenate([np.arange(qlo, qlo + 2048), np.arange(0, qlo),
                               np.arange(qlo + 2048, 4096)])
        in_maps.append({
            "xin": np.ascontiguousarray(xfm[b][:, perm]),
            "wq": wb["l1aa_wq"], "wk": wb["l1aa_wk"], "wv": wb["l1aa_wv"], "wo": wb["l1aa_wo"],
            "bqd": bq, "bkd": bk, "bvd": bv, "bod": bo,
        })
    res = _run(p, in_maps)
    out = np.empty_like(xfm)
    for c in range(8):
        b, qh = c // 2, c % 2
        out[b, :, qh * 2048:(qh + 1) * 2048] = res[c]["xout"]
    return out


def build_ffn2(tpc=2048):
    p = Prog()
    xin = p.din("xin", [1024, tpc], F32)
    w1qd = p.din("w1qd", [1024, 4096], BF16)
    w2qd = p.din("w2qd", [4096, 1024], BF16)
    wsd = p.din("wsd", [1, 2], F32)
    b1d = p.din("b1d", [128, 32], F32)
    b2d = p.din("b2d", [128, 8], F32)
    identd = p.din("identd", [128, 128], F32)
    xout = p.dout("xout", [1024, tpc], F32)

    idsb = p.sb("idsb", 128, 128, F32)
    wss = p.sb("wss", 1, 2, F32)
    b1s = p.sb("b1s", 128, 32, F32)
    b2s = p.sb("b2s", 128, 8, F32)
    ones1f = p.sb("ones1f", 1, 128, F32)
    with p.grp() as g:
        g.sync.dma_start(out=idsb[:], in_=identd[:])
        g.scalar.dma_start(out=wss[:], in_=wsd[:])
        g.scalar.dma_start(out=b1s[:], in_=b1d[:])
        g.scalar.dma_start(out=b2s[:], in_=b2d[:])
        g.gpsimd.memset(ones1f[:], 1.0)

    PS = [p.psum(f"PS{i}", 128, 512) for i in range(8)]
    L = ln_alloc(p)

    w1sb = p.sb("w1sb", 128, 8 * 4096, BF16)
    w2sb = [p.sb(f"w2sb{i}", 128, 1024, BF16) for i in range(2)]
    hsb = p.sb("hsb", 128, 32 * 512, BF16)
    xq = p.sb("xq", 128, 8 * 512, BF16)
    hqp = [p.sb(f"hqp{i}", 128, 512, BF16) for i in range(2)]
    macc = p.sb("macc", 128, 512, F32)
    absb = [p.sb(f"absb{i}", 128, 512, F32) for i in range(2)]
    mtb = p.sb("mtb", 128, 128, F32)
    xscol = p.sb("xscol", 128, 4, F32)
    rrow = p.sb("rrow", 1, 512, F32)
    srow = p.sb("srow", 1, 512, F32)
    rbc = p.sb("rbc", 128, 512, F32)
    sbc = p.sb("sbc", 128, 512, F32)
    qtmp = p.sb("qtmp", 128, 512, F32)
    ot1 = p.sb("ot1", 128, 512, F32)
    oout = p.sb("oout", 128, 512, F32)

    with p.grp() as g:
        for c in range(8):
            g.sync.dma_start(out=w1sb[:, c * 4096:(c + 1) * 4096],
                             in_=w1qd[c * 128:(c + 1) * 128, :])

    def quant_dance(p, src_sb, nch, which_ws):
        with p.grp() as g:
            g.scalar.activation(out=macc[:], in_=src_sb[:, 0:512], func=AF.Abs)
        with p.grp() as g:
            g.scalar.activation(out=absb[1][:], in_=src_sb[:, 512:1024], func=AF.Abs)
        for c in range(1, nch):
            with p.grp() as g:
                if c + 1 < nch:
                    g.scalar.activation(out=absb[(c + 1) % 2][:],
                                        in_=src_sb[:, (c + 1) * 512:(c + 2) * 512], func=AF.Abs)
                g.vector.tensor_tensor(out=macc[:], in0=macc[:], in1=absb[c % 2][:], op=ALU.max)
        for j in range(4):
            with p.grp() as g:
                g.tensor.transpose(PS[2][:, 0:128], macc[:, j * 128:(j + 1) * 128], idsb[:])
            with p.grp() as g:
                g.scalar.copy(out=mtb[:], in_=PS[2][:, 0:128])
            with p.grp() as g:
                g.vector.reduce_max(xscol[:, j:j + 1], mtb[:], axis=AX.X)
        with p.grp() as g:
            g.vector.tensor_scalar(out=xscol[:], in0=xscol[:], scalar1=1e-5, scalar2=None,
                                   op0=ALU.max, op1=ALU.bypass)
        for j in range(4):
            with p.grp() as g:
                g.tensor.transpose(PS[3][0:1, 0:128], xscol[:, j:j + 1], idsb[:])
            with p.grp() as g:
                g.scalar.copy(out=srow[0:1, j * 128:(j + 1) * 128], in_=PS[3][0:1, 0:128])
        with p.grp() as g:
            g.scalar.activation(out=rrow[:], in_=srow[:], func=AF.Ln)
        with p.grp() as g:
            g.scalar.activation(out=rrow[:], in_=rrow[:], func=AF.Exp, scale=-1.0)
        with p.grp() as g:
            g.vector.tensor_scalar(out=rrow[:], in0=rrow[:], scalar1=127.0, scalar2=None,
                                   op0=ALU.mult, op1=ALU.bypass)
            g.scalar.mul(out=srow[:], in_=srow[:], mul=1.0 / 127.0)
        with p.grp() as g:
            g.vector.tensor_scalar(out=srow[:], in0=srow[:], scalar1=wss[0:1, which_ws:which_ws + 1],
                                   scalar2=None, op0=ALU.mult, op1=ALU.bypass)
        with p.grp() as g:
            g.tensor.matmul(PS[0][:], ones1f[:], rrow[:], start=True, stop=True)
            g.tensor.matmul(PS[1][:], ones1f[:], srow[:], start=True, stop=True)
        with p.grp() as g:
            g.scalar.copy(out=rbc[:], in_=PS[0][:])
            g.vector.tensor_copy(sbc[:], PS[1][:])

    def quantize(p, src, dst):
        with p.grp() as g:
            g.vector.tensor_tensor(out=qtmp[:], in0=src, in1=rbc[:], op=ALU.mult)
        with p.grp() as g:
            g.vector.tensor_scalar(out=qtmp[:], in0=qtmp[:], scalar1=RINT_C, scalar2=None,
                                   op0=ALU.add, op1=ALU.bypass)
        with p.grp() as g:
            g.vector.tensor_scalar(out=qtmp[:], in0=qtmp[:], scalar1=RINT_C, scalar2=None,
                                   op0=ALU.subtract, op1=ALU.bypass)
        with p.grp() as g:
            g.vector.tensor_scalar(out=dst, in0=qtmp[:], scalar1=127.0, scalar2=-128.0,
                                   op0=ALU.min, op1=ALU.max)

    NT = tpc // 512
    for tt in range(NT):
        emit_ln_tile(p, L, xin, tt, None, 0, 0, PS[0][:], PS[1][:])
        quant_dance(p, L["xf"], 8, 0)
        for c in range(8):
            quantize(p, L["xf"][:, c * 512:(c + 1) * 512], xq[:, c * 512:(c + 1) * 512])
        for r in range(32):
            Ph = PS[r % 2]
            with p.grp() as g:
                for c in range(8):
                    g.tensor.matmul(Ph[:], w1sb[:, c * 4096 + r * 128:c * 4096 + (r + 1) * 128],
                                    xq[:, c * 512:(c + 1) * 512], start=(c == 0), stop=(c == 7))
            with p.grp() as g:
                g.vector.tensor_tensor(out=qtmp[:], in0=Ph[:], in1=sbc[:], op=ALU.mult)
            with p.grp() as g:
                g.scalar.activation(out=hsb[:, r * 512:(r + 1) * 512], in_=qtmp[:],
                                    func=AF.Gelu, bias=b1s[:, r:r + 1])
        quant_dance(p, hsb, 32, 1)
        quantize(p, hsb[:, 0:512], hqp[0][:])
        with p.grp() as g:
            g.sync.dma_start(out=w2sb[0][:], in_=w2qd[0:128, :])
        for r in range(32):
            with p.grp() as g:
                if r + 1 < 32:
                    g.sync.dma_start(out=w2sb[(r + 1) % 2][:],
                                     in_=w2qd[(r + 1) * 128:(r + 2) * 128, :])
                for d in range(8):
                    g.tensor.matmul(PS[d][:], w2sb[r % 2][:, d * 128:(d + 1) * 128],
                                    hqp[r % 2][:], start=(r == 0), stop=(r == 31))
            if r + 1 < 32:
                quantize(p, hsb[:, (r + 1) * 512:(r + 2) * 512], hqp[(r + 1) % 2][:])
        with p.grp() as g:
            for c in range(8):
                g.sync.dma_start(out=L["xg"][:, c * 512:(c + 1) * 512],
                                 in_=xin[c * 128:(c + 1) * 128, tt * 512:(tt + 1) * 512])
        for d in range(8):
            with p.grp() as g:
                g.vector.tensor_tensor(out=qtmp[:], in0=PS[d][:], in1=sbc[:], op=ALU.mult)
            with p.grp() as g:
                g.scalar.activation(out=ot1[:], in_=qtmp[:], func=AF.Identity, bias=b2s[:, d:d + 1])
            with p.grp() as g:
                g.vector.tensor_tensor(out=oout[:], in0=ot1[:],
                                       in1=L["xg"][:, d * 512:(d + 1) * 512], op=ALU.add)
            with p.grp() as g:
                g.scalar.dma_start(out=xout[d * 128:(d + 1) * 128, tt * 512:(tt + 1) * 512],
                                   in_=oout[:])
    return p


def host_ffn2(xfm, wb, inputs):
    p = build_ffn2()
    b1 = np.ascontiguousarray(inputs["L1_ffn_b1"].reshape(32, 128).T, np.float32)
    b2 = np.ascontiguousarray(inputs["L1_ffn_b2"].reshape(8, 128).T, np.float32)
    ident = np.eye(128, dtype=np.float32)
    ws = np.ascontiguousarray(wb["l1f_wscale"].reshape(1, 2), np.float32)
    in_maps = []
    for c in range(8):
        b, h = c // 2, c % 2
        in_maps.append({
            "xin": np.ascontiguousarray(xfm[b, :, h * 2048:(h + 1) * 2048]),
            "w1qd": wb["l1f_w1q"], "w2qd": wb["l1f_w2q"], "wsd": ws,
            "b1d": b1, "b2d": b2, "identd": ident,
        })
    res = _run(p, in_maps)
    out = np.empty_like(xfm)
    for c in range(8):
        b, h = c // 2, c % 2
        out[b, :, h * 2048:(h + 1) * 2048] = res[c]["xout"]
    return out


CPI2 = 1.0 + math.pi / 2.0
CLAMPC = 2e-7 - 1e-14


def build_attn_b():
    p = Prog()
    T = 4096
    xin = p.din("xin", [1024, T], F32)
    wqs = p.din("wqs", [1024, 512], BF16)
    wks = p.din("wks", [1024, 512], BF16)
    wvs = p.din("wvs", [1024, 512], BF16)
    bqd = p.din("bqd", [128, 4], F32)
    bkd = p.din("bkd", [128, 4], F32)
    bvd = p.din("bvd", [1, 512], F32)
    ohd = p.dout("ohd", [512, T], BF16)
    ascr = p.dtmp("ascr", [128, 2 * 8 * 32 * 512], BF16)

    xn = p.sb("xn", 128, 8 * T, BF16)
    ones1f = p.sb("ones1f", 1, 128, F32)
    onestop = p.sb("onestop", 128, 1, F32)
    onesbot = p.sb("onesbot", 128, 1, F32)
    rtop = p.sb("rtop", 1, 128, F32)
    rbot = p.sb("rbot", 1, 128, F32)
    cpi = p.sb("cpi", 128, 1, F32)
    bqs = p.sb("bqs", 128, 4, F32)
    bks = p.sb("bks", 128, 4, F32)
    bvf = p.sb("bvf", 1, 512, F32)
    bvs = p.sb("bvs", 1, 512, BF16)
    onesr1b = p.sb("onesr1b", 1, 128, BF16)

    with p.grp() as g:
        g.scalar.dma_start(out=bqs[:], in_=bqd[:])
        g.scalar.dma_start(out=bks[:], in_=bkd[:])
        g.scalar.dma_start(out=bvf[:], in_=bvd[:])
        g.vector.memset(onestop[:], 1.0)
        g.gpsimd.memset(onesbot[:], 0.0)
        g.vector.memset(rtop[:], 1.0)
        g.gpsimd.memset(rbot[:], 0.0)
        g.vector.memset(cpi[:], CPI2)
        g.gpsimd.memset(ones1f[:], 1.0)
        g.vector.memset(onesr1b[:], 1.0)
    with p.grp() as g:
        g.vector.memset(onestop[64:128, 0:1], 0.0)
        g.gpsimd.memset(onesbot[64:128, 0:1], 1.0)
        g.vector.memset(rtop[0:1, 64:128], 0.0)
        g.gpsimd.memset(rbot[0:1, 64:128], 1.0)
        g.scalar.copy(out=bvs[:], in_=bvf[:])

    Pq = p.psum("Pq", 128, 512)
    Pav = p.psum("Pav", 128, 512)
    Ps0 = p.psum("Ps0", 128, 1024)
    Ps1 = p.psum("Ps1", 128, 1024)
    Pb = p.psum("Pb", 128, 512)
    Pden = p.psum("Pden", 1, 512)

    L = ln_alloc(p)
    for tt in range(8):
        emit_ln_tile(p, L, xin, tt, xn, T, 0, Pq[:], Pav[:])
    ln_free(p, L)

    kfp = p.sb("kfp", 128, 512, F32)
    khat = p.sb("khat", 128, T, BF16)
    vtm = p.sb("vtm", 128, 32 * 128, BF16)
    vsc = p.sb("vsc", 128, 32 * 128, BF16)
    sq5 = p.sb("sq5", 128, 512, F32)
    nrm = p.sb("nrm", 1, 1024, F32)
    rkbc = p.sb("rkbc", 128, 512, F32)
    qm = p.sb("qm", 128, 512, BF16)
    qhat = p.sb("qhat", 128, 512, BF16)
    sqa = p.sb("sqa", 128, 1024, F32)
    rchn = p.sb("rchn", 128, 1024, F32)
    ab = [p.sb(f"ab{i}", 128, 1024, BF16) for i in range(2)]
    ncq = p.sb("ncq", 128, 32, F32)
    ncacc = [p.sb(f"ncacc{i}", 128, 32, F32) for i in range(2)]
    crb = p.sb("crb", 128, 32, BF16)
    rden = p.sb("rden", 1, 512, F32)
    bcr = p.sb("bcr", 128, 512, F32)
    obuf = p.sb("obuf", 128, 512, BF16)

    def xnc(c, off, n=512):
        return xn[:, c * T + off:c * T + off + n]

    wqp = p.sb("wqp", 128, 8 * 128, BF16)
    wkp = p.sb("wkp", 128, 8 * 128, BF16)
    wvp = p.sb("wvp", 128, 8 * 128, BF16)

    def l2pair(pidx, bias, dst, toff):
        with p.grp() as g:
            g.scalar.activation(out=kfp[:], in_=Pq[:], func=AF.Identity, bias=bias)
        with p.grp() as g:
            g.scalar.activation(out=sq5[:], in_=kfp[:], func=AF.Square)
        with p.grp() as g:
            g.tensor.matmul(Pb[0:1, :], onestop[:], sq5[:], start=True, stop=True)
        with p.grp() as g:
            g.scalar.copy(out=nrm[0:1, 0:512], in_=Pb[0:1, :])
        with p.grp() as g:
            g.tensor.matmul(Pb[0:1, :], onesbot[:], sq5[:], start=True, stop=True)
        with p.grp() as g:
            g.scalar.copy(out=nrm[0:1, 512:1024], in_=Pb[0:1, :])
        with p.grp() as g:
            g.vector.tensor_scalar(out=nrm[:], in0=nrm[:], scalar1=1e-24, scalar2=None,
                                   op0=ALU.max, op1=ALU.bypass)
        with p.grp() as g:
            g.scalar.activation(out=nrm[:], in_=nrm[:], func=AF.Ln)
        with p.grp() as g:
            g.scalar.activation(out=nrm[:], in_=nrm[:], func=AF.Exp, scale=-0.5)
        with p.grp() as g:
            g.tensor.matmul(Pb[:], rtop[:], nrm[0:1, 0:512], start=True, stop=False)
            g.tensor.matmul(Pb[:], rbot[:], nrm[0:1, 512:1024], start=False, stop=True)
        with p.grp() as g:
            g.scalar.copy(out=rkbc[:], in_=Pb[:])
        with p.grp() as g:
            g.vector.tensor_tensor(out=dst[:, toff:toff + 512], in0=kfp[:], in1=rkbc[:],
                                   op=ALU.mult)

    for hp in range(4):
        with p.grp() as g:
            for c in range(8):
                g.sync.dma_start(out=wqp[:, c * 128:(c + 1) * 128],
                                 in_=wqs[c * 128:(c + 1) * 128, hp * 128:(hp + 1) * 128])
                g.sync.dma_start(out=wkp[:, c * 128:(c + 1) * 128],
                                 in_=wks[c * 128:(c + 1) * 128, hp * 128:(hp + 1) * 128])
                g.scalar.dma_start(out=wvp[:, c * 128:(c + 1) * 128],
                                   in_=wvs[c * 128:(c + 1) * 128, hp * 128:(hp + 1) * 128])
        for tt in range(8):
            with p.grp() as g:
                for c in range(8):
                    g.tensor.matmul(Pq[:], wkp[:, c * 128:(c + 1) * 128],
                                    xnc(c, tt * 512), start=(c == 0), stop=(c == 7))
            l2pair(hp, bks[:, hp:hp + 1], khat, tt * 512)
        for kt in range(32):
            with p.grp() as g:
                for c in range(8):
                    g.tensor.matmul(Pav[:, 0:128], xnc(c, kt * 128, 128),
                                    wvp[:, c * 128:(c + 1) * 128],
                                    start=(c == 0), stop=False)
                g.tensor.matmul(Pav[:, 0:128], onesr1b[:],
                                bvs[0:1, hp * 128:(hp + 1) * 128], start=False, stop=True)
            with p.grp() as g:
                g.scalar.copy(out=vtm[:, kt * 128:(kt + 1) * 128], in_=Pav[:, 0:128])
        with p.grp() as g:
            g.vector.memset(ncacc[0][:], 0.0)
            g.gpsimd.memset(ncacc[1][:], 0.0)
        for tt in range(8):
            with p.grp() as g:
                for c in range(8):
                    g.tensor.matmul(Pq[:], wqp[:, c * 128:(c + 1) * 128],
                                    xnc(c, tt * 512), start=(c == 0), stop=(c == 7))
            l2pair(hp, bqs[:, hp:hp + 1], qhat, 0)
            for hh in range(2):
                lo, hi = (0, 64) if hh == 0 else (64, 128)
                with p.grp() as g:
                    g.vector.memset(qm[:], 0.0)
                with p.grp() as g:
                    g.scalar.copy(out=qm[lo:hi, :], in_=qhat[lo:hi, :])
                for k2 in range(16):
                    Ps = Ps0 if k2 % 2 == 0 else Ps1
                    with p.grp() as g:
                        g.tensor.matmul(Ps[:, 0:512], khat[:, (2 * k2) * 128:(2 * k2 + 1) * 128],
                                        qm[:], start=True, stop=True)
                        g.tensor.matmul(Ps[:, 512:1024], khat[:, (2 * k2 + 1) * 128:(2 * k2 + 2) * 128],
                                        qm[:], start=True, stop=True)
                    with p.grp() as g:
                        g.scalar.activation(out=sqa[:], in_=Ps[:], func=AF.Square)
                    with p.grp() as g:
                        g.vector.tensor_scalar(out=sqa[:], in0=sqa[:], scalar1=-1.0, scalar2=1.0,
                                               op0=ALU.mult, op1=ALU.add)
                    with p.grp() as g:
                        g.vector.tensor_scalar(out=sqa[:], in0=sqa[:], scalar1=CLAMPC, scalar2=None,
                                               op0=ALU.max, op1=ALU.bypass)
                    with p.grp() as g:
                        g.scalar.activation(out=sqa[:], in_=sqa[:], func=AF.Ln)
                    with p.grp() as g:
                        g.scalar.activation(out=rchn[:], in_=sqa[:], func=AF.Exp, scale=-0.5)
                    with p.grp() as g:
                        g.vector.tensor_tensor(out=rchn[:], in0=Ps[:], in1=rchn[:], op=ALU.mult)
                    with p.grp() as g:
                        g.scalar.activation(out=rchn[:], in_=rchn[:], func=AF.Arctan)
                    with p.grp() as g:
                        g.scalar.activation(out=rchn[:], in_=rchn[:], func=AF.Ln, scale=-1.0,
                                            bias=cpi[:])
                    abt = ab[k2 % 2]
                    with p.grp() as g:
                        g.scalar.activation(out=abt[:, 0:512], in_=rchn[:, 0:512], func=AF.Exp,
                                            scale=-3.2, accum_out=ncq[:, 2 * k2:2 * k2 + 1])
                    with p.grp() as g:
                        g.scalar.activation(out=abt[:, 512:1024], in_=rchn[:, 512:1024], func=AF.Exp,
                                            scale=-3.2, accum_out=ncq[:, 2 * k2 + 1:2 * k2 + 2])
                    with p.grp() as g:
                        g.scalar.dma_start(
                            out=ascr[:, ((hh * 8 + tt) * 32 + 2 * k2) * 512:((hh * 8 + tt) * 32 + 2 * k2 + 2) * 512],
                            in_=abt[:])
                with p.grp() as g:
                    g.vector.tensor_tensor(out=ncacc[hh][:], in0=ncacc[hh][:], in1=ncq[:], op=ALU.add)
        for hh in range(2):
            with p.grp() as g:
                g.scalar.activation(out=ncacc[hh][:], in_=ncacc[hh][:], func=AF.Ln)
            with p.grp() as g:
                g.scalar.activation(out=ncacc[hh][:], in_=ncacc[hh][:], func=AF.Exp, scale=-1.0)
            with p.grp() as g:
                g.vector.tensor_copy(crb[:], ncacc[hh][:])
            with p.grp() as g:
                for kt in range(32):
                    eng = g.vector if kt % 2 == 0 else g.gpsimd
                    eng.tensor_scalar(out=vsc[:, kt * 128 + hh * 64:kt * 128 + hh * 64 + 64],
                                      in0=vtm[:, kt * 128 + hh * 64:kt * 128 + hh * 64 + 64],
                                      scalar1=ncacc[hh][:, kt:kt + 1], scalar2=None,
                                      op0=ALU.mult, op1=ALU.bypass)
            for tt in range(8):
                with p.grp() as g:
                    g.scalar.dma_start(out=ab[0][:],
                                       in_=ascr[:, ((hh * 8 + tt) * 32 + 0) * 512:((hh * 8 + tt) * 32 + 2) * 512])
                for k2 in range(16):
                    abt = ab[k2 % 2]
                    with p.grp() as g:
                        if k2 + 1 < 16:
                            g.scalar.dma_start(
                                out=ab[(k2 + 1) % 2][:],
                                in_=ascr[:, ((hh * 8 + tt) * 32 + 2 * k2 + 2) * 512:((hh * 8 + tt) * 32 + 2 * k2 + 4) * 512])
                        for dk in range(2):
                            kt = 2 * k2 + dk
                            g.tensor.matmul(Pden[:], crb[:, kt:kt + 1], abt[:, dk * 512:(dk + 1) * 512],
                                            start=(kt == 0), stop=(kt == 31))
                            g.tensor.matmul(Pav[0:64, :], vsc[:, kt * 128 + hh * 64:kt * 128 + hh * 64 + 64],
                                            abt[:, dk * 512:(dk + 1) * 512],
                                            start=(kt == 0), stop=(kt == 31))
                with p.grp() as g:
                    g.scalar.activation(out=rden[:], in_=Pden[:], func=AF.Ln)
                with p.grp() as g:
                    g.scalar.activation(out=rden[:], in_=rden[:], func=AF.Exp, scale=-1.0)
                with p.grp() as g:
                    g.tensor.matmul(Pb[:], ones1f[:], rden[:], start=True, stop=True)
                with p.grp() as g:
                    g.scalar.copy(out=bcr[:], in_=Pb[:])
                with p.grp() as g:
                    g.vector.tensor_tensor(out=obuf[0:64, :], in0=Pav[0:64, :], in1=bcr[0:64, :],
                                           op=ALU.mult)
                with p.grp() as g:
                    g.scalar.dma_start(out=ohd[(hp * 2 + hh) * 64:(hp * 2 + hh + 1) * 64, tt * 512:(tt + 1) * 512],
                                       in_=obuf[0:64, :])
    return p


def build_oproj(tpc=2048):
    p = Prog()
    xin = p.din("xin", [1024, tpc], F32)
    ofm = p.din("ofm", [1024, tpc], BF16)
    wT = p.din("wT", [1024, 1024], BF16)
    bd = p.din("bd", [128, 8], F32)
    xout = p.dout("xout", [1024, tpc], F32)

    wsb = p.sb("wsb", 128, 8 * 1024, BF16)
    bs = p.sb("bs", 128, 8, F32)
    osb = p.sb("osb", 128, 8 * 512, BF16)
    xres = p.sb("xres", 128, 512, F32)
    t1 = p.sb("t1", 128, 512, F32)
    oout = p.sb("oout", 128, 512, F32)
    Pq = p.psum("Pq", 128, 512)

    with p.grp() as g:
        for c in range(8):
            g.sync.dma_start(out=wsb[:, c * 1024:(c + 1) * 1024], in_=wT[c * 128:(c + 1) * 128, :])
        g.scalar.dma_start(out=bs[:], in_=bd[:])
    for tt in range(tpc // 512):
        with p.grp() as g:
            for c in range(8):
                g.sync.dma_start(out=osb[:, c * 512:(c + 1) * 512],
                                 in_=ofm[c * 128:(c + 1) * 128, tt * 512:(tt + 1) * 512])
        for d in range(8):
            with p.grp() as g:
                g.sync.dma_start(out=xres[:], in_=xin[d * 128:(d + 1) * 128, tt * 512:(tt + 1) * 512])
                for c in range(8):
                    g.tensor.matmul(Pq[:], wsb[:, c * 1024 + d * 128:c * 1024 + (d + 1) * 128],
                                    osb[:, c * 512:(c + 1) * 512], start=(c == 0), stop=(c == 7))
            with p.grp() as g:
                g.scalar.activation(out=t1[:], in_=Pq[:], func=AF.Identity, bias=bs[:, d:d + 1])
            with p.grp() as g:
                g.vector.tensor_tensor(out=oout[:], in0=t1[:], in1=xres[:], op=ALU.add)
            with p.grp() as g:
                g.scalar.dma_start(out=xout[d * 128:(d + 1) * 128, tt * 512:(tt + 1) * 512],
                                   in_=oout[:])
    return p


def host_oproj(xfm, ofm_full, wT_bf, bias1024):
    p = build_oproj()
    bd = np.ascontiguousarray(np.asarray(bias1024, np.float32).reshape(8, 128).T, np.float32)
    in_maps = []
    for c in range(8):
        b, h = c // 2, c % 2
        in_maps.append({
            "xin": np.ascontiguousarray(xfm[b, :, h * 2048:(h + 1) * 2048]),
            "ofm": np.ascontiguousarray(ofm_full[b][:, h * 2048:(h + 1) * 2048]),
            "wT": wT_bf, "bd": bd,
        })
    res = _run(p, in_maps)
    out = np.empty_like(xfm)
    for c in range(8):
        b, h = c // 2, c % 2
        out[b, :, h * 2048:(h + 1) * 2048] = res[c]["xout"]
    return out


def host_attn_b(xfm, wb, inputs):
    p = build_attn_b()
    wqkvT = wb["l0ab_wqkv"]
    bqkv = np.asarray(inputs["L0_attn_b_b_qkv"], np.float32)
    in_maps = []
    for c in range(8):
        b, hh = c // 2, c % 2
        cols = slice(hh * 512, (hh + 1) * 512)
        in_maps.append({
            "xin": np.ascontiguousarray(xfm[b]),
            "wqs": np.ascontiguousarray(wqkvT[:, 0:1024][:, cols]),
            "wks": np.ascontiguousarray(wqkvT[:, 1024:2048][:, cols]),
            "wvs": np.ascontiguousarray(wqkvT[:, 2048:3072][:, cols]),
            "bqd": np.ascontiguousarray(bqkv[0:1024][hh * 512:(hh + 1) * 512].reshape(4, 128).T),
            "bkd": np.ascontiguousarray(bqkv[1024:2048][hh * 512:(hh + 1) * 512].reshape(4, 128).T),
            "bvd": np.ascontiguousarray(bqkv[2048:3072][hh * 512:(hh + 1) * 512].reshape(1, 512)),
        })
    res = _run(p, in_maps)
    import ml_dtypes
    ofm = np.empty((4, 1024, 4096), dtype=ml_dtypes.bfloat16)
    for c in range(8):
        b, hh = c // 2, c % 2
        ofm[b, hh * 512:(hh + 1) * 512, :] = res[c]["ohd"]
    return host_oproj(xfm, ofm, wb["l0ab_wout"], inputs["L0_attn_b_b_out"])


NFC = 33


def build_convb():
    p = Prog()
    T = 4096
    xin = p.din("xin", [1024, T], F32)
    wus = p.din("wus", [1024, 512], BF16)
    wvs = p.din("wvs", [1024, 512], BF16)
    rinw = p.din("rinw", [1, 128], F32)
    rhw = p.din("rhw", [3, 128, 32], F32)
    rlg = p.din("rlg", [128, 3], F32)
    rlb = p.din("rlb", [128, 3], F32)
    routw = p.din("routw", [128, 512], F32)
    dgrow = p.din("dgrow", [1, 512], F32)
    idxr = p.din("idxr", [1, T], F32)
    cfd = p.din("cfd", [NFC, 128, 32 * 128], BF16)
    sfd = p.din("sfd", [NFC, 128, 32 * 128], BF16)
    cid = p.din("cid", [32, 128, NFC * 128], BF16)
    sid = p.din("sid", [32, 128, NFC * 128], BF16)
    uod = p.dout("uod", [128, 32 * 512], BF16)
    uscr = p.dtmp("uscr", [128, 32 * 512], BF16)

    ones128 = p.sb("ones128", 128, 1, F32)
    ones1f = p.sb("ones1f", 1, 128, F32)
    rins = p.sb("rins", 1, 128, F32)
    rhs_ = p.sb("rhs", 128, 3 * 32, F32)
    rlgs = p.sb("rlgs", 128, 3, F32)
    rlbs = p.sb("rlbs", 128, 3, F32)
    routs = p.sb("routs", 128, 512, F32)
    dgs = p.sb("dgs", 1, 512, F32)
    idxs = p.sb("idxs", 1, T, F32)

    xn = p.sb("xn", 128, 8 * T, BF16, side="right")
    wusb = p.sb("wusb", 128, 8 * 512, BF16, side="right")
    wvsb = p.sb("wvsb", 128, 8 * 512, BF16, side="right")

    with p.grp() as g:
        for c in range(8):
            g.sync.dma_start(out=wusb[:, c * 512:(c + 1) * 512], in_=wus[c * 128:(c + 1) * 128, :])
            g.sync.dma_start(out=wvsb[:, c * 512:(c + 1) * 512], in_=wvs[c * 128:(c + 1) * 128, :])
        g.scalar.dma_start(out=rins[:], in_=rinw[:])
        for i in range(3):
            g.scalar.dma_start(out=rhs_[:, i * 32:(i + 1) * 32], in_=rhw[i, :, :])
        g.scalar.dma_start(out=rlgs[:], in_=rlg[:])
        g.scalar.dma_start(out=rlbs[:], in_=rlb[:])
        g.scalar.dma_start(out=routs[:], in_=routw[:])
        g.scalar.dma_start(out=dgs[:], in_=dgrow[:])
        g.scalar.dma_start(out=idxs[:], in_=idxr[:])
        g.vector.memset(ones128[:], 1.0)
        g.gpsimd.memset(ones1f[:], 1.0)

    P0 = p.psum("P0", 128, 512)
    P1 = p.psum("P1", 128, 512)
    P2 = p.psum("P2", 128, 512)
    P3 = p.psum("P3", 128, 512)

    L = ln_alloc(p)
    for tt in range(8):
        emit_ln_tile(p, L, xin, tt, xn, T, 0, P0[:], P1[:])
    ln_free(p, L)

    vtm = p.sb("vtm", 128, 32 * 512, BF16)
    tsb = p.sb("tsb", 128, 32 * 512, BF16)
    ustg = p.sb("ustg", 128, 512, BF16)

    def xnc(c, off, n=512):
        return xn[:, c * T + off:c * T + off + n]

    for kt in range(32):
        with p.grp() as g:
            for c in range(8):
                g.tensor.matmul(P0[:], xnc(c, kt * 128, 128), wusb[:, c * 512:(c + 1) * 512],
                                start=(c == 0), stop=(c == 7))
            for c in range(8):
                g.tensor.matmul(P1[:], xnc(c, kt * 128, 128), wvsb[:, c * 512:(c + 1) * 512],
                                start=(c == 0), stop=(c == 7))
        with p.grp() as g:
            g.scalar.activation(out=ustg[:], in_=P0[:], func=AF.Silu)
        with p.grp() as g:
            g.scalar.activation(out=vtm[:, kt * 512:(kt + 1) * 512], in_=P1[:], func=AF.Silu)
            g.scalar.dma_start(out=uscr[:, kt * 512:(kt + 1) * 512], in_=ustg[:])
    for t_ in (wvsb, wusb, xn):
        p.free(t_)

    hfm = p.sb("hfm", 128, T, F32)
    st1 = p.sb("st1", 1, 1024, F32)
    sqb = p.sb("sqb", 128, 512, F32)
    bcs2 = p.sb("bcs2", 128, 1024, F32)
    lgrow = p.sb("lgrow", 1, 512, F32)
    decf = p.sb("decf", 128, 512, F32)
    hnb = p.sb("hnb", 128, 512, BF16)
    wib = p.sb("wib", 128, 3 * 32, BF16)

    with p.grp() as g:
        g.vector.tensor_copy(wib[:], rhs_[:])
    for tt in range(8):
        with p.grp() as g:
            g.tensor.matmul(P0[:], rins[:], idxs[0:1, tt * 512:(tt + 1) * 512], start=True, stop=True)
        with p.grp() as g:
            g.scalar.activation(out=hfm[:, tt * 512:(tt + 1) * 512], in_=P0[:], func=AF.Silu)
    for i in range(3):
        for tt in range(8):
            hsl = hfm[:, tt * 512:(tt + 1) * 512]
            with p.grp() as g:
                g.scalar.activation(out=sqb[:], in_=hsl, func=AF.Square)
            with p.grp() as g:
                g.tensor.matmul(P0[0:1, :], ones128[:], hsl, start=True, stop=True)
                g.tensor.matmul(P1[0:1, :], ones128[:], sqb[:], start=True, stop=True)
            with p.grp() as g:
                g.scalar.mul(out=st1[0:1, 0:512], in_=P0[0:1, :], mul=1.0 / 32.0)
            with p.grp() as g:
                g.scalar.mul(out=st1[0:1, 512:1024], in_=P1[0:1, :], mul=1.0 / 32.0)
                g.vector.tensor_tensor(out=sqb[0:1, 0:512], in0=st1[0:1, 0:512],
                                       in1=st1[0:1, 0:512], op=ALU.mult)
            with p.grp() as g:
                g.vector.tensor_tensor(out=st1[0:1, 512:1024], in0=st1[0:1, 512:1024],
                                       in1=sqb[0:1, 0:512], op=ALU.subtract)
            with p.grp() as g:
                g.vector.tensor_scalar(out=st1[0:1, 512:1024], in0=st1[0:1, 512:1024],
                                       scalar1=LN_EPS, scalar2=None, op0=ALU.add, op1=ALU.bypass)
            with p.grp() as g:
                g.scalar.activation(out=st1[0:1, 512:1024], in_=st1[0:1, 512:1024], func=AF.Ln)
            with p.grp() as g:
                g.scalar.activation(out=st1[0:1, 512:1024], in_=st1[0:1, 512:1024],
                                    func=AF.Exp, scale=-0.5)
            with p.grp() as g:
                g.tensor.matmul(P0[:], ones1f[:], st1[0:1, 0:512], start=True, stop=True)
                g.tensor.matmul(P1[:], ones1f[:], st1[0:1, 512:1024], start=True, stop=True)
            with p.grp() as g:
                g.scalar.copy(out=bcs2[:, 0:512], in_=P0[:])
                g.vector.tensor_copy(bcs2[:, 512:1024], P1[:])
            with p.grp() as g:
                g.vector.tensor_tensor(out=sqb[:], in0=hsl, in1=bcs2[:, 0:512], op=ALU.subtract)
            with p.grp() as g:
                g.vector.tensor_tensor(out=sqb[:], in0=sqb[:], in1=bcs2[:, 512:1024], op=ALU.mult)
            with p.grp() as g:
                g.vector.tensor_scalar(out=sqb[:], in0=sqb[:], scalar1=rlgs[:, i:i + 1],
                                       scalar2=None, op0=ALU.mult, op1=ALU.bypass)
            with p.grp() as g:
                g.vector.tensor_scalar(out=sqb[:], in0=sqb[:], scalar1=rlbs[:, i:i + 1],
                                       scalar2=None, op0=ALU.add, op1=ALU.bypass)
            with p.grp() as g:
                g.vector.tensor_copy(hnb[:], sqb[:])
            with p.grp() as g:
                g.tensor.matmul(P2[0:32, :], wib[:, i * 32:(i + 1) * 32], hnb[:], start=True, stop=True)
            with p.grp() as g:
                g.scalar.activation(out=hfm[0:32, tt * 512:(tt + 1) * 512], in_=P2[0:32, :],
                                    func=AF.Silu)

    with p.grp() as g:
        g.scalar.activation(out=lgrow[:], in_=dgs[:], func=AF.Sigmoid)
    with p.grp() as g:
        g.vector.tensor_scalar(out=lgrow[:], in0=lgrow[:], scalar1=0.01, scalar2=0.99,
                               op0=ALU.mult, op1=ALU.add)
    with p.grp() as g:
        g.scalar.activation(out=lgrow[:], in_=lgrow[:], func=AF.Ln)
    for kt in range(32):
        with p.grp() as g:
            g.tensor.matmul(P0[:], hfm[:, kt * 128:(kt + 1) * 128], routs[:], start=True, stop=True)
            g.tensor.matmul(P1[:], idxs[0:1, kt * 128:(kt + 1) * 128], lgrow[:], start=True, stop=True)
        with p.grp() as g:
            g.scalar.activation(out=decf[:], in_=P1[:], func=AF.Exp)
        with p.grp() as g:
            g.vector.tensor_tensor(out=tsb[:, kt * 512:(kt + 1) * 512], in0=P0[:], in1=decf[:],
                                   op=ALU.mult)
    for t_ in (wib, hnb, decf, lgrow, bcs2, sqb, st1, hfm):
        p.free(t_)

    orsb = p.sb("orsb", 128, NFC * 512, BF16)
    oisb = p.sb("oisb", 128, NFC * 512, BF16)
    cblk = p.sb("cblk", 128, 32 * 128, BF16)
    sblk = p.sb("sblk", 128, 32 * 128, BF16)
    vrs = p.sb("vrs", 128, 512, F32)
    vis = p.sb("vis", 128, 512, F32)
    tt1 = p.sb("tt1", 128, 512, F32)
    tt2 = p.sb("tt2", 128, 512, F32)
    for fc in range(NFC):
        with p.grp() as g:
            g.sync.dma_start(out=cblk[:], in_=cfd[fc, :, :])
            g.scalar.dma_start(out=sblk[:], in_=sfd[fc, :, :])
        with p.grp() as g:
            for kc in range(32):
                g.tensor.matmul(P0[:], cblk[:, kc * 128:(kc + 1) * 128],
                                vtm[:, kc * 512:(kc + 1) * 512], start=(kc == 0), stop=(kc == 31))
            for kc in range(32):
                g.tensor.matmul(P1[:], sblk[:, kc * 128:(kc + 1) * 128],
                                vtm[:, kc * 512:(kc + 1) * 512], start=(kc == 0), stop=(kc == 31))
        with p.grp() as g:
            g.scalar.copy(out=vrs[:], in_=P0[:])
            g.vector.tensor_copy(vis[:], P1[:])
        with p.grp() as g:
            for kc in range(32):
                g.tensor.matmul(P2[:], cblk[:, kc * 128:(kc + 1) * 128],
                                tsb[:, kc * 512:(kc + 1) * 512], start=(kc == 0), stop=(kc == 31))
            for kc in range(32):
                g.tensor.matmul(P3[:], sblk[:, kc * 128:(kc + 1) * 128],
                                tsb[:, kc * 512:(kc + 1) * 512], start=(kc == 0), stop=(kc == 31))
        with p.grp() as g:
            g.vector.tensor_tensor(out=tt1[:], in0=P2[:], in1=vrs[:], op=ALU.mult)
        with p.grp() as g:
            g.vector.tensor_tensor(out=tt2[:], in0=P3[:], in1=vis[:], op=ALU.mult)
        with p.grp() as g:
            g.vector.tensor_tensor(out=orsb[:, fc * 512:(fc + 1) * 512], in0=tt1[:], in1=tt2[:],
                                   op=ALU.subtract)
        with p.grp() as g:
            g.vector.tensor_tensor(out=tt1[:], in0=P2[:], in1=vis[:], op=ALU.mult)
        with p.grp() as g:
            g.vector.tensor_tensor(out=tt2[:], in0=P3[:], in1=vrs[:], op=ALU.mult)
        with p.grp() as g:
            g.vector.tensor_tensor(out=oisb[:, fc * 512:(fc + 1) * 512], in0=tt1[:], in1=tt2[:],
                                   op=ALU.add)
    for t_ in (tt2, tt1, vis, vrs, sblk, cblk):
        p.free(t_)

    ciblk = p.sb("ciblk", 128, NFC * 128, BF16)
    siblk = p.sb("siblk", 128, NFC * 128, BF16)
    osb = p.sb("osb", 128, 512, F32)
    uob = p.sb("uob", 128, 512, BF16)
    for sc in range(32):
        with p.grp() as g:
            g.sync.dma_start(out=ciblk[:], in_=cid[sc, :, :])
            g.scalar.dma_start(out=siblk[:], in_=sid[sc, :, :])
            g.scalar.dma_start(out=ustg[:], in_=uscr[:, sc * 512:(sc + 1) * 512])
        with p.grp() as g:
            for fc in range(NFC):
                g.tensor.matmul(P0[:], ciblk[:, fc * 128:(fc + 1) * 128],
                                orsb[:, fc * 512:(fc + 1) * 512], start=(fc == 0), stop=False)
            for fc in range(NFC):
                g.tensor.matmul(P0[:], siblk[:, fc * 128:(fc + 1) * 128],
                                oisb[:, fc * 512:(fc + 1) * 512], start=False, stop=(fc == NFC - 1))
        with p.grp() as g:
            g.scalar.copy(out=osb[:], in_=P0[:])
        with p.grp() as g:
            g.vector.tensor_tensor(out=uob[:], in0=osb[:], in1=ustg[:], op=ALU.mult)
        with p.grp() as g:
            g.scalar.dma_start(out=uod[:, sc * 512:(sc + 1) * 512], in_=uob[:])
    return p


_DFT_CACHE = None


def _dft_blocks():
    global _DFT_CACHE
    if _DFT_CACHE is None:
        L = 8192.0
        s = np.arange(4096)
        f = np.arange(4224)
        ang = (2.0 * np.pi / L) * np.outer(s, f)
        Cf = np.cos(ang)
        Sf = -np.sin(ang)
        Cf[:, 4097:] = 0.0
        Sf[:, 4097:] = 0.0
        w = np.full(4224, 2.0)
        w[0] = 1.0
        w[4096] = 1.0
        w[4097:] = 0.0
        Ci = (w[:, None] / L) * np.cos(ang.T)
        Si = -(w[:, None] / L) * np.sin(ang.T)
        cfd = _bf16(Cf.reshape(32, 128, 33, 128).transpose(2, 1, 0, 3).reshape(33, 128, 32 * 128))
        sfd = _bf16(Sf.reshape(32, 128, 33, 128).transpose(2, 1, 0, 3).reshape(33, 128, 32 * 128))
        cid = _bf16(Ci.reshape(33, 128, 32, 128).transpose(2, 1, 0, 3).reshape(32, 128, 33 * 128))
        sid = _bf16(Si.reshape(33, 128, 32, 128).transpose(2, 1, 0, 3).reshape(32, 128, 33 * 128))
        _DFT_CACHE = (cfd, sfd, cid, sid)
    return _DFT_CACHE


def host_convb(xfm, wb, inputs):
    import ml_dtypes
    p = build_convb()
    cfd, sfd, cid, sid = _dft_blocks()
    rin = np.zeros((1, 128), np.float32); rin[0, :32] = inputs["L1_attn_b_rpe_in_w"][0]
    rhw = np.zeros((3, 128, 32), np.float32); rhw[:, :32, :] = inputs["L1_attn_b_rpe_hid_w"]
    rlg = np.zeros((128, 3), np.float32); rlg[:32, :] = inputs["L1_attn_b_rpe_ln_g"].T
    rlb = np.zeros((128, 3), np.float32); rlb[:32, :] = inputs["L1_attn_b_rpe_ln_b"].T
    routp = np.zeros((128, 1024), np.float32); routp[:32, :] = inputs["L1_attn_b_rpe_out_w"]
    idxr = np.arange(4096, dtype=np.float32).reshape(1, 4096)
    wuv = wb["l1ab_wuv"]
    dg = np.asarray(inputs["L1_attn_b_decay_gamma"], np.float32)
    in_maps = []
    for c in range(8):
        b, ch = c // 2, c % 2
        in_maps.append({
            "xin": np.ascontiguousarray(xfm[b]),
            "wus": np.ascontiguousarray(wuv[:, ch * 512:(ch + 1) * 512]),
            "wvs": np.ascontiguousarray(wuv[:, 1024 + ch * 512:1024 + (ch + 1) * 512]),
            "rinw": rin, "rhw": rhw, "rlg": rlg, "rlb": rlb,
            "routw": np.ascontiguousarray(routp[:, ch * 512:(ch + 1) * 512]),
            "dgrow": np.ascontiguousarray(dg[ch * 512:(ch + 1) * 512].reshape(1, 512)),
            "idxr": idxr,
            "cfd": cfd, "sfd": sfd, "cid": cid, "sid": sid,
        })
    res = _run(p, in_maps)
    ofm = np.empty((4, 1024, 4096), dtype=ml_dtypes.bfloat16)
    for c in range(8):
        b, ch = c // 2, c % 2
        uo = res[c]["uod"].reshape(128, 32, 512).transpose(2, 1, 0).reshape(512, 4096)
        ofm[b, ch * 512:(ch + 1) * 512, :] = uo
    return host_oproj(xfm, ofm, wb["l1ab_wo"], np.zeros(1024, np.float32))


def kernel(**inputs):
    wb = host_launch0(inputs)
    x = inputs["L0_attn_a_x"]
    xfm = np.ascontiguousarray(x.transpose(0, 2, 1))
    xfm = host_attn_a(xfm, wb)
    xfm = host_moe(xfm, wb, inputs)
    xfm = host_attn_b(xfm, wb, inputs)
    xfm = host_moe(xfm, wb, inputs)
    xfm = host_attn2(xfm, wb, inputs)
    xfm = host_ffn2(xfm, wb, inputs)
    xfm = host_convb(xfm, wb, inputs)
    xfm = host_ffn2(xfm, wb, inputs)
    return np.ascontiguousarray(xfm.transpose(0, 2, 1))


# version 02
# speedup vs baseline: 1.0932x; 1.0932x over previous
import contextlib
import math

import numpy as np
import concourse.bass as bass
import concourse.mybir as mybir
from concourse.bass_utils import run_bass_kernel_spmd

F32 = mybir.dt.float32
BF16 = mybir.dt.bfloat16
AF = mybir.ActivationFunctionType
ALU = mybir.AluOpType
AX = mybir.AxisListType

DMA_OPS = ("dma_start", "dma_start_transpose")
ENGINES = ("sync", "tensor", "vector", "scalar", "gpsimd")

RINT_C = 3.0 * (2.0 ** 22)


class _GrpEng:
    def __init__(self, group, name):
        self._group = group
        self._name = name

    def __getattr__(self, meth):
        def call(*a, **k):
            self._group.append((self._name, meth, a, k))
        return call


class _Grp:
    def __init__(self, items):
        for e in ENGINES:
            setattr(self, e, _GrpEng(items, e))


class Prog:
    def __init__(self):
        self.nc = bass.Bass("TRN2", target_bir_lowering=False)
        self.stack = contextlib.ExitStack()
        self.groups = []
        self._cms = {}

    def _enter(self, cm):
        h = cm.__enter__()
        self._cms[id(h)] = cm
        return h

    def free(self, h):
        cm = self._cms.pop(id(h))
        cm.__exit__(None, None, None)

    def sb(self, name, p, f, dt, side=None):
        if side is None:
            return self._enter(self.nc.sbuf_tensor(name, [p, f], dt))
        return self._enter(self.nc.sbuf_tensor(name, [p, f], dt, side=side))

    def psum(self, name, p, f, dt=F32):
        return self._enter(self.nc.psum_tensor(name, [p, f], dt))

    def din(self, name, shape, dt=F32):
        return self.nc.dram_tensor(name, list(shape), dt, kind="ExternalInput").ap()

    def dout(self, name, shape, dt=F32):
        return self.nc.dram_tensor(name, list(shape), dt, kind="ExternalOutput").ap()

    def dtmp(self, name, shape, dt=F32):
        return self.nc.dram_tensor(name, list(shape), dt, kind="Internal").ap()

    @contextlib.contextmanager
    def grp(self):
        items = []
        self.groups.append(items)
        yield _Grp(items)

    def finalize(self):
        nc = self.nc
        per = {e: [] for e in ENGINES}
        base = 0
        sem = self.stack.enter_context(nc.semaphore("gsem"))
        for group in self.groups:
            if not group:
                continue
            for (e, m, _, _) in group:
                assert not (e == "gpsimd" and m in DMA_OPS), "SWDGE (gpsimd) DMAs break the sim's semaphore rule"
            ginc = sum(16 if m in DMA_OPS else 1 for (_, m, _, _) in group)
            seen = set()
            for (e, m, a, k) in group:
                if e not in seen:
                    seen.add(e)
                    if base > 0:
                        per[e].append(("w", base))
                per[e].append(("op", m, a, k, 16 if m in DMA_OPS else 1))
            base += ginc
        total = base

        with nc.Block() as block:
            def mk(ename):
                def emit(eng):
                    for item in per[ename]:
                        if item[0] == "w":
                            eng.wait_ge(sem, item[1])
                        else:
                            _, m, a, k, inc = item
                            try:
                                inst = getattr(eng, m)(*a, **k)
                            except Exception:
                                import sys
                                print("FAILED OP:", ename, m,
                                      [getattr(x, "tensor", None) and x.tensor.name for x in a
                                       if hasattr(x, "tensor")],
                                      {kk: (getattr(vv, "tensor", None) and vv.tensor.name)
                                       for kk, vv in k.items() if hasattr(vv, "tensor")},
                                      file=sys.stderr)
                                raise
                            inst.then_inc(sem, inc)
                    if ename == "sync":
                        eng.wait_ge(sem, total)
                return emit

            block.sync(mk("sync"))
            block.tensor(mk("tensor"))
            block.vector(mk("vector"))
            block.scalar(mk("scalar"))
            block.gpsimd(mk("gpsimd"))
        for cm in reversed(list(self._cms.values())):
            cm.__exit__(None, None, None)
        self._cms.clear()
        self.stack.close()
        return nc


def _run(prog, in_maps, n_cores=8):
    nc = prog.finalize()
    res = run_bass_kernel_spmd(nc, in_maps, core_ids=list(range(n_cores)))
    return res.results


B, S, D = 4, 4096, 1024
NTOK = B * S
DC = D // 128
TT = 512
LN_EPS = 1e-5


def _bf16(x):
    import ml_dtypes
    return np.ascontiguousarray(x).astype(ml_dtypes.bfloat16)


CAST_FREE = 8192


def build_launch0(n_per_core, w1_cols, w2_cols):
    p = Prog()
    nc = p.nc
    wflat = p.din("wflat", [n_per_core], F32)
    wbf = p.dout("wbf", [n_per_core], BF16)
    w1t = p.din("w1t", [1024, 4096], F32)
    w2t = p.din("w2t", [4096, 1024], F32)
    w1q = p.dout("w1q", [128, 4096], BF16)
    w2q = p.dout("w2q", [512, 1024], BF16)
    wscale = p.dout("wscale", [1, 2], F32)

    ntile = n_per_core // (128 * CAST_FREE)
    wf2 = wflat.rearrange("(t p f) -> t p f", p=128, f=CAST_FREE)
    wb2 = wbf.rearrange("(t p f) -> t p f", p=128, f=CAST_FREE)
    del w1_cols, w2_cols

    a = p.sb("a", 128, CAST_FREE, F32)
    b_ = p.sb("b", 128, CAST_FREE, F32)
    abf = p.sb("abf", 128, CAST_FREE, BF16)
    bbf = p.sb("bbf", 128, CAST_FREE, BF16)

    bufs = [(a, abf), (b_, bbf)]
    with p.grp() as g:
        g.sync.dma_start(out=a[:], in_=wf2[0, :, :])
    for t in range(ntile):
        src, dst = bufs[t % 2]
        with p.grp() as g:
            g.vector.tensor_copy(dst[:], src[:])
            if t + 1 < ntile:
                nsrc = bufs[(t + 1) % 2][0]
                g.sync.dma_start(out=nsrc[:], in_=wf2[t + 1, :, :])
        with p.grp() as g:
            g.scalar.dma_start(out=wb2[t, :, :], in_=dst[:])

    wtile = p.sb("wtile", 128, 4096, F32)
    wtile2 = p.sb("wtile2", 128, 4096, F32)
    acc1 = p.sb("acc1", 128, 1, F32)
    acc2 = p.sb("acc2", 128, 1, F32)
    rowsum = p.sb("rowsum", 128, 1, F32)
    absd = p.sb("absd", 128, 4096, F32)
    ones = p.sb("ones", 128, 1, F32)
    scal = p.sb("scal", 1, 2, F32)
    sps = p.psum("sps", 1, 2, F32)
    bc = p.sb("bc", 128, 2, F32)
    bcp = p.psum("bcp", 128, 2, F32)
    ones1 = p.sb("ones1", 1, 128, F32)

    with p.grp() as g:
        g.vector.memset(acc1[:], 0.0)
        g.vector.memset(acc2[:], 0.0)
        g.gpsimd.memset(ones[:], 1.0)
    with p.grp() as g:
        g.gpsimd.memset(ones1[:], 1.0)
    for w_in, acc in ((w1t, acc1), (w2t, acc2)):
        nrt = w_in.shape[0] // 128
        for r in range(nrt):
            wt = (wtile, wtile2)[r % 2]
            with p.grp() as g:
                g.sync.dma_start(out=wt[:, 0:w_in.shape[1]], in_=w_in[r * 128:(r + 1) * 128, :])
            with p.grp() as g:
                g.scalar.activation(out=absd[:, 0:w_in.shape[1]], in_=wt[:, 0:w_in.shape[1]],
                                    func=AF.Abs, accum_out=rowsum[:])
            with p.grp() as g:
                g.vector.tensor_tensor(out=acc[:], in0=acc[:], in1=rowsum[:], op=ALU.add)
    accb = p.sb("accb", 128, 2, F32)
    with p.grp() as g:
        g.vector.tensor_copy(accb[:, 0:1], acc1[:])
        g.scalar.copy(out=accb[:, 1:2], in_=acc2[:])
    with p.grp() as g:
        g.tensor.matmul(sps[:], ones[:], accb[:], start=True, stop=True)
    nel = float(1024 * 4096)
    with p.grp() as g:
        g.scalar.mul(out=scal[:], in_=sps[:], mul=1.0 / nel)
    with p.grp() as g:
        g.vector.tensor_scalar(out=scal[:], in0=scal[:], scalar1=1e-5, scalar2=None,
                               op0=ALU.max, op1=ALU.bypass)
    with p.grp() as g:
        g.scalar.dma_start(out=wscale[:], in_=scal[:])
    with p.grp() as g:
        g.vector.reciprocal(out=scal[:], in_=scal[:])
    with p.grp() as g:
        g.tensor.matmul(bcp[:], ones1[:], scal[:], start=True, stop=True)
    with p.grp() as g:
        g.scalar.copy(out=bc[:], in_=bcp[:])

    w1s = p.din("w1s", [128, 4096], F32)
    w2s = p.din("w2s", [512, 1024], F32)
    q = p.sb("q", 128, 4096, F32)
    qb = p.sb("qb", 128, 4096, BF16)
    for (w_in, w_out, col, nrt, ncol) in ((w1s, w1q, 0, 1, 4096), (w2s, w2q, 1, 4, 1024)):
        for r in range(nrt):
            with p.grp() as g:
                g.sync.dma_start(out=q[:, 0:ncol], in_=w_in[r * 128:(r + 1) * 128, :])
            with p.grp() as g:
                g.vector.tensor_scalar(out=q[:, 0:ncol], in0=q[:, 0:ncol],
                                       scalar1=bc[:, col:col + 1], scalar2=None,
                                       op0=ALU.mult, op1=ALU.bypass)
            with p.grp() as g:
                g.vector.tensor_scalar(out=q[:, 0:ncol], in0=q[:, 0:ncol],
                                       scalar1=RINT_C, scalar2=None, op0=ALU.add, op1=ALU.bypass)
            with p.grp() as g:
                g.vector.tensor_scalar(out=q[:, 0:ncol], in0=q[:, 0:ncol],
                                       scalar1=RINT_C, scalar2=None, op0=ALU.subtract, op1=ALU.bypass)
            with p.grp() as g:
                g.vector.tensor_scalar(out=qb[:, 0:ncol], in0=q[:, 0:ncol],
                                       scalar1=1.0, scalar2=-1.0, op0=ALU.min, op1=ALU.max)
            with p.grp() as g:
                g.scalar.dma_start(out=w_out[r * 128:(r + 1) * 128, :], in_=qb[:, 0:ncol])
    return p


def host_launch0(inputs):
    specs = []

    perm = np.concatenate([np.arange(0, 128, 2), np.arange(1, 128, 2)])
    wq = inputs["L0_attn_a_Wq"]
    wqT = np.concatenate([wq[h].T[:, perm] for h in range(2)], axis=1)
    specs.append(("l0aa_wq", wqT))
    specs.append(("l0aa_wk", inputs["L0_attn_a_Wk"].T[:, perm]))
    specs.append(("l0aa_wv", inputs["L0_attn_a_Wv"].T))
    specs.append(("l0aa_wo", inputs["L0_attn_a_Wo"].T))
    specs.append(("l0ab_wqkv", inputs["L0_attn_b_w_qkv"].T))
    specs.append(("l0ab_wout", inputs["L0_attn_b_w_out"].T))
    specs.append(("l0f_w1", inputs["L0_ffn_w1"]))
    specs.append(("l0f_w2", inputs["L0_ffn_w2"]))
    specs.append(("l1aa_wq", inputs["L1_attn_a_Wq"].reshape(1024, 1024)))
    specs.append(("l1aa_wk", inputs["L1_attn_a_Wk"].reshape(1024, 1024)))
    specs.append(("l1aa_wv", inputs["L1_attn_a_Wv"].reshape(1024, 1024)))
    specs.append(("l1aa_wo", inputs["L1_attn_a_Wo"].reshape(1024, 1024)))
    specs.append(("l1ab_wuv", inputs["L1_attn_b_W_uv"]))
    specs.append(("l1ab_wo", inputs["L1_attn_b_W_o"]))

    shapes = [(k, a.shape) for k, a in specs]
    flat = np.concatenate([np.ascontiguousarray(a, dtype=np.float32).reshape(-1) for _, a in specs])
    unit = 8 * 128 * CAST_FREE
    pad = (-len(flat)) % unit
    flat = np.concatenate([flat, np.zeros(pad, np.float32)])
    npc = len(flat) // 8

    w1t = np.ascontiguousarray(inputs["L1_ffn_W1"].T, dtype=np.float32)
    w2t = np.ascontiguousarray(inputs["L1_ffn_W2"].T, dtype=np.float32)

    p = build_launch0(npc, 4096, 1024)
    in_maps = []
    for c in range(8):
        in_maps.append({
            "wflat": flat[c * npc:(c + 1) * npc],
            "w1t": w1t, "w2t": w2t,
            "w1s": w1t[c * 128:(c + 1) * 128, :],
            "w2s": w2t[c * 512:(c + 1) * 512, :],
        })
    res = _run(p, in_maps)
    flat_bf = np.concatenate([r["wbf"] for r in res])
    out = {}
    off = 0
    for k, shp in shapes:
        n = int(np.prod(shp))
        out[k] = flat_bf[off:off + n].reshape(shp)
        off += n
    out["l1f_w1q"] = np.concatenate([r["w1q"] for r in res], axis=0)
    out["l1f_w2q"] = np.concatenate([r["w2q"] for r in res], axis=0)
    out["l1f_wscale"] = res[0]["wscale"].reshape(2)
    return out


def ln_alloc(p, pfx=""):
    L = {}
    L["ones128"] = p.sb(pfx + "Lones128", 128, 1, F32)
    L["ones1"] = p.sb(pfx + "Lones1", 1, 128, F32)
    L["stats"] = p.sb(pfx + "Lstats", 1, 1024, F32)
    L["varb"] = p.sb(pfx + "Lvarb", 1, 512, F32)
    L["m2b"] = p.sb(pfx + "Lm2b", 1, 512, F32)
    L["bcs"] = p.sb(pfx + "Lbcs", 128, 1024, F32)
    L["xf"] = p.sb(pfx + "Lxf", 128, 8 * 512, F32)
    L["xg"] = p.sb(pfx + "Lxg", 128, 8 * 512, F32)
    with p.grp() as g:
        g.vector.memset(L["ones128"][:], 1.0)
        g.gpsimd.memset(L["ones1"][:], 1.0)
    return L


def ln_free(p, L):
    for k in ("xg", "xf", "bcs", "m2b", "varb", "stats", "ones1", "ones128"):
        p.free(L[k])


def emit_ln_tile(p, L, xin, tt, dst_sb, dst_T, dst_off, PA, PB):
    xf, xg = L["xf"], L["xg"]
    stats, varb, m2b, bcs = L["stats"], L["varb"], L["m2b"], L["bcs"]

    def xfc(c):
        return xf[:, c * 512:(c + 1) * 512]

    def xgc(c):
        return xg[:, c * 512:(c + 1) * 512]

    with p.grp() as g:
        for c in range(8):
            g.sync.dma_start(out=xfc(c), in_=xin[c * 128:(c + 1) * 128, tt * 512:(tt + 1) * 512])
    with p.grp() as g:
        for c in range(8):
            g.scalar.activation(out=xgc(c), in_=xfc(c), func=AF.Square)
    with p.grp() as g:
        for c in range(8):
            g.tensor.matmul(PA[0:1, :], L["ones128"][:], xfc(c), start=(c == 0), stop=(c == 7))
        for c in range(8):
            g.tensor.matmul(PB[0:1, :], L["ones128"][:], xgc(c), start=(c == 0), stop=(c == 7))
    with p.grp() as g:
        g.scalar.mul(out=stats[0:1, 0:512], in_=PA[0:1, :], mul=1.0 / 1024.0)
    with p.grp() as g:
        g.scalar.mul(out=varb[:], in_=PB[0:1, :], mul=1.0 / 1024.0)
        g.vector.tensor_tensor(out=m2b[:], in0=stats[0:1, 0:512], in1=stats[0:1, 0:512], op=ALU.mult)
    with p.grp() as g:
        g.vector.tensor_tensor(out=varb[:], in0=varb[:], in1=m2b[:], op=ALU.subtract)
    with p.grp() as g:
        g.vector.tensor_scalar(out=varb[:], in0=varb[:], scalar1=LN_EPS, scalar2=None,
                               op0=ALU.add, op1=ALU.bypass)
    with p.grp() as g:
        g.scalar.activation(out=m2b[:], in_=varb[:], func=AF.Ln)
    with p.grp() as g:
        g.scalar.activation(out=stats[0:1, 512:1024], in_=m2b[:], func=AF.Exp, scale=-0.5)
    with p.grp() as g:
        g.tensor.matmul(PA[:], L["ones1"][:], stats[0:1, 0:512], start=True, stop=True)
        g.tensor.matmul(PB[:], L["ones1"][:], stats[0:1, 512:1024], start=True, stop=True)
    with p.grp() as g:
        g.scalar.copy(out=bcs[:, 0:512], in_=PA[:])
        g.vector.tensor_copy(bcs[:, 512:1024], PB[:])
    with p.grp() as g:
        for c in range(8):
            g.vector.tensor_tensor(out=xgc(c), in0=xfc(c), in1=bcs[:, 0:512], op=ALU.subtract)
    with p.grp() as g:
        for c in range(8):
            g.vector.tensor_tensor(out=xfc(c), in0=xgc(c), in1=bcs[:, 512:1024], op=ALU.mult)
    if dst_sb is None:
        return
    o = dst_off + tt * 512
    with p.grp() as g:
        for c in range(8):
            if c % 2 == 0:
                g.vector.tensor_copy(dst_sb[:, c * dst_T + o:c * dst_T + o + 512], xfc(c))
            else:
                g.scalar.copy(out=dst_sb[:, c * dst_T + o:c * dst_T + o + 512], in_=xfc(c))


def build_moe(tpc=2048):
    p = Prog()
    xin = p.din("xin", [1024, tpc], F32)
    gw = p.din("gw", [1024, 8], F32)
    w1 = p.din("w1", [8, 1024, 4096], BF16)
    w2 = p.din("w2", [8, 4096, 1024], BF16)
    b1d = p.din("b1d", [8, 128, 32], F32)
    b2d = p.din("b2d", [8, 1024], F32)
    ident = p.din("ident", [128, 128], F32)
    sel = p.din("sel", [8, 1024], F32)
    xout = p.dout("xout", [1024, tpc], F32)
    xnscr = p.dtmp("xnscr", [128, 8 * tpc], BF16)

    NT = tpc // 512

    ones128 = p.sb("ones128", 128, 1, F32)
    ones1 = p.sb("ones1", 1, 128, F32)
    onesrow = p.sb("onesrow", 1, 512, BF16)
    gwsb = p.sb("gwsb", 128, 64, F32)
    idsb = p.sb("idsb", 128, 128, F32)
    selsb = p.sb("selsb", 8, 1024, F32)
    wmT = p.sb("wmT", 8, tpc, F32)
    accum = p.sb("accum", 128, 8 * tpc, BF16)
    stats = p.sb("stats", 1, 1024, F32)
    varb = p.sb("varb", 1, 512, F32)
    m2b = p.sb("m2b", 1, 512, F32)
    bcs = p.sb("bcs", 128, 1024, F32)
    mx = p.sb("mx", 128, 1, F32)
    negmx = p.sb("negmx", 128, 1, F32)
    zrow = p.sb("zrow", 128, 1, F32)
    zrec = p.sb("zrec", 128, 1, F32)
    pr = p.sb("pr", 128, 8, F32)
    m8 = p.sb("m8", 128, 8, F32)
    wm = p.sb("wm", 128, 8, F32)

    xf = p.sb("xf", 128, 8 * 512, F32)
    xg = p.sb("xg", 128, 8 * 512, F32)
    xnb = p.sb("xnb", 128, 8 * 512, BF16)

    ps0 = p.psum("ps0", 1, 512)
    ps1 = p.psum("ps1", 1, 512)
    pbc = p.psum("pbc", 128, 1024)
    pglog = p.psum("pglog", 128, 8)
    pwmT = p.psum("pwmT", 8, 128)

    with p.grp() as g:
        g.vector.memset(ones128[:], 1.0)
        g.gpsimd.memset(ones1[:], 1.0)
        g.gpsimd.memset(onesrow[:], 1.0)
        for c in range(8):
            g.sync.dma_start(out=gwsb[:, 8 * c:8 * c + 8], in_=gw[c * 128:(c + 1) * 128, :])
        g.sync.dma_start(out=idsb[:], in_=ident[:])
        g.sync.dma_start(out=selsb[:], in_=sel[:])

    def xfc(c, j0=0, n=512):
        return xf[:, c * 512 + j0:c * 512 + j0 + n]

    for tt in range(NT):
        with p.grp() as g:
            for c in range(8):
                g.sync.dma_start(out=xfc(c), in_=xin[c * 128:(c + 1) * 128, tt * 512:(tt + 1) * 512])
        with p.grp() as g:
            for c in range(8):
                g.scalar.activation(out=xg[:, c * 512:(c + 1) * 512], in_=xfc(c), func=AF.Square)
        with p.grp() as g:
            for c in range(8):
                g.tensor.matmul(ps0[:], ones128[:], xfc(c), start=(c == 0), stop=(c == 7))
            for c in range(8):
                g.tensor.matmul(ps1[:], ones128[:], xg[:, c * 512:(c + 1) * 512],
                                start=(c == 0), stop=(c == 7))
        with p.grp() as g:
            g.scalar.mul(out=stats[0:1, 0:512], in_=ps0[:], mul=1.0 / 1024.0)
        with p.grp() as g:
            g.scalar.mul(out=varb[:], in_=ps1[:], mul=1.0 / 1024.0)
            g.vector.tensor_tensor(out=m2b[:], in0=stats[0:1, 0:512], in1=stats[0:1, 0:512], op=ALU.mult)
        with p.grp() as g:
            g.vector.tensor_tensor(out=varb[:], in0=varb[:], in1=m2b[:], op=ALU.subtract)
        with p.grp() as g:
            g.vector.tensor_scalar(out=varb[:], in0=varb[:], scalar1=LN_EPS, scalar2=None,
                                   op0=ALU.add, op1=ALU.bypass)
        with p.grp() as g:
            g.scalar.activation(out=m2b[:], in_=varb[:], func=AF.Ln)
        with p.grp() as g:
            g.scalar.activation(out=stats[0:1, 512:1024], in_=m2b[:], func=AF.Exp, scale=-0.5)
        with p.grp() as g:
            g.tensor.matmul(pbc[:, 0:512], ones1[:], stats[0:1, 0:512], start=True, stop=True)
            g.tensor.matmul(pbc[:, 512:1024], ones1[:], stats[0:1, 512:1024], start=True, stop=True)
        with p.grp() as g:
            g.scalar.copy(out=bcs[:], in_=pbc[:])
        with p.grp() as g:
            for c in range(8):
                g.vector.tensor_tensor(out=xg[:, c * 512:(c + 1) * 512], in0=xfc(c),
                                       in1=bcs[:, 0:512], op=ALU.subtract)
        with p.grp() as g:
            for c in range(8):
                g.vector.tensor_tensor(out=xfc(c), in0=xg[:, c * 512:(c + 1) * 512],
                                       in1=bcs[:, 512:1024], op=ALU.mult)
        with p.grp() as g:
            for c in range(8):
                eng = g.vector if c % 2 == 0 else g.scalar
                if c % 2 == 0:
                    g.vector.tensor_copy(xnb[:, c * 512:(c + 1) * 512], xfc(c))
                else:
                    g.scalar.copy(out=xnb[:, c * 512:(c + 1) * 512], in_=xfc(c))
        with p.grp() as g:
            for c in range(8):
                g.sync.dma_start(out=xnscr[:, c * tpc + tt * 512:c * tpc + (tt + 1) * 512],
                                 in_=xnb[:, c * 512:(c + 1) * 512])
        for j in range(4):
            with p.grp() as g:
                for c in range(8):
                    g.tensor.matmul(pglog[:], xfc(c, j * 128, 128), gwsb[:, 8 * c:8 * c + 8],
                                    start=(c == 0), stop=(c == 7))
            with p.grp() as g:
                g.vector.reduce_max(mx[:], pglog[:], axis=AX.X)
            with p.grp() as g:
                g.scalar.mul(out=negmx[:], in_=mx[:], mul=-1.0)
            with p.grp() as g:
                g.scalar.activation(out=pr[:], in_=pglog[:], func=AF.Exp, bias=negmx[:],
                                    accum_out=zrow[:])
            with p.grp() as g:
                g.vector.tensor_scalar(out=m8[:], in0=pglog[:], scalar1=mx[:], scalar2=None,
                                       op0=ALU.is_equal, op1=ALU.bypass)
            with p.grp() as g:
                g.vector.reciprocal(out=zrec[:], in_=zrow[:])
            with p.grp() as g:
                g.vector.tensor_scalar(out=wm[:], in0=m8[:], scalar1=zrec[:], scalar2=None,
                                       op0=ALU.mult, op1=ALU.bypass)
            with p.grp() as g:
                g.tensor.transpose(pwmT[:], wm[:], idsb[:])
            with p.grp() as g:
                g.scalar.copy(out=wmT[0:8, tt * 512 + j * 128:tt * 512 + (j + 1) * 128], in_=pwmT[:])

    for t_ in (xnb, xg, xf, wm, m8, pr, zrec, zrow, negmx, mx, bcs, m2b, varb, stats):
        p.free(t_)
    for t_ in (pwmT, pglog, pbc, ps1, ps0):
        p.free(t_)

    xnh = p.sb("xnh", 128, 8 * 1024, BF16)
    w1e = [p.sb(f"w1e{c}", 128, 4096, BF16) for c in range(8)]
    hsb = p.sb("hsb", 128, 32 * 1024, BF16)
    w2b = [p.sb(f"w2b{i}", 128, 1024, BF16) for i in range(2)]
    b1sb = p.sb("b1sb", 128, 32, F32)
    b2f = p.sb("b2f", 1, 1024, F32)
    b2sb = p.sb("b2sb", 1, 1024, BF16)
    wb2_ = [p.sb(f"wb{i}", 128, 512, F32) for i in range(2)]
    tb = p.sb("tb", 128, 512, F32)
    P = [p.psum(f"P{i}", 128, 512) for i in range(8)]

    nh = tpc // 1024
    for half in range(nh):
        with p.grp() as g:
            for c in range(8):
                g.sync.dma_start(out=xnh[:, c * 1024:(c + 1) * 1024],
                                 in_=xnscr[:, c * tpc + half * 1024:c * tpc + (half + 1) * 1024])
        for e in range(8):
            with p.grp() as g:
                for c in range(8):
                    g.sync.dma_start(out=w1e[c][:], in_=w1[e, c * 128:(c + 1) * 128, :])
                g.scalar.dma_start(out=b1sb[:], in_=b1d[e, :, :])
                g.scalar.dma_start(out=b2f[:], in_=b2d[e:e + 1, :])
            with p.grp() as g:
                g.vector.tensor_copy(b2sb[:], b2f[:])
            prev = None
            for r in range(32):
                for j in range(2):
                    idx = (2 * r + j) % 2
                    with p.grp() as g:
                        for c in range(8):
                            g.tensor.matmul(P[idx][:], w1e[c][:, r * 128:(r + 1) * 128],
                                            xnh[:, c * 1024 + j * 512:c * 1024 + (j + 1) * 512],
                                            start=(c == 0), stop=(c == 7))
                        if prev is not None:
                            pr_, pj, pidx = prev
                            g.scalar.activation(
                                out=hsb[:, pr_ * 1024 + pj * 512:pr_ * 1024 + (pj + 1) * 512],
                                in_=P[pidx][:], func=AF.Relu, bias=b1sb[:, pr_:pr_ + 1])
                    prev = (r, j, idx)
            with p.grp() as g:
                pr_, pj, pidx = prev
                g.scalar.activation(out=hsb[:, pr_ * 1024 + pj * 512:pr_ * 1024 + (pj + 1) * 512],
                                    in_=P[pidx][:], func=AF.Relu, bias=b1sb[:, pr_:pr_ + 1])
            for j in range(2):
                with p.grp() as g:
                    g.tensor.matmul(P[2][:], selsb[:, e * 128:(e + 1) * 128],
                                    wmT[0:8, half * 1024 + j * 512:half * 1024 + (j + 1) * 512],
                                    start=True, stop=True)
                with p.grp() as g:
                    g.scalar.copy(out=wb2_[j][:], in_=P[2][:])
            for j in range(2):
                with p.grp() as g:
                    g.sync.dma_start(out=w2b[0][:], in_=w2[e, 0:128, :])
                for r in range(32):
                    with p.grp() as g:
                        if r + 1 < 32:
                            g.sync.dma_start(out=w2b[(r + 1) % 2][:],
                                             in_=w2[e, (r + 1) * 128:(r + 2) * 128, :])
                        for d in range(8):
                            g.tensor.matmul(P[d][:], w2b[r % 2][:, d * 128:(d + 1) * 128],
                                            hsb[:, r * 1024 + j * 512:r * 1024 + (j + 1) * 512],
                                            start=(r == 0), stop=False)
                with p.grp() as g:
                    for d in range(8):
                        g.tensor.matmul(P[d][:], b2sb[0:1, d * 128:(d + 1) * 128], onesrow[:],
                                        start=False, stop=True)
                for d in range(8):
                    asl = accum[:, d * tpc + half * 1024 + j * 512:d * tpc + half * 1024 + (j + 1) * 512]
                    if e == 0:
                        with p.grp() as g:
                            g.vector.tensor_tensor(out=asl, in0=P[d][:], in1=wb2_[j][:], op=ALU.mult)
                    else:
                        with p.grp() as g:
                            g.vector.tensor_tensor(out=tb[:], in0=P[d][:], in1=wb2_[j][:], op=ALU.mult)
                        with p.grp() as g:
                            g.vector.tensor_tensor(out=asl, in0=asl, in1=tb[:], op=ALU.add)

    for t_ in [tb] + wb2_[::-1] + [b2sb, b2f, b1sb] + w2b[::-1] + [hsb] + w1e[::-1] + [xnh]:
        p.free(t_)

    xep = p.sb("xep", 128, 8 * 512, F32)
    oep = p.sb("oep", 128, 8 * 512, F32)
    for tt in range(NT):
        with p.grp() as g:
            for c in range(8):
                g.sync.dma_start(out=xep[:, c * 512:(c + 1) * 512],
                                 in_=xin[c * 128:(c + 1) * 128, tt * 512:(tt + 1) * 512])
        with p.grp() as g:
            for c in range(8):
                g.vector.tensor_tensor(out=oep[:, c * 512:(c + 1) * 512],
                                       in0=xep[:, c * 512:(c + 1) * 512],
                                       in1=accum[:, c * tpc + tt * 512:c * tpc + (tt + 1) * 512],
                                       op=ALU.add)
        with p.grp() as g:
            for c in range(8):
                g.scalar.dma_start(out=xout[c * 128:(c + 1) * 128, tt * 512:(tt + 1) * 512],
                                   in_=oep[:, c * 512:(c + 1) * 512])
    return p


_SEL8 = None


def _moe_consts():
    global _SEL8
    if _SEL8 is None:
        s = np.zeros((8, 1024), np.float32)
        for e in range(8):
            s[e, e * 128:(e + 1) * 128] = 1.0
        _SEL8 = s
    return _SEL8


def host_moe(xfm, wb, inputs):
    b1 = inputs["L0_ffn_b1"]
    b1d = np.ascontiguousarray(np.stack([b1[e].reshape(32, 128).T for e in range(8)]), np.float32)
    p = build_moe()
    ident = np.eye(128, dtype=np.float32)
    in_maps = []
    for c in range(8):
        b, h = c // 2, c % 2
        in_maps.append({
            "xin": np.ascontiguousarray(xfm[b, :, h * 2048:(h + 1) * 2048]),
            "gw": np.ascontiguousarray(inputs["L0_ffn_gate_w"].T, np.float32),
            "w1": wb["l0f_w1"], "w2": wb["l0f_w2"],
            "b1d": b1d, "b2d": np.ascontiguousarray(inputs["L0_ffn_b2"], np.float32),
            "ident": ident, "sel": _moe_consts(),
        })
    res = _run(p, in_maps)
    out = np.empty_like(xfm)
    for c in range(8):
        b, h = c // 2, c % 2
        out[b, :, h * 2048:(h + 1) * 2048] = res[c]["xout"]
    return out


def build_attn_a():
    p = Prog()
    TQ, TK = 2048, 4096
    xin = p.din("xin", [1024, TK], F32)
    wq = p.din("wq", [1024, 256], BF16)
    wk = p.din("wk", [1024, 128], BF16)
    wv = p.din("wv", [1024, 128], BF16)
    wo = p.din("wo", [256, 1024], BF16)
    cosq = p.din("cosq", [128, TQ], BF16)
    sinq = p.din("sinq", [128, TQ], BF16)
    cosk = p.din("cosk", [128, TK], BF16)
    sink = p.din("sink", [128, TK], BF16)
    kglob = p.din("kglob", [128, 32], F32)
    qbase = p.din("qbase", [1, 4], F32)
    qiota = p.din("qiota", [1, 512], F32)
    xout = p.dout("xout", [1024, TQ], F32)

    xn = p.sb("xn", 128, 8 * TK, BF16)
    krot = p.sb("krot", 128, TK, BF16)
    vtm = p.sb("vtm", 128, 32 * 128, BF16)
    wqsb = p.sb("wqsb", 128, 8 * 256, BF16)
    wksb = p.sb("wksb", 128, 8 * 128, BF16)
    wvsb = p.sb("wvsb", 128, 8 * 128, BF16)
    wosb = p.sb("wosb", 128, 2 * 1024, BF16)
    cqs = p.sb("cqs", 128, TQ, BF16)
    sqs = p.sb("sqs", 128, TQ, BF16)
    cks = p.sb("cks", 128, TK, BF16)
    sks = p.sb("sks", 128, TK, BF16)
    kgs = p.sb("kgs", 128, 32, F32)
    qbs = p.sb("qbs", 1, 4, F32)
    qis = p.sb("qis", 1, 512, F32)
    ones128b = p.sb("ones128b", 128, 1, BF16)
    ones128f = p.sb("ones128f", 128, 1, F32)
    ones1f = p.sb("ones1f", 1, 128, F32)
    onesr1b = p.sb("onesr1b", 1, 128, BF16)
    kmax2 = p.sb("kmax2", 1, 1, F32)

    with p.grp() as g:
        for c in range(8):
            g.sync.dma_start(out=wqsb[:, c * 256:(c + 1) * 256], in_=wq[c * 128:(c + 1) * 128, :])
            g.sync.dma_start(out=wksb[:, c * 128:(c + 1) * 128], in_=wk[c * 128:(c + 1) * 128, :])
            g.sync.dma_start(out=wvsb[:, c * 128:(c + 1) * 128], in_=wv[c * 128:(c + 1) * 128, :])
        for h in range(2):
            g.sync.dma_start(out=wosb[:, h * 1024:(h + 1) * 1024], in_=wo[h * 128:(h + 1) * 128, :])
        g.scalar.dma_start(out=cqs[:], in_=cosq[:])
        g.scalar.dma_start(out=sqs[:], in_=sinq[:])
        g.scalar.dma_start(out=cks[:], in_=cosk[:])
        g.scalar.dma_start(out=sks[:], in_=sink[:])
        g.scalar.dma_start(out=kgs[:], in_=kglob[:])
        g.scalar.dma_start(out=qbs[:], in_=qbase[:])
        g.scalar.dma_start(out=qis[:], in_=qiota[:])
        g.vector.memset(ones128b[:], 1.0)
        g.vector.memset(ones128f[:], 1.0)
        g.gpsimd.memset(ones1f[:], 1.0)
        g.gpsimd.memset(onesr1b[:], 1.0)
        g.gpsimd.memset(kmax2[:], 0.0)

    Pq = p.psum("Pq", 128, 512)
    Ps0 = p.psum("Ps0", 128, 512)
    Ps1 = p.psum("Ps1", 128, 512)
    Pav = p.psum("Pav", 128, 512)
    Pb = p.psum("Pb", 128, 512)
    Pden = p.psum("Pden", 1, 512)

    L = ln_alloc(p)
    for tt in range(8):
        emit_ln_tile(p, L, xin, tt, xn, TK, 0, Pq[:], Ps0[:])
    ln_free(p, L)

    kfp = p.sb("kfp", 128, 512, F32)
    s64a = p.sb("s64a", 64, 512, F32)
    s64a2 = p.sb("s64a2", 64, 512, F32)
    s64b = p.sb("s64b", 64, 512, F32)
    sqbuf = p.sb("sqbuf", 128, 512, F32)
    st1 = p.sb("st1", 1, 512, F32)
    tmax = p.sb("tmax", 1, 1, F32)
    qrot = p.sb("qrot", 128, 512, BF16)
    qgrow = p.sb("qgrow", 1, 512, F32)
    qgbc = p.sb("qgbc", 128, 512, F32)
    maskb = [p.sb(f"maskb{i}", 128, 512, BF16) for i in range(2)]
    Eb = [p.sb(f"Eb{i}", 128, 512, BF16) for i in range(2)]
    Em = [p.sb(f"Em{i}", 128, 512, BF16) for i in range(2)]
    mrow = p.sb("mrow", 1, 512, F32)
    negmb = p.sb("negmb", 1, 512, BF16)
    rden = p.sb("rden", 1, 512, F32)
    bcr = p.sb("bcr", 128, 512, F32)
    obuf = p.sb("obuf", 128, 1024, BF16)
    xres = p.sb("xres", 128, 512, F32)
    oout = p.sb("oout", 128, 512, F32)

    def xnc(c, off, n=512):
        return xn[:, c * TK + off:c * TK + off + n]

    for tt in range(8):
        with p.grp() as g:
            for c in range(8):
                g.tensor.matmul(Pq[:], wksb[:, c * 128:(c + 1) * 128], xnc(c, tt * 512),
                                start=(c == 0), stop=(c == 7))
        with p.grp() as g:
            g.scalar.copy(out=kfp[:], in_=Pq[:])
        ce_t = cks[0:64, tt * 512:(tt + 1) * 512]
        co_t = cks[64:128, tt * 512:(tt + 1) * 512]
        se_t = sks[0:64, tt * 512:(tt + 1) * 512]
        so_t = sks[64:128, tt * 512:(tt + 1) * 512]
        with p.grp() as g:
            g.vector.tensor_tensor(out=s64a[:], in0=kfp[0:64, :], in1=ce_t, op=ALU.mult)
            g.gpsimd.tensor_tensor(out=s64b[:], in0=kfp[64:128, :], in1=so_t, op=ALU.mult)
        with p.grp() as g:
            g.vector.tensor_tensor(out=krot[0:64, tt * 512:(tt + 1) * 512],
                                   in0=s64a[:], in1=s64b[:], op=ALU.subtract)
        with p.grp() as g:
            g.vector.tensor_tensor(out=s64a2[:], in0=kfp[64:128, :], in1=co_t, op=ALU.mult)
            g.gpsimd.tensor_tensor(out=s64b[:], in0=kfp[0:64, :], in1=se_t, op=ALU.mult)
        with p.grp() as g:
            g.vector.tensor_tensor(out=krot[64:128, tt * 512:(tt + 1) * 512],
                                   in0=s64a2[:], in1=s64b[:], op=ALU.add)
        with p.grp() as g:
            g.scalar.activation(out=sqbuf[:], in_=krot[:, tt * 512:(tt + 1) * 512], func=AF.Square)
        with p.grp() as g:
            g.tensor.matmul(Pb[0:1, :], ones128f[:], sqbuf[:], start=True, stop=True)
        with p.grp() as g:
            g.scalar.copy(out=st1[:], in_=Pb[0:1, :])
        with p.grp() as g:
            g.vector.reduce_max(tmax[:], st1[:], axis=AX.X)
        with p.grp() as g:
            g.vector.tensor_tensor(out=kmax2[:], in0=kmax2[:], in1=tmax[:], op=ALU.max)

    for kt in range(32):
        with p.grp() as g:
            for c in range(8):
                g.tensor.matmul(Ps0[:, 0:128], xnc(c, kt * 128, 128), wvsb[:, c * 128:(c + 1) * 128],
                                start=(c == 0), stop=(c == 7))
        with p.grp() as g:
            g.scalar.copy(out=vtm[:, kt * 128:(kt + 1) * 128], in_=Ps0[:, 0:128])

    for tt in range(4):
        with p.grp() as g:
            g.vector.tensor_scalar(out=qgrow[:], in0=qis[:], scalar1=qbs[0:1, tt:tt + 1],
                                   scalar2=None, op0=ALU.add, op1=ALU.bypass)
        with p.grp() as g:
            g.tensor.matmul(Pb[:], ones1f[:], qgrow[:], start=True, stop=True)
        with p.grp() as g:
            g.scalar.copy(out=qgbc[:], in_=Pb[:])
        for h in range(2):
            with p.grp() as g:
                for c in range(8):
                    g.tensor.matmul(Pq[:], wqsb[:, c * 256 + h * 128:c * 256 + (h + 1) * 128],
                                    xnc(c, tt * 512), start=(c == 0), stop=(c == 7))
            with p.grp() as g:
                g.scalar.copy(out=kfp[:], in_=Pq[:])
            ce_t = cqs[0:64, tt * 512:(tt + 1) * 512]
            co_t = cqs[64:128, tt * 512:(tt + 1) * 512]
            se_t = sqs[0:64, tt * 512:(tt + 1) * 512]
            so_t = sqs[64:128, tt * 512:(tt + 1) * 512]
            with p.grp() as g:
                g.vector.tensor_tensor(out=s64a[:], in0=kfp[0:64, :], in1=ce_t, op=ALU.mult)
                g.gpsimd.tensor_tensor(out=s64b[:], in0=kfp[64:128, :], in1=so_t, op=ALU.mult)
            with p.grp() as g:
                g.vector.tensor_tensor(out=qrot[0:64, :], in0=s64a[:], in1=s64b[:], op=ALU.subtract)
            with p.grp() as g:
                g.vector.tensor_tensor(out=s64a2[:], in0=kfp[64:128, :], in1=co_t, op=ALU.mult)
                g.gpsimd.tensor_tensor(out=s64b[:], in0=kfp[0:64, :], in1=se_t, op=ALU.mult)
            with p.grp() as g:
                g.vector.tensor_tensor(out=qrot[64:128, :], in0=s64a2[:], in1=s64b[:], op=ALU.add)
            with p.grp() as g:
                g.scalar.activation(out=sqbuf[:], in_=qrot[:], func=AF.Square)
            with p.grp() as g:
                g.tensor.matmul(Pb[0:1, :], ones128f[:], sqbuf[:], start=True, stop=True)
            with p.grp() as g:
                g.scalar.copy(out=st1[:], in_=Pb[0:1, :])
            with p.grp() as g:
                g.vector.tensor_scalar(out=st1[:], in0=st1[:], scalar1=kmax2[0:1, 0:1],
                                       scalar2=None, op0=ALU.mult, op1=ALU.bypass)
            with p.grp() as g:
                g.scalar.activation(out=st1[:], in_=st1[:], func=AF.Ln)
            with p.grp() as g:
                g.scalar.activation(out=mrow[:], in_=st1[:], func=AF.Exp, scale=0.5)
            with p.grp() as g:
                g.scalar.mul(out=mrow[:], in_=mrow[:], mul=-1.0)
            with p.grp() as g:
                g.vector.tensor_copy(negmb[:], mrow[:])
            for kt in range(32):
                pidx = kt % 2
                Ps = Ps0 if pidx == 0 else Ps1
                with p.grp() as g:
                    g.tensor.matmul(Ps[:], krot[:, kt * 128:(kt + 1) * 128], qrot[:],
                                    start=True, stop=False)
                    g.gpsimd.tensor_scalar(out=maskb[pidx][:], in0=qgbc[:],
                                           scalar1=kgs[:, kt:kt + 1], scalar2=0.0,
                                           op0=ALU.subtract, op1=ALU.is_ge)
                with p.grp() as g:
                    g.tensor.matmul(Ps[:], onesr1b[:], negmb[:], start=False, stop=True)
                with p.grp() as g:
                    g.scalar.activation(out=Eb[pidx][:], in_=Ps[:], func=AF.Exp)
                with p.grp() as g:
                    g.vector.scalar_tensor_tensor(out=Em[pidx][:], in0=Eb[pidx][:], scalar=1.0,
                                                  in1=maskb[pidx][:], op0=ALU.mult, op1=ALU.mult)
                with p.grp() as g:
                    g.tensor.matmul(Pden[:], ones128b[:], Em[pidx][:],
                                    start=(kt == 0), stop=(kt == 31))
                    g.tensor.matmul(Pav[:], vtm[:, kt * 128:(kt + 1) * 128], Em[pidx][:],
                                    start=(kt == 0), stop=(kt == 31))
            with p.grp() as g:
                g.scalar.activation(out=rden[:], in_=Pden[:], func=AF.Ln)
            with p.grp() as g:
                g.scalar.activation(out=rden[:], in_=rden[:], func=AF.Exp, scale=-1.0)
            with p.grp() as g:
                g.tensor.matmul(Pb[:], ones1f[:], rden[:], start=True, stop=True)
            with p.grp() as g:
                g.scalar.copy(out=bcr[:], in_=Pb[:])
            with p.grp() as g:
                g.vector.tensor_tensor(out=obuf[:, h * 512:(h + 1) * 512], in0=Pav[:],
                                       in1=bcr[:], op=ALU.mult)
        for d in range(8):
            with p.grp() as g:
                g.sync.dma_start(out=xres[:], in_=xin[d * 128:(d + 1) * 128, tt * 512:(tt + 1) * 512])
                for h in range(2):
                    g.tensor.matmul(Pq[:], wosb[:, h * 1024 + d * 128:h * 1024 + (d + 1) * 128],
                                    obuf[:, h * 512:(h + 1) * 512], start=(h == 0), stop=(h == 1))
            with p.grp() as g:
                g.vector.tensor_tensor(out=oout[:], in0=Pq[:], in1=xres[:], op=ALU.add)
            with p.grp() as g:
                g.scalar.dma_start(out=xout[d * 128:(d + 1) * 128, tt * 512:(tt + 1) * 512],
                                   in_=oout[:])
    return p


_ROPE_CACHE = None


def _rope_tables():
    global _ROPE_CACHE
    if _ROPE_CACHE is None:
        i = np.arange(64, dtype=np.float64)
        theta = 10000.0 ** (-i / 32.0)
        t = np.arange(4096, dtype=np.float64)
        ang = np.outer(theta, t)
        _ROPE_CACHE = (np.cos(ang), np.sin(ang))
    return _ROPE_CACHE


def host_attn_a(xfm, wb):
    cosg, sing = _rope_tables()
    scale = 1.0 / math.sqrt(128.0)
    p = build_attn_a()
    qiota = np.arange(512, dtype=np.float32).reshape(1, 512)
    kidx = np.arange(128, dtype=np.float32)
    in_maps = []
    for c in range(8):
        b, qh = c // 2, c % 2
        qlo = qh * 2048
        perm = np.concatenate([np.arange(qlo, qlo + 2048), np.arange(0, qlo),
                               np.arange(qlo + 2048, 4096)])
        in_maps.append({
            "xin": np.ascontiguousarray(xfm[b][:, perm]),
            "wq": wb["l0aa_wq"], "wk": wb["l0aa_wk"], "wv": wb["l0aa_wv"], "wo": wb["l0aa_wo"],
            "cosq": _bf16(np.vstack([cosg[:, qlo:qlo + 2048]] * 2) * scale),
            "sinq": _bf16(np.vstack([sing[:, qlo:qlo + 2048]] * 2) * scale),
            "cosk": _bf16(np.vstack([cosg[:, perm]] * 2)),
            "sink": _bf16(np.vstack([sing[:, perm]] * 2)),
            "kglob": np.ascontiguousarray(
                (perm.reshape(32, 128).T).astype(np.float32)),
            "qbase": (qlo + np.arange(4, dtype=np.float32).reshape(1, 4) * 512),
            "qiota": qiota,
        })
    res = _run(p, in_maps)
    out = np.empty_like(xfm)
    for c in range(8):
        b, qh = c // 2, c % 2
        out[b, :, qh * 2048:(qh + 1) * 2048] = res[c]["xout"]
    return out


def build_attn2():
    p = Prog()
    TQ, TK = 2048, 4096
    xin = p.din("xin", [1024, TK], F32)
    wq = p.din("wq", [1024, 1024], BF16)
    wk = p.din("wk", [1024, 1024], BF16)
    wv = p.din("wv", [1024, 1024], BF16)
    wo = p.din("wo", [1024, 1024], BF16)
    bqd = p.din("bqd", [128, 8], F32)
    bkd = p.din("bkd", [128, 8], F32)
    bvd = p.din("bvd", [1, 1024], F32)
    bod = p.din("bod", [128, 8], F32)
    xout = p.dout("xout", [1024, TQ], F32)
    oscr = p.dtmp("oscr", [128, 8 * TQ], BF16)

    xn = p.sb("xn", 128, 8 * TK, BF16)
    wosb = p.sb("wosb", 128, 8 * 1024, BF16)
    bqs = p.sb("bqs", 128, 8, F32)
    bks = p.sb("bks", 128, 8, F32)
    bvf = p.sb("bvf", 1, 1024, F32)
    bvs = p.sb("bvs", 1, 1024, BF16)
    bos = p.sb("bos", 128, 8, F32)
    ones128b = p.sb("ones128b", 128, 1, BF16)
    onestop = p.sb("onestop", 128, 1, F32)
    onesbot = p.sb("onesbot", 128, 1, F32)
    ones1f = p.sb("ones1f", 1, 128, F32)
    onesr1b = p.sb("onesr1b", 1, 128, BF16)

    with p.grp() as g:
        for c in range(8):
            g.sync.dma_start(out=wosb[:, c * 1024:(c + 1) * 1024], in_=wo[c * 128:(c + 1) * 128, :])
        g.scalar.dma_start(out=bqs[:], in_=bqd[:])
        g.scalar.dma_start(out=bks[:], in_=bkd[:])
        g.scalar.dma_start(out=bvf[:], in_=bvd[:])
        g.scalar.dma_start(out=bos[:], in_=bod[:])
        g.vector.memset(ones128b[:], 1.0)
        g.vector.memset(onestop[:], 1.0)
        g.gpsimd.memset(onesbot[:], 0.0)
        g.gpsimd.memset(ones1f[:], 1.0)
    with p.grp() as g:
        g.vector.memset(onestop[64:128, 0:1], 0.0)
        g.gpsimd.memset(onesbot[64:128, 0:1], 1.0)
        g.scalar.mul(out=bqs[:], in_=bqs[:], mul=0.125)
    with p.grp() as g:
        g.gpsimd.memset(onesr1b[:], 1.0)
        g.vector.tensor_copy(bvs[:], bvf[:])

    Pq = p.psum("Pq", 128, 512)
    Ps0 = p.psum("Ps0", 128, 512)
    Ps1 = p.psum("Ps1", 128, 512)
    Pav = p.psum("Pav", 128, 512)
    Pb = p.psum("Pb", 128, 512)
    Pden = p.psum("Pden", 1, 512)

    L = ln_alloc(p)
    for tt in range(8):
        emit_ln_tile(p, L, xin, tt, xn, TK, 0, Pq[:], Ps0[:])
    ln_free(p, L)

    kp = p.sb("kp", 128, TK, BF16)
    vp = p.sb("vp", 128, 32 * 128, BF16)
    wqp = p.sb("wqp", 128, 8 * 128, BF16)
    wkp = p.sb("wkp", 128, 8 * 128, BF16)
    wvp = p.sb("wvp", 128, 8 * 128, BF16)
    sqbuf = p.sb("sqbuf", 128, 512, F32)
    st1 = p.sb("st1", 1, 512, F32)
    tmax = p.sb("tmax", 1, 1, F32)
    kmax2 = p.sb("kmax2", 1, 2, F32)
    qm = p.sb("qm", 128, 512, BF16)
    qhat = p.sb("qhat", 128, 512, BF16)
    Eb = [p.sb(f"Eb{i}", 128, 512, BF16) for i in range(2)]
    mrow = p.sb("mrow", 1, 512, F32)
    negmb = p.sb("negmb", 1, 512, BF16)
    rden = p.sb("rden", 1, 512, F32)
    bcr = p.sb("bcr", 128, 512, F32)
    obuf = p.sb("obuf", 128, 512, BF16)
    xres = p.sb("xres", 128, 512, F32)
    oout = p.sb("oout", 128, 512, F32)

    def xnc(c, off, n=512):
        return xn[:, c * TK + off:c * TK + off + n]

    for hp in range(8):
        with p.grp() as g:
            for c in range(8):
                g.sync.dma_start(out=wqp[:, c * 128:(c + 1) * 128],
                                 in_=wq[c * 128:(c + 1) * 128, hp * 128:(hp + 1) * 128])
                g.sync.dma_start(out=wkp[:, c * 128:(c + 1) * 128],
                                 in_=wk[c * 128:(c + 1) * 128, hp * 128:(hp + 1) * 128])
                g.scalar.dma_start(out=wvp[:, c * 128:(c + 1) * 128],
                                   in_=wv[c * 128:(c + 1) * 128, hp * 128:(hp + 1) * 128])
        with p.grp() as g:
            g.vector.memset(kmax2[:], 0.0)
        for tt in range(8):
            with p.grp() as g:
                for c in range(8):
                    g.tensor.matmul(Pq[:], wkp[:, c * 128:(c + 1) * 128], xnc(c, tt * 512),
                                    start=(c == 0), stop=(c == 7))
            with p.grp() as g:
                g.scalar.activation(out=kp[:, tt * 512:(tt + 1) * 512], in_=Pq[:],
                                    func=AF.Identity, bias=bks[:, hp:hp + 1])
            with p.grp() as g:
                g.scalar.activation(out=sqbuf[:], in_=kp[:, tt * 512:(tt + 1) * 512], func=AF.Square)
            for hh, ov in ((0, onestop), (1, onesbot)):
                with p.grp() as g:
                    g.tensor.matmul(Pb[0:1, :], ov[:], sqbuf[:], start=True, stop=True)
                with p.grp() as g:
                    g.scalar.copy(out=st1[:], in_=Pb[0:1, :])
                with p.grp() as g:
                    g.vector.reduce_max(tmax[:], st1[:], axis=AX.X)
                with p.grp() as g:
                    g.vector.tensor_tensor(out=kmax2[0:1, hh:hh + 1], in0=kmax2[0:1, hh:hh + 1],
                                           in1=tmax[:], op=ALU.max)
        for kt in range(32):
            with p.grp() as g:
                for c in range(8):
                    g.tensor.matmul(Ps0[:, 0:128], xnc(c, kt * 128, 128),
                                    wvp[:, c * 128:(c + 1) * 128], start=(c == 0), stop=False)
                g.tensor.matmul(Ps0[:, 0:128], onesr1b[:], bvs[0:1, hp * 128:(hp + 1) * 128],
                                start=False, stop=True)
            with p.grp() as g:
                g.scalar.copy(out=vp[:, kt * 128:(kt + 1) * 128], in_=Ps0[:, 0:128])
        for tt in range(4):
            with p.grp() as g:
                for c in range(8):
                    g.tensor.matmul(Pq[:], wqp[:, c * 128:(c + 1) * 128], xnc(c, tt * 512),
                                    start=(c == 0), stop=(c == 7))
            for hh in range(2):
                with p.grp() as g:
                    g.vector.memset(qm[:], 0.0)
                lo, hi = (0, 64) if hh == 0 else (64, 128)
                with p.grp() as g:
                    g.scalar.activation(out=qm[lo:hi, :], in_=Pq[lo:hi, :], func=AF.Identity,
                                        scale=0.125, bias=bqs[lo:hi, hp:hp + 1])
                with p.grp() as g:
                    g.scalar.activation(out=sqbuf[:], in_=qm[:], func=AF.Square)
                with p.grp() as g:
                    g.tensor.matmul(Pb[0:1, :], onestop[:] if hh == 0 else onesbot[:],
                                    sqbuf[:], start=True, stop=True)
                with p.grp() as g:
                    g.scalar.copy(out=st1[:], in_=Pb[0:1, :])
                with p.grp() as g:
                    g.vector.tensor_scalar(out=st1[:], in0=st1[:], scalar1=kmax2[0:1, hh:hh + 1],
                                           scalar2=None, op0=ALU.mult, op1=ALU.bypass)
                with p.grp() as g:
                    g.scalar.activation(out=st1[:], in_=st1[:], func=AF.Ln)
                with p.grp() as g:
                    g.scalar.activation(out=mrow[:], in_=st1[:], func=AF.Exp, scale=0.5)
                with p.grp() as g:
                    g.scalar.mul(out=mrow[:], in_=mrow[:], mul=-1.0)
                with p.grp() as g:
                    g.vector.tensor_copy(negmb[:], mrow[:])
                for kt in range(32):
                    Ps = Ps0 if kt % 2 == 0 else Ps1
                    with p.grp() as g:
                        g.tensor.matmul(Ps[:], kp[:, kt * 128:(kt + 1) * 128], qm[:],
                                        start=True, stop=False)
                    with p.grp() as g:
                        g.tensor.matmul(Ps[:], onesr1b[:], negmb[:], start=False, stop=True)
                    with p.grp() as g:
                        g.scalar.activation(out=Eb[kt % 2][:], in_=Ps[:], func=AF.Exp)
                    with p.grp() as g:
                        g.tensor.matmul(Pden[:], ones128b[:], Eb[kt % 2][:],
                                        start=(kt == 0), stop=(kt == 31))
                        g.tensor.matmul(Pav[0:64, :], vp[:, kt * 128 + hh * 64:kt * 128 + hh * 64 + 64],
                                        Eb[kt % 2][:], start=(kt == 0), stop=(kt == 31))
                with p.grp() as g:
                    g.scalar.activation(out=rden[:], in_=Pden[:], func=AF.Ln)
                with p.grp() as g:
                    g.scalar.activation(out=rden[:], in_=rden[:], func=AF.Exp, scale=-1.0)
                with p.grp() as g:
                    g.tensor.matmul(Pb[:], ones1f[:], rden[:], start=True, stop=True)
                with p.grp() as g:
                    g.scalar.copy(out=bcr[:], in_=Pb[:])
                with p.grp() as g:
                    g.vector.tensor_tensor(out=obuf[hh * 64:(hh + 1) * 64, :], in0=Pav[0:64, :],
                                           in1=bcr[0:64, :], op=ALU.mult)
            with p.grp() as g:
                g.scalar.dma_start(out=oscr[:, hp * TQ + tt * 512:hp * TQ + (tt + 1) * 512],
                                   in_=obuf[:])

    oc = p.sb("oc", 128, 8 * 512, BF16)
    o1 = p.sb("o1", 128, 512, F32)
    for tt in range(4):
        with p.grp() as g:
            for hp in range(8):
                g.sync.dma_start(out=oc[:, hp * 512:(hp + 1) * 512],
                                 in_=oscr[:, hp * TQ + tt * 512:hp * TQ + (tt + 1) * 512])
        for d in range(8):
            with p.grp() as g:
                g.sync.dma_start(out=xres[:], in_=xin[d * 128:(d + 1) * 128, tt * 512:(tt + 1) * 512])
                for hp in range(8):
                    g.tensor.matmul(Pq[:], wosb[:, hp * 1024 + d * 128:hp * 1024 + (d + 1) * 128],
                                    oc[:, hp * 512:(hp + 1) * 512], start=(hp == 0), stop=(hp == 7))
            with p.grp() as g:
                g.scalar.activation(out=o1[:], in_=Pq[:], func=AF.Identity, bias=bos[:, d:d + 1])
            with p.grp() as g:
                g.vector.tensor_tensor(out=oout[:], in0=o1[:], in1=xres[:], op=ALU.add)
            with p.grp() as g:
                g.scalar.dma_start(out=xout[d * 128:(d + 1) * 128, tt * 512:(tt + 1) * 512],
                                   in_=oout[:])
    return p


def host_attn2(xfm, wb, inputs):
    p = build_attn2()
    bq = np.ascontiguousarray(inputs["L1_attn_a_bq"].reshape(8, 128).T, np.float32)
    bk = np.ascontiguousarray(inputs["L1_attn_a_bk"].reshape(8, 128).T, np.float32)
    bv = np.ascontiguousarray(inputs["L1_attn_a_bv"].reshape(1, 1024), np.float32)
    bo = np.ascontiguousarray(inputs["L1_attn_a_bo"].reshape(8, 128).T, np.float32)
    in_maps = []
    for c in range(8):
        b, qh = c // 2, c % 2
        qlo = qh * 2048
        perm = np.concatenate([np.arange(qlo, qlo + 2048), np.arange(0, qlo),
                               np.arange(qlo + 2048, 4096)])
        in_maps.append({
            "xin": np.ascontiguousarray(xfm[b][:, perm]),
            "wq": wb["l1aa_wq"], "wk": wb["l1aa_wk"], "wv": wb["l1aa_wv"], "wo": wb["l1aa_wo"],
            "bqd": bq, "bkd": bk, "bvd": bv, "bod": bo,
        })
    res = _run(p, in_maps)
    out = np.empty_like(xfm)
    for c in range(8):
        b, qh = c // 2, c % 2
        out[b, :, qh * 2048:(qh + 1) * 2048] = res[c]["xout"]
    return out


def build_ffn2(tpc=2048):
    p = Prog()
    xin = p.din("xin", [1024, tpc], F32)
    w1qd = p.din("w1qd", [1024, 4096], BF16)
    w2qd = p.din("w2qd", [4096, 1024], BF16)
    wsd = p.din("wsd", [1, 2], F32)
    b1d = p.din("b1d", [128, 32], F32)
    b2d = p.din("b2d", [128, 8], F32)
    identd = p.din("identd", [128, 128], F32)
    xout = p.dout("xout", [1024, tpc], F32)

    idsb = p.sb("idsb", 128, 128, F32)
    wss = p.sb("wss", 1, 2, F32)
    b1s = p.sb("b1s", 128, 32, F32)
    b2s = p.sb("b2s", 128, 8, F32)
    ones1f = p.sb("ones1f", 1, 128, F32)
    with p.grp() as g:
        g.sync.dma_start(out=idsb[:], in_=identd[:])
        g.scalar.dma_start(out=wss[:], in_=wsd[:])
        g.scalar.dma_start(out=b1s[:], in_=b1d[:])
        g.scalar.dma_start(out=b2s[:], in_=b2d[:])
        g.gpsimd.memset(ones1f[:], 1.0)

    PS = [p.psum(f"PS{i}", 128, 512) for i in range(8)]
    L = ln_alloc(p)

    w1sb = p.sb("w1sb", 128, 8 * 4096, BF16)
    w2sb = [p.sb(f"w2sb{i}", 128, 1024, BF16) for i in range(2)]
    hsb = p.sb("hsb", 128, 32 * 512, BF16)
    xq = p.sb("xq", 128, 8 * 512, BF16)
    hqp = [p.sb(f"hqp{i}", 128, 512, BF16) for i in range(2)]
    macc = p.sb("macc", 128, 512, F32)
    absb = [p.sb(f"absb{i}", 128, 512, F32) for i in range(2)]
    mtb = p.sb("mtb", 128, 128, F32)
    xscol = p.sb("xscol", 128, 4, F32)
    rrow = p.sb("rrow", 1, 512, F32)
    srow = p.sb("srow", 1, 512, F32)
    rbc = p.sb("rbc", 128, 512, F32)
    sbc = p.sb("sbc", 128, 512, F32)
    qtmp = p.sb("qtmp", 128, 512, F32)
    ot1 = p.sb("ot1", 128, 512, F32)
    oout = p.sb("oout", 128, 512, F32)

    with p.grp() as g:
        for c in range(8):
            g.sync.dma_start(out=w1sb[:, c * 4096:(c + 1) * 4096],
                             in_=w1qd[c * 128:(c + 1) * 128, :])

    def quant_dance(p, src_sb, nch, which_ws):
        with p.grp() as g:
            g.scalar.activation(out=macc[:], in_=src_sb[:, 0:512], func=AF.Abs)
        with p.grp() as g:
            g.scalar.activation(out=absb[1][:], in_=src_sb[:, 512:1024], func=AF.Abs)
        for c in range(1, nch):
            with p.grp() as g:
                if c + 1 < nch:
                    g.scalar.activation(out=absb[(c + 1) % 2][:],
                                        in_=src_sb[:, (c + 1) * 512:(c + 2) * 512], func=AF.Abs)
                g.vector.tensor_tensor(out=macc[:], in0=macc[:], in1=absb[c % 2][:], op=ALU.max)
        for j in range(4):
            with p.grp() as g:
                g.tensor.transpose(PS[2][:, 0:128], macc[:, j * 128:(j + 1) * 128], idsb[:])
            with p.grp() as g:
                g.scalar.copy(out=mtb[:], in_=PS[2][:, 0:128])
            with p.grp() as g:
                g.vector.reduce_max(xscol[:, j:j + 1], mtb[:], axis=AX.X)
        with p.grp() as g:
            g.vector.tensor_scalar(out=xscol[:], in0=xscol[:], scalar1=1e-5, scalar2=None,
                                   op0=ALU.max, op1=ALU.bypass)
        for j in range(4):
            with p.grp() as g:
                g.tensor.transpose(PS[3][0:1, 0:128], xscol[:, j:j + 1], idsb[:])
            with p.grp() as g:
                g.scalar.copy(out=srow[0:1, j * 128:(j + 1) * 128], in_=PS[3][0:1, 0:128])
        with p.grp() as g:
            g.scalar.activation(out=rrow[:], in_=srow[:], func=AF.Ln)
        with p.grp() as g:
            g.scalar.activation(out=rrow[:], in_=rrow[:], func=AF.Exp, scale=-1.0)
        with p.grp() as g:
            g.vector.tensor_scalar(out=rrow[:], in0=rrow[:], scalar1=127.0, scalar2=None,
                                   op0=ALU.mult, op1=ALU.bypass)
            g.scalar.mul(out=srow[:], in_=srow[:], mul=1.0 / 127.0)
        with p.grp() as g:
            g.vector.tensor_scalar(out=srow[:], in0=srow[:], scalar1=wss[0:1, which_ws:which_ws + 1],
                                   scalar2=None, op0=ALU.mult, op1=ALU.bypass)
        with p.grp() as g:
            g.tensor.matmul(PS[0][:], ones1f[:], rrow[:], start=True, stop=True)
            g.tensor.matmul(PS[1][:], ones1f[:], srow[:], start=True, stop=True)
        with p.grp() as g:
            g.scalar.copy(out=rbc[:], in_=PS[0][:])
            g.vector.tensor_copy(sbc[:], PS[1][:])

    def quantize(p, src, dst):
        with p.grp() as g:
            g.vector.tensor_tensor(out=qtmp[:], in0=src, in1=rbc[:], op=ALU.mult)
        with p.grp() as g:
            g.vector.tensor_scalar(out=qtmp[:], in0=qtmp[:], scalar1=RINT_C, scalar2=None,
                                   op0=ALU.add, op1=ALU.bypass)
        with p.grp() as g:
            g.vector.tensor_scalar(out=qtmp[:], in0=qtmp[:], scalar1=RINT_C, scalar2=None,
                                   op0=ALU.subtract, op1=ALU.bypass)
        with p.grp() as g:
            g.vector.tensor_scalar(out=dst, in0=qtmp[:], scalar1=127.0, scalar2=-128.0,
                                   op0=ALU.min, op1=ALU.max)

    NT = tpc // 512
    for tt in range(NT):
        emit_ln_tile(p, L, xin, tt, None, 0, 0, PS[0][:], PS[1][:])
        quant_dance(p, L["xf"], 8, 0)
        for c in range(8):
            quantize(p, L["xf"][:, c * 512:(c + 1) * 512], xq[:, c * 512:(c + 1) * 512])
        for r in range(32):
            Ph = PS[r % 2]
            with p.grp() as g:
                for c in range(8):
                    g.tensor.matmul(Ph[:], w1sb[:, c * 4096 + r * 128:c * 4096 + (r + 1) * 128],
                                    xq[:, c * 512:(c + 1) * 512], start=(c == 0), stop=(c == 7))
            with p.grp() as g:
                g.vector.tensor_tensor(out=qtmp[:], in0=Ph[:], in1=sbc[:], op=ALU.mult)
            with p.grp() as g:
                g.scalar.activation(out=hsb[:, r * 512:(r + 1) * 512], in_=qtmp[:],
                                    func=AF.Gelu, bias=b1s[:, r:r + 1])
        quant_dance(p, hsb, 32, 1)
        quantize(p, hsb[:, 0:512], hqp[0][:])
        with p.grp() as g:
            g.sync.dma_start(out=w2sb[0][:], in_=w2qd[0:128, :])
        for r in range(32):
            with p.grp() as g:
                if r + 1 < 32:
                    g.sync.dma_start(out=w2sb[(r + 1) % 2][:],
                                     in_=w2qd[(r + 1) * 128:(r + 2) * 128, :])
                for d in range(8):
                    g.tensor.matmul(PS[d][:], w2sb[r % 2][:, d * 128:(d + 1) * 128],
                                    hqp[r % 2][:], start=(r == 0), stop=(r == 31))
            if r + 1 < 32:
                quantize(p, hsb[:, (r + 1) * 512:(r + 2) * 512], hqp[(r + 1) % 2][:])
        with p.grp() as g:
            for c in range(8):
                g.sync.dma_start(out=L["xg"][:, c * 512:(c + 1) * 512],
                                 in_=xin[c * 128:(c + 1) * 128, tt * 512:(tt + 1) * 512])
        for d in range(8):
            with p.grp() as g:
                g.vector.tensor_tensor(out=qtmp[:], in0=PS[d][:], in1=sbc[:], op=ALU.mult)
            with p.grp() as g:
                g.scalar.activation(out=ot1[:], in_=qtmp[:], func=AF.Identity, bias=b2s[:, d:d + 1])
            with p.grp() as g:
                g.vector.tensor_tensor(out=oout[:], in0=ot1[:],
                                       in1=L["xg"][:, d * 512:(d + 1) * 512], op=ALU.add)
            with p.grp() as g:
                g.scalar.dma_start(out=xout[d * 128:(d + 1) * 128, tt * 512:(tt + 1) * 512],
                                   in_=oout[:])
    return p


def host_ffn2(xfm, wb, inputs):
    p = build_ffn2()
    b1 = np.ascontiguousarray(inputs["L1_ffn_b1"].reshape(32, 128).T, np.float32)
    b2 = np.ascontiguousarray(inputs["L1_ffn_b2"].reshape(8, 128).T, np.float32)
    ident = np.eye(128, dtype=np.float32)
    ws = np.ascontiguousarray(wb["l1f_wscale"].reshape(1, 2), np.float32)
    in_maps = []
    for c in range(8):
        b, h = c // 2, c % 2
        in_maps.append({
            "xin": np.ascontiguousarray(xfm[b, :, h * 2048:(h + 1) * 2048]),
            "w1qd": wb["l1f_w1q"], "w2qd": wb["l1f_w2q"], "wsd": ws,
            "b1d": b1, "b2d": b2, "identd": ident,
        })
    res = _run(p, in_maps)
    out = np.empty_like(xfm)
    for c in range(8):
        b, h = c // 2, c % 2
        out[b, :, h * 2048:(h + 1) * 2048] = res[c]["xout"]
    return out


CPI2 = 1.0 + math.pi / 2.0
CLAMPC = 2e-7 - 1e-14


def build_attn_b():
    p = Prog()
    T = 4096
    xin = p.din("xin", [1024, T], F32)
    wqs = p.din("wqs", [1024, 512], BF16)
    wks = p.din("wks", [1024, 512], BF16)
    wvs = p.din("wvs", [1024, 512], BF16)
    bqd = p.din("bqd", [128, 4], F32)
    bkd = p.din("bkd", [128, 4], F32)
    bvd = p.din("bvd", [1, 512], F32)
    ohd = p.dout("ohd", [512, T], BF16)
    ascr = p.dtmp("ascr", [128, 2 * 8 * 32 * 512], BF16)

    xn = p.sb("xn", 128, 8 * T, BF16)
    ones1f = p.sb("ones1f", 1, 128, F32)
    onestop = p.sb("onestop", 128, 1, F32)
    onesbot = p.sb("onesbot", 128, 1, F32)
    rtop = p.sb("rtop", 1, 128, F32)
    rbot = p.sb("rbot", 1, 128, F32)
    halfc = p.sb("halfc", 128, 1, F32)
    onec = p.sb("onec", 128, 1, F32)
    bqs = p.sb("bqs", 128, 4, F32)
    bks = p.sb("bks", 128, 4, F32)
    bvf = p.sb("bvf", 1, 512, F32)
    bvs = p.sb("bvs", 1, 512, BF16)
    onesr1b = p.sb("onesr1b", 1, 128, BF16)

    with p.grp() as g:
        g.scalar.dma_start(out=bqs[:], in_=bqd[:])
        g.scalar.dma_start(out=bks[:], in_=bkd[:])
        g.scalar.dma_start(out=bvf[:], in_=bvd[:])
        g.vector.memset(onestop[:], 1.0)
        g.gpsimd.memset(onesbot[:], 0.0)
        g.vector.memset(rtop[:], 1.0)
        g.gpsimd.memset(rbot[:], 0.0)
        g.vector.memset(halfc[:], 0.5)
        g.gpsimd.memset(onec[:], 1.0)
        g.gpsimd.memset(ones1f[:], 1.0)
        g.vector.memset(onesr1b[:], 1.0)
    with p.grp() as g:
        g.vector.memset(onestop[64:128, 0:1], 0.0)
        g.gpsimd.memset(onesbot[64:128, 0:1], 1.0)
        g.vector.memset(rtop[0:1, 64:128], 0.0)
        g.gpsimd.memset(rbot[0:1, 64:128], 1.0)
        g.scalar.copy(out=bvs[:], in_=bvf[:])

    Pq = p.psum("Pq", 128, 512)
    Pav = p.psum("Pav", 128, 512)
    Ps0 = p.psum("Ps0", 128, 1024)
    Ps1 = p.psum("Ps1", 128, 1024)
    Pb = p.psum("Pb", 128, 512)
    Pden = p.psum("Pden", 1, 512)

    L = ln_alloc(p)
    for tt in range(8):
        emit_ln_tile(p, L, xin, tt, xn, T, 0, Pq[:], Pav[:])
    ln_free(p, L)

    kfp = p.sb("kfp", 128, 512, F32)
    khat = p.sb("khat", 128, T, BF16)
    vtm = p.sb("vtm", 128, 32 * 128, BF16)
    vsc = p.sb("vsc", 128, 32 * 128, BF16)
    sq5 = p.sb("sq5", 128, 512, F32)
    nrm = p.sb("nrm", 1, 1024, F32)
    rkbc = p.sb("rkbc", 128, 512, F32)
    qm = p.sb("qm", 128, 512, BF16)
    qhat = p.sb("qhat", 128, 512, BF16)
    sqa = p.sb("sqa", 128, 1024, F32)
    ccb = p.sb("ccb", 128, 1024, F32)
    rchn = p.sb("rchn", 128, 1024, F32)
    ab = [p.sb(f"ab{i}", 128, 1024, BF16) for i in range(2)]
    ncq = p.sb("ncq", 128, 32, F32)
    ncacc = [p.sb(f"ncacc{i}", 128, 32, F32) for i in range(2)]
    crb = p.sb("crb", 128, 32, BF16)
    rden = p.sb("rden", 1, 512, F32)
    bcr = p.sb("bcr", 128, 512, F32)
    obuf = p.sb("obuf", 128, 512, BF16)

    def xnc(c, off, n=512):
        return xn[:, c * T + off:c * T + off + n]

    wqp = p.sb("wqp", 128, 8 * 128, BF16)
    wkp = p.sb("wkp", 128, 8 * 128, BF16)
    wvp = p.sb("wvp", 128, 8 * 128, BF16)

    def l2pair(pidx, bias, dst, toff):
        with p.grp() as g:
            g.scalar.activation(out=kfp[:], in_=Pq[:], func=AF.Identity, bias=bias)
        with p.grp() as g:
            g.scalar.activation(out=sq5[:], in_=kfp[:], func=AF.Square)
        with p.grp() as g:
            g.tensor.matmul(Pb[0:1, :], onestop[:], sq5[:], start=True, stop=True)
        with p.grp() as g:
            g.scalar.copy(out=nrm[0:1, 0:512], in_=Pb[0:1, :])
        with p.grp() as g:
            g.tensor.matmul(Pb[0:1, :], onesbot[:], sq5[:], start=True, stop=True)
        with p.grp() as g:
            g.scalar.copy(out=nrm[0:1, 512:1024], in_=Pb[0:1, :])
        with p.grp() as g:
            g.vector.tensor_scalar(out=nrm[:], in0=nrm[:], scalar1=1e-24, scalar2=None,
                                   op0=ALU.max, op1=ALU.bypass)
        with p.grp() as g:
            g.scalar.activation(out=nrm[:], in_=nrm[:], func=AF.Ln)
        with p.grp() as g:
            g.scalar.activation(out=nrm[:], in_=nrm[:], func=AF.Exp, scale=-0.5)
        with p.grp() as g:
            g.tensor.matmul(Pb[:], rtop[:], nrm[0:1, 0:512], start=True, stop=False)
            g.tensor.matmul(Pb[:], rbot[:], nrm[0:1, 512:1024], start=False, stop=True)
        with p.grp() as g:
            g.scalar.copy(out=rkbc[:], in_=Pb[:])
        with p.grp() as g:
            g.vector.tensor_tensor(out=dst[:, toff:toff + 512], in0=kfp[:], in1=rkbc[:],
                                   op=ALU.mult)

    for hp in range(4):
        with p.grp() as g:
            for c in range(8):
                g.sync.dma_start(out=wqp[:, c * 128:(c + 1) * 128],
                                 in_=wqs[c * 128:(c + 1) * 128, hp * 128:(hp + 1) * 128])
                g.sync.dma_start(out=wkp[:, c * 128:(c + 1) * 128],
                                 in_=wks[c * 128:(c + 1) * 128, hp * 128:(hp + 1) * 128])
                g.scalar.dma_start(out=wvp[:, c * 128:(c + 1) * 128],
                                   in_=wvs[c * 128:(c + 1) * 128, hp * 128:(hp + 1) * 128])
        for tt in range(8):
            with p.grp() as g:
                for c in range(8):
                    g.tensor.matmul(Pq[:], wkp[:, c * 128:(c + 1) * 128],
                                    xnc(c, tt * 512), start=(c == 0), stop=(c == 7))
            l2pair(hp, bks[:, hp:hp + 1], khat, tt * 512)
        for kt in range(32):
            with p.grp() as g:
                for c in range(8):
                    g.tensor.matmul(Pav[:, 0:128], xnc(c, kt * 128, 128),
                                    wvp[:, c * 128:(c + 1) * 128],
                                    start=(c == 0), stop=False)
                g.tensor.matmul(Pav[:, 0:128], onesr1b[:],
                                bvs[0:1, hp * 128:(hp + 1) * 128], start=False, stop=True)
            with p.grp() as g:
                g.scalar.copy(out=vtm[:, kt * 128:(kt + 1) * 128], in_=Pav[:, 0:128])
        with p.grp() as g:
            g.vector.memset(ncacc[0][:], 0.0)
            g.gpsimd.memset(ncacc[1][:], 0.0)
        for tt in range(8):
            with p.grp() as g:
                for c in range(8):
                    g.tensor.matmul(Pq[:], wqp[:, c * 128:(c + 1) * 128],
                                    xnc(c, tt * 512), start=(c == 0), stop=(c == 7))
            l2pair(hp, bqs[:, hp:hp + 1], qhat, 0)
            for hh in range(2):
                lo, hi = (0, 64) if hh == 0 else (64, 128)
                with p.grp() as g:
                    g.vector.memset(qm[:], 0.0)
                with p.grp() as g:
                    g.scalar.copy(out=qm[lo:hi, :], in_=qhat[lo:hi, :])
                for k2 in range(16):
                    Ps = Ps0 if k2 % 2 == 0 else Ps1
                    with p.grp() as g:
                        g.tensor.matmul(Ps[:, 0:512], khat[:, (2 * k2) * 128:(2 * k2 + 1) * 128],
                                        qm[:], start=True, stop=True)
                        g.tensor.matmul(Ps[:, 512:1024], khat[:, (2 * k2 + 1) * 128:(2 * k2 + 2) * 128],
                                        qm[:], start=True, stop=True)
                    with p.grp() as g:
                        g.vector.tensor_scalar(out=ccb[:], in0=Ps[:], scalar1=1.0 - 1e-7,
                                               scalar2=-(1.0 - 1e-7), op0=ALU.min, op1=ALU.max)
                    with p.grp() as g:
                        g.scalar.activation(out=sqa[:], in_=ccb[:], func=AF.Sqrt, scale=-0.5,
                                            bias=halfc[:])
                    with p.grp() as g:
                        g.scalar.activation(out=rchn[:], in_=ccb[:], func=AF.Sqrt, scale=0.5,
                                            bias=halfc[:])
                    with p.grp() as g:
                        g.scalar.activation(out=rchn[:], in_=rchn[:], func=AF.Ln, bias=onec[:])
                    with p.grp() as g:
                        g.scalar.activation(out=rchn[:], in_=rchn[:], func=AF.Exp, scale=-1.0)
                    with p.grp() as g:
                        g.vector.tensor_tensor(out=rchn[:], in0=sqa[:], in1=rchn[:], op=ALU.mult)
                    with p.grp() as g:
                        g.scalar.activation(out=rchn[:], in_=rchn[:], func=AF.Arctan)
                    with p.grp() as g:
                        g.scalar.activation(out=rchn[:], in_=rchn[:], func=AF.Ln, scale=4.0,
                                            bias=onec[:])
                    abt = ab[k2 % 2]
                    with p.grp() as g:
                        g.scalar.activation(out=abt[:, 0:512], in_=rchn[:, 0:512], func=AF.Exp,
                                            scale=-3.2, accum_out=ncq[:, 2 * k2:2 * k2 + 1])
                    with p.grp() as g:
                        g.scalar.activation(out=abt[:, 512:1024], in_=rchn[:, 512:1024], func=AF.Exp,
                                            scale=-3.2, accum_out=ncq[:, 2 * k2 + 1:2 * k2 + 2])
                    with p.grp() as g:
                        g.scalar.dma_start(
                            out=ascr[:, ((hh * 8 + tt) * 32 + 2 * k2) * 512:((hh * 8 + tt) * 32 + 2 * k2 + 2) * 512],
                            in_=abt[:])
                with p.grp() as g:
                    g.vector.tensor_tensor(out=ncacc[hh][:], in0=ncacc[hh][:], in1=ncq[:], op=ALU.add)
        for hh in range(2):
            with p.grp() as g:
                g.scalar.activation(out=ncacc[hh][:], in_=ncacc[hh][:], func=AF.Ln)
            with p.grp() as g:
                g.scalar.activation(out=ncacc[hh][:], in_=ncacc[hh][:], func=AF.Exp, scale=-1.0)
            with p.grp() as g:
                g.vector.tensor_copy(crb[:], ncacc[hh][:])
            with p.grp() as g:
                for kt in range(32):
                    eng = g.vector if kt % 2 == 0 else g.gpsimd
                    eng.tensor_scalar(out=vsc[:, kt * 128 + hh * 64:kt * 128 + hh * 64 + 64],
                                      in0=vtm[:, kt * 128 + hh * 64:kt * 128 + hh * 64 + 64],
                                      scalar1=ncacc[hh][:, kt:kt + 1], scalar2=None,
                                      op0=ALU.mult, op1=ALU.bypass)
            for tt in range(8):
                with p.grp() as g:
                    g.scalar.dma_start(out=ab[0][:],
                                       in_=ascr[:, ((hh * 8 + tt) * 32 + 0) * 512:((hh * 8 + tt) * 32 + 2) * 512])
                for k2 in range(16):
                    abt = ab[k2 % 2]
                    with p.grp() as g:
                        if k2 + 1 < 16:
                            g.scalar.dma_start(
                                out=ab[(k2 + 1) % 2][:],
                                in_=ascr[:, ((hh * 8 + tt) * 32 + 2 * k2 + 2) * 512:((hh * 8 + tt) * 32 + 2 * k2 + 4) * 512])
                        for dk in range(2):
                            kt = 2 * k2 + dk
                            g.tensor.matmul(Pden[:], crb[:, kt:kt + 1], abt[:, dk * 512:(dk + 1) * 512],
                                            start=(kt == 0), stop=(kt == 31))
                            g.tensor.matmul(Pav[0:64, :], vsc[:, kt * 128 + hh * 64:kt * 128 + hh * 64 + 64],
                                            abt[:, dk * 512:(dk + 1) * 512],
                                            start=(kt == 0), stop=(kt == 31))
                with p.grp() as g:
                    g.scalar.activation(out=rden[:], in_=Pden[:], func=AF.Ln)
                with p.grp() as g:
                    g.scalar.activation(out=rden[:], in_=rden[:], func=AF.Exp, scale=-1.0)
                with p.grp() as g:
                    g.tensor.matmul(Pb[:], ones1f[:], rden[:], start=True, stop=True)
                with p.grp() as g:
                    g.scalar.copy(out=bcr[:], in_=Pb[:])
                with p.grp() as g:
                    g.vector.tensor_tensor(out=obuf[0:64, :], in0=Pav[0:64, :], in1=bcr[0:64, :],
                                           op=ALU.mult)
                with p.grp() as g:
                    g.scalar.dma_start(out=ohd[(hp * 2 + hh) * 64:(hp * 2 + hh + 1) * 64, tt * 512:(tt + 1) * 512],
                                       in_=obuf[0:64, :])
    return p


def build_oproj(tpc=2048):
    p = Prog()
    xin = p.din("xin", [1024, tpc], F32)
    ofm = p.din("ofm", [1024, tpc], BF16)
    wT = p.din("wT", [1024, 1024], BF16)
    bd = p.din("bd", [128, 8], F32)
    xout = p.dout("xout", [1024, tpc], F32)

    wsb = p.sb("wsb", 128, 8 * 1024, BF16)
    bs = p.sb("bs", 128, 8, F32)
    osb = p.sb("osb", 128, 8 * 512, BF16)
    xres = p.sb("xres", 128, 512, F32)
    t1 = p.sb("t1", 128, 512, F32)
    oout = p.sb("oout", 128, 512, F32)
    Pq = p.psum("Pq", 128, 512)

    with p.grp() as g:
        for c in range(8):
            g.sync.dma_start(out=wsb[:, c * 1024:(c + 1) * 1024], in_=wT[c * 128:(c + 1) * 128, :])
        g.scalar.dma_start(out=bs[:], in_=bd[:])
    for tt in range(tpc // 512):
        with p.grp() as g:
            for c in range(8):
                g.sync.dma_start(out=osb[:, c * 512:(c + 1) * 512],
                                 in_=ofm[c * 128:(c + 1) * 128, tt * 512:(tt + 1) * 512])
        for d in range(8):
            with p.grp() as g:
                g.sync.dma_start(out=xres[:], in_=xin[d * 128:(d + 1) * 128, tt * 512:(tt + 1) * 512])
                for c in range(8):
                    g.tensor.matmul(Pq[:], wsb[:, c * 1024 + d * 128:c * 1024 + (d + 1) * 128],
                                    osb[:, c * 512:(c + 1) * 512], start=(c == 0), stop=(c == 7))
            with p.grp() as g:
                g.scalar.activation(out=t1[:], in_=Pq[:], func=AF.Identity, bias=bs[:, d:d + 1])
            with p.grp() as g:
                g.vector.tensor_tensor(out=oout[:], in0=t1[:], in1=xres[:], op=ALU.add)
            with p.grp() as g:
                g.scalar.dma_start(out=xout[d * 128:(d + 1) * 128, tt * 512:(tt + 1) * 512],
                                   in_=oout[:])
    return p


def host_oproj(xfm, ofm_full, wT_bf, bias1024):
    p = build_oproj()
    bd = np.ascontiguousarray(np.asarray(bias1024, np.float32).reshape(8, 128).T, np.float32)
    in_maps = []
    for c in range(8):
        b, h = c // 2, c % 2
        in_maps.append({
            "xin": np.ascontiguousarray(xfm[b, :, h * 2048:(h + 1) * 2048]),
            "ofm": np.ascontiguousarray(ofm_full[b][:, h * 2048:(h + 1) * 2048]),
            "wT": wT_bf, "bd": bd,
        })
    res = _run(p, in_maps)
    out = np.empty_like(xfm)
    for c in range(8):
        b, h = c // 2, c % 2
        out[b, :, h * 2048:(h + 1) * 2048] = res[c]["xout"]
    return out


def host_attn_b(xfm, wb, inputs):
    p = build_attn_b()
    wqkvT = wb["l0ab_wqkv"]
    bqkv = np.asarray(inputs["L0_attn_b_b_qkv"], np.float32)
    in_maps = []
    for c in range(8):
        b, hh = c // 2, c % 2
        cols = slice(hh * 512, (hh + 1) * 512)
        in_maps.append({
            "xin": np.ascontiguousarray(xfm[b]),
            "wqs": np.ascontiguousarray(wqkvT[:, 0:1024][:, cols]),
            "wks": np.ascontiguousarray(wqkvT[:, 1024:2048][:, cols]),
            "wvs": np.ascontiguousarray(wqkvT[:, 2048:3072][:, cols]),
            "bqd": np.ascontiguousarray(bqkv[0:1024][hh * 512:(hh + 1) * 512].reshape(4, 128).T),
            "bkd": np.ascontiguousarray(bqkv[1024:2048][hh * 512:(hh + 1) * 512].reshape(4, 128).T),
            "bvd": np.ascontiguousarray(bqkv[2048:3072][hh * 512:(hh + 1) * 512].reshape(1, 512)),
        })
    res = _run(p, in_maps)
    import ml_dtypes
    ofm = np.empty((4, 1024, 4096), dtype=ml_dtypes.bfloat16)
    for c in range(8):
        b, hh = c // 2, c % 2
        ofm[b, hh * 512:(hh + 1) * 512, :] = res[c]["ohd"]
    return host_oproj(xfm, ofm, wb["l0ab_wout"], inputs["L0_attn_b_b_out"])


NFC = 33


def build_convb():
    p = Prog()
    T = 4096
    xin = p.din("xin", [1024, T], F32)
    wus = p.din("wus", [1024, 512], BF16)
    wvs = p.din("wvs", [1024, 512], BF16)
    rinw = p.din("rinw", [1, 128], F32)
    rhw = p.din("rhw", [3, 128, 32], F32)
    rlg = p.din("rlg", [128, 3], F32)
    rlb = p.din("rlb", [128, 3], F32)
    routw = p.din("routw", [128, 512], F32)
    dgrow = p.din("dgrow", [1, 512], F32)
    idxr = p.din("idxr", [1, T], F32)
    cfd = p.din("cfd", [NFC, 128, 32 * 128], BF16)
    sfd = p.din("sfd", [NFC, 128, 32 * 128], BF16)
    cid = p.din("cid", [32, 128, NFC * 128], BF16)
    sid = p.din("sid", [32, 128, NFC * 128], BF16)
    uod = p.dout("uod", [128, 32 * 512], BF16)
    uscr = p.dtmp("uscr", [128, 32 * 512], BF16)

    ones128 = p.sb("ones128", 128, 1, F32)
    ones1f = p.sb("ones1f", 1, 128, F32)
    rins = p.sb("rins", 1, 128, F32)
    rhs_ = p.sb("rhs", 128, 3 * 32, F32)
    rlgs = p.sb("rlgs", 128, 3, F32)
    rlbs = p.sb("rlbs", 128, 3, F32)
    routs = p.sb("routs", 128, 512, F32)
    dgs = p.sb("dgs", 1, 512, F32)
    idxs = p.sb("idxs", 1, T, F32)

    xn = p.sb("xn", 128, 8 * T, BF16, side="right")
    wusb = p.sb("wusb", 128, 8 * 512, BF16, side="right")
    wvsb = p.sb("wvsb", 128, 8 * 512, BF16, side="right")

    with p.grp() as g:
        for c in range(8):
            g.sync.dma_start(out=wusb[:, c * 512:(c + 1) * 512], in_=wus[c * 128:(c + 1) * 128, :])
            g.sync.dma_start(out=wvsb[:, c * 512:(c + 1) * 512], in_=wvs[c * 128:(c + 1) * 128, :])
        g.scalar.dma_start(out=rins[:], in_=rinw[:])
        for i in range(3):
            g.scalar.dma_start(out=rhs_[:, i * 32:(i + 1) * 32], in_=rhw[i, :, :])
        g.scalar.dma_start(out=rlgs[:], in_=rlg[:])
        g.scalar.dma_start(out=rlbs[:], in_=rlb[:])
        g.scalar.dma_start(out=routs[:], in_=routw[:])
        g.scalar.dma_start(out=dgs[:], in_=dgrow[:])
        g.scalar.dma_start(out=idxs[:], in_=idxr[:])
        g.vector.memset(ones128[:], 1.0)
        g.gpsimd.memset(ones1f[:], 1.0)

    P0 = p.psum("P0", 128, 512)
    P1 = p.psum("P1", 128, 512)
    P2 = p.psum("P2", 128, 512)
    P3 = p.psum("P3", 128, 512)

    L = ln_alloc(p)
    for tt in range(8):
        emit_ln_tile(p, L, xin, tt, xn, T, 0, P0[:], P1[:])
    ln_free(p, L)

    vtm = p.sb("vtm", 128, 32 * 512, BF16)
    tsb = p.sb("tsb", 128, 32 * 512, BF16)
    ustg = p.sb("ustg", 128, 512, BF16)

    def xnc(c, off, n=512):
        return xn[:, c * T + off:c * T + off + n]

    for kt in range(32):
        with p.grp() as g:
            for c in range(8):
                g.tensor.matmul(P0[:], xnc(c, kt * 128, 128), wusb[:, c * 512:(c + 1) * 512],
                                start=(c == 0), stop=(c == 7))
            for c in range(8):
                g.tensor.matmul(P1[:], xnc(c, kt * 128, 128), wvsb[:, c * 512:(c + 1) * 512],
                                start=(c == 0), stop=(c == 7))
        with p.grp() as g:
            g.scalar.activation(out=ustg[:], in_=P0[:], func=AF.Silu)
        with p.grp() as g:
            g.scalar.activation(out=vtm[:, kt * 512:(kt + 1) * 512], in_=P1[:], func=AF.Silu)
            g.scalar.dma_start(out=uscr[:, kt * 512:(kt + 1) * 512], in_=ustg[:])
    for t_ in (wvsb, wusb, xn):
        p.free(t_)

    hfm = p.sb("hfm", 128, T, F32)
    st1 = p.sb("st1", 1, 1024, F32)
    sqb = p.sb("sqb", 128, 512, F32)
    bcs2 = p.sb("bcs2", 128, 1024, F32)
    lgrow = p.sb("lgrow", 1, 512, F32)
    decf = p.sb("decf", 128, 512, F32)
    hnb = p.sb("hnb", 128, 512, BF16)
    wib = p.sb("wib", 128, 3 * 32, BF16)

    with p.grp() as g:
        g.vector.tensor_copy(wib[:], rhs_[:])
    for tt in range(8):
        with p.grp() as g:
            g.tensor.matmul(P0[:], rins[:], idxs[0:1, tt * 512:(tt + 1) * 512], start=True, stop=True)
        with p.grp() as g:
            g.scalar.activation(out=hfm[:, tt * 512:(tt + 1) * 512], in_=P0[:], func=AF.Silu)
    for i in range(3):
        for tt in range(8):
            hsl = hfm[:, tt * 512:(tt + 1) * 512]
            with p.grp() as g:
                g.scalar.activation(out=sqb[:], in_=hsl, func=AF.Square)
            with p.grp() as g:
                g.tensor.matmul(P0[0:1, :], ones128[:], hsl, start=True, stop=True)
                g.tensor.matmul(P1[0:1, :], ones128[:], sqb[:], start=True, stop=True)
            with p.grp() as g:
                g.scalar.mul(out=st1[0:1, 0:512], in_=P0[0:1, :], mul=1.0 / 32.0)
            with p.grp() as g:
                g.scalar.mul(out=st1[0:1, 512:1024], in_=P1[0:1, :], mul=1.0 / 32.0)
                g.vector.tensor_tensor(out=sqb[0:1, 0:512], in0=st1[0:1, 0:512],
                                       in1=st1[0:1, 0:512], op=ALU.mult)
            with p.grp() as g:
                g.vector.tensor_tensor(out=st1[0:1, 512:1024], in0=st1[0:1, 512:1024],
                                       in1=sqb[0:1, 0:512], op=ALU.subtract)
            with p.grp() as g:
                g.vector.tensor_scalar(out=st1[0:1, 512:1024], in0=st1[0:1, 512:1024],
                                       scalar1=LN_EPS, scalar2=None, op0=ALU.add, op1=ALU.bypass)
            with p.grp() as g:
                g.scalar.activation(out=st1[0:1, 512:1024], in_=st1[0:1, 512:1024], func=AF.Ln)
            with p.grp() as g:
                g.scalar.activation(out=st1[0:1, 512:1024], in_=st1[0:1, 512:1024],
                                    func=AF.Exp, scale=-0.5)
            with p.grp() as g:
                g.tensor.matmul(P0[:], ones1f[:], st1[0:1, 0:512], start=True, stop=True)
                g.tensor.matmul(P1[:], ones1f[:], st1[0:1, 512:1024], start=True, stop=True)
            with p.grp() as g:
                g.scalar.copy(out=bcs2[:, 0:512], in_=P0[:])
                g.vector.tensor_copy(bcs2[:, 512:1024], P1[:])
            with p.grp() as g:
                g.vector.tensor_tensor(out=sqb[:], in0=hsl, in1=bcs2[:, 0:512], op=ALU.subtract)
            with p.grp() as g:
                g.vector.tensor_tensor(out=sqb[:], in0=sqb[:], in1=bcs2[:, 512:1024], op=ALU.mult)
            with p.grp() as g:
                g.vector.tensor_scalar(out=sqb[:], in0=sqb[:], scalar1=rlgs[:, i:i + 1],
                                       scalar2=None, op0=ALU.mult, op1=ALU.bypass)
            with p.grp() as g:
                g.vector.tensor_scalar(out=sqb[:], in0=sqb[:], scalar1=rlbs[:, i:i + 1],
                                       scalar2=None, op0=ALU.add, op1=ALU.bypass)
            with p.grp() as g:
                g.vector.tensor_copy(hnb[:], sqb[:])
            with p.grp() as g:
                g.tensor.matmul(P2[0:32, :], wib[:, i * 32:(i + 1) * 32], hnb[:], start=True, stop=True)
            with p.grp() as g:
                g.scalar.activation(out=hfm[0:32, tt * 512:(tt + 1) * 512], in_=P2[0:32, :],
                                    func=AF.Silu)

    with p.grp() as g:
        g.scalar.activation(out=lgrow[:], in_=dgs[:], func=AF.Sigmoid)
    with p.grp() as g:
        g.vector.tensor_scalar(out=lgrow[:], in0=lgrow[:], scalar1=0.01, scalar2=0.99,
                               op0=ALU.mult, op1=ALU.add)
    with p.grp() as g:
        g.scalar.activation(out=lgrow[:], in_=lgrow[:], func=AF.Ln)
    for kt in range(32):
        with p.grp() as g:
            g.tensor.matmul(P0[:], hfm[:, kt * 128:(kt + 1) * 128], routs[:], start=True, stop=True)
            g.tensor.matmul(P1[:], idxs[0:1, kt * 128:(kt + 1) * 128], lgrow[:], start=True, stop=True)
        with p.grp() as g:
            g.scalar.activation(out=decf[:], in_=P1[:], func=AF.Exp)
        with p.grp() as g:
            g.vector.tensor_tensor(out=tsb[:, kt * 512:(kt + 1) * 512], in0=P0[:], in1=decf[:],
                                   op=ALU.mult)
    for t_ in (wib, hnb, decf, lgrow, bcs2, sqb, st1, hfm):
        p.free(t_)

    orsb = p.sb("orsb", 128, NFC * 512, BF16)
    oisb = p.sb("oisb", 128, NFC * 512, BF16)
    cblk = p.sb("cblk", 128, 32 * 128, BF16)
    sblk = p.sb("sblk", 128, 32 * 128, BF16)
    vrs = p.sb("vrs", 128, 512, F32)
    vis = p.sb("vis", 128, 512, F32)
    tt1 = p.sb("tt1", 128, 512, F32)
    tt2 = p.sb("tt2", 128, 512, F32)
    for fc in range(NFC):
        with p.grp() as g:
            g.sync.dma_start(out=cblk[:], in_=cfd[fc, :, :])
            g.scalar.dma_start(out=sblk[:], in_=sfd[fc, :, :])
        with p.grp() as g:
            for kc in range(32):
                g.tensor.matmul(P0[:], cblk[:, kc * 128:(kc + 1) * 128],
                                vtm[:, kc * 512:(kc + 1) * 512], start=(kc == 0), stop=(kc == 31))
            for kc in range(32):
                g.tensor.matmul(P1[:], sblk[:, kc * 128:(kc + 1) * 128],
                                vtm[:, kc * 512:(kc + 1) * 512], start=(kc == 0), stop=(kc == 31))
        with p.grp() as g:
            g.scalar.copy(out=vrs[:], in_=P0[:])
            g.vector.tensor_copy(vis[:], P1[:])
        with p.grp() as g:
            for kc in range(32):
                g.tensor.matmul(P2[:], cblk[:, kc * 128:(kc + 1) * 128],
                                tsb[:, kc * 512:(kc + 1) * 512], start=(kc == 0), stop=(kc == 31))
            for kc in range(32):
                g.tensor.matmul(P3[:], sblk[:, kc * 128:(kc + 1) * 128],
                                tsb[:, kc * 512:(kc + 1) * 512], start=(kc == 0), stop=(kc == 31))
        with p.grp() as g:
            g.vector.tensor_tensor(out=tt1[:], in0=P2[:], in1=vrs[:], op=ALU.mult)
        with p.grp() as g:
            g.vector.tensor_tensor(out=tt2[:], in0=P3[:], in1=vis[:], op=ALU.mult)
        with p.grp() as g:
            g.vector.tensor_tensor(out=orsb[:, fc * 512:(fc + 1) * 512], in0=tt1[:], in1=tt2[:],
                                   op=ALU.subtract)
        with p.grp() as g:
            g.vector.tensor_tensor(out=tt1[:], in0=P2[:], in1=vis[:], op=ALU.mult)
        with p.grp() as g:
            g.vector.tensor_tensor(out=tt2[:], in0=P3[:], in1=vrs[:], op=ALU.mult)
        with p.grp() as g:
            g.vector.tensor_tensor(out=oisb[:, fc * 512:(fc + 1) * 512], in0=tt1[:], in1=tt2[:],
                                   op=ALU.add)
    for t_ in (tt2, tt1, vis, vrs, sblk, cblk):
        p.free(t_)

    ciblk = p.sb("ciblk", 128, NFC * 128, BF16)
    siblk = p.sb("siblk", 128, NFC * 128, BF16)
    osb = p.sb("osb", 128, 512, F32)
    uob = p.sb("uob", 128, 512, BF16)
    for sc in range(32):
        with p.grp() as g:
            g.sync.dma_start(out=ciblk[:], in_=cid[sc, :, :])
            g.scalar.dma_start(out=siblk[:], in_=sid[sc, :, :])
            g.scalar.dma_start(out=ustg[:], in_=uscr[:, sc * 512:(sc + 1) * 512])
        with p.grp() as g:
            for fc in range(NFC):
                g.tensor.matmul(P0[:], ciblk[:, fc * 128:(fc + 1) * 128],
                                orsb[:, fc * 512:(fc + 1) * 512], start=(fc == 0), stop=False)
            for fc in range(NFC):
                g.tensor.matmul(P0[:], siblk[:, fc * 128:(fc + 1) * 128],
                                oisb[:, fc * 512:(fc + 1) * 512], start=False, stop=(fc == NFC - 1))
        with p.grp() as g:
            g.scalar.copy(out=osb[:], in_=P0[:])
        with p.grp() as g:
            g.vector.tensor_tensor(out=uob[:], in0=osb[:], in1=ustg[:], op=ALU.mult)
        with p.grp() as g:
            g.scalar.dma_start(out=uod[:, sc * 512:(sc + 1) * 512], in_=uob[:])
    return p


_DFT_CACHE = None


def _dft_blocks():
    global _DFT_CACHE
    if _DFT_CACHE is None:
        L = 8192.0
        s = np.arange(4096)
        f = np.arange(4224)
        ang = (2.0 * np.pi / L) * np.outer(s, f)
        Cf = np.cos(ang)
        Sf = -np.sin(ang)
        Cf[:, 4097:] = 0.0
        Sf[:, 4097:] = 0.0
        w = np.full(4224, 2.0)
        w[0] = 1.0
        w[4096] = 1.0
        w[4097:] = 0.0
        Ci = (w[:, None] / L) * np.cos(ang.T)
        Si = -(w[:, None] / L) * np.sin(ang.T)
        cfd = _bf16(Cf.reshape(32, 128, 33, 128).transpose(2, 1, 0, 3).reshape(33, 128, 32 * 128))
        sfd = _bf16(Sf.reshape(32, 128, 33, 128).transpose(2, 1, 0, 3).reshape(33, 128, 32 * 128))
        cid = _bf16(Ci.reshape(33, 128, 32, 128).transpose(2, 1, 0, 3).reshape(32, 128, 33 * 128))
        sid = _bf16(Si.reshape(33, 128, 32, 128).transpose(2, 1, 0, 3).reshape(32, 128, 33 * 128))
        _DFT_CACHE = (cfd, sfd, cid, sid)
    return _DFT_CACHE


def host_convb(xfm, wb, inputs):
    import ml_dtypes
    p = build_convb()
    cfd, sfd, cid, sid = _dft_blocks()
    rin = np.zeros((1, 128), np.float32); rin[0, :32] = inputs["L1_attn_b_rpe_in_w"][0]
    rhw = np.zeros((3, 128, 32), np.float32); rhw[:, :32, :] = inputs["L1_attn_b_rpe_hid_w"]
    rlg = np.zeros((128, 3), np.float32); rlg[:32, :] = inputs["L1_attn_b_rpe_ln_g"].T
    rlb = np.zeros((128, 3), np.float32); rlb[:32, :] = inputs["L1_attn_b_rpe_ln_b"].T
    routp = np.zeros((128, 1024), np.float32); routp[:32, :] = inputs["L1_attn_b_rpe_out_w"]
    idxr = np.arange(4096, dtype=np.float32).reshape(1, 4096)
    wuv = wb["l1ab_wuv"]
    dg = np.asarray(inputs["L1_attn_b_decay_gamma"], np.float32)
    in_maps = []
    for c in range(8):
        b, ch = c // 2, c % 2
        in_maps.append({
            "xin": np.ascontiguousarray(xfm[b]),
            "wus": np.ascontiguousarray(wuv[:, ch * 512:(ch + 1) * 512]),
            "wvs": np.ascontiguousarray(wuv[:, 1024 + ch * 512:1024 + (ch + 1) * 512]),
            "rinw": rin, "rhw": rhw, "rlg": rlg, "rlb": rlb,
            "routw": np.ascontiguousarray(routp[:, ch * 512:(ch + 1) * 512]),
            "dgrow": np.ascontiguousarray(dg[ch * 512:(ch + 1) * 512].reshape(1, 512)),
            "idxr": idxr,
            "cfd": cfd, "sfd": sfd, "cid": cid, "sid": sid,
        })
    res = _run(p, in_maps)
    ofm = np.empty((4, 1024, 4096), dtype=ml_dtypes.bfloat16)
    for c in range(8):
        b, ch = c // 2, c % 2
        uo = res[c]["uod"].reshape(128, 32, 512).transpose(2, 1, 0).reshape(512, 4096)
        ofm[b, ch * 512:(ch + 1) * 512, :] = uo
    return host_oproj(xfm, ofm, wb["l1ab_wo"], np.zeros(1024, np.float32))


def kernel(**inputs):
    wb = host_launch0(inputs)
    x = inputs["L0_attn_a_x"]
    xfm = np.ascontiguousarray(x.transpose(0, 2, 1))
    xfm = host_attn_a(xfm, wb)
    xfm = host_moe(xfm, wb, inputs)
    xfm = host_attn_b(xfm, wb, inputs)
    xfm = host_moe(xfm, wb, inputs)
    xfm = host_attn2(xfm, wb, inputs)
    xfm = host_ffn2(xfm, wb, inputs)
    xfm = host_convb(xfm, wb, inputs)
    xfm = host_ffn2(xfm, wb, inputs)
    return np.ascontiguousarray(xfm.transpose(0, 2, 1))
